# version 03
# baseline (speedup 1.0000x reference)
import numpy as np
from contextlib import ExitStack
import concourse.bass as bass
import concourse.mybir as mybir
from concourse.ap import AP
from concourse.bass_utils import run_bass_kernel_spmd

F32 = mybir.dt.float32
BF16 = mybir.dt.bfloat16
AF = mybir.ActivationFunctionType
ALU = mybir.AluOpType
AX = mybir.AxisListType

TOK = 4096
D = 1024
NCH = D // 128
NCG = TOK // 512
NTT = TOK // 128
NCORES = 8
NEXP = 8


def bfree(ap, n):
    return AP(ap.tensor, ap.offset, [list(p) for p in ap.ap] + [[0, n]])


_SEMCNT = {}


class Sem:
    def __init__(self, nc, es, name):
        self.h = es.enter_context(nc.semaphore(name))
        self._key = (id(nc), int(self.h.num))
        self._base = _SEMCNT.get(self._key, 0)
        self._rel = 0

    @property
    def cnt(self):
        return self._base + self._rel

    @cnt.setter
    def cnt(self, v):
        self._rel = v - self._base
        _SEMCNT[self._key] = self._base + self._rel

    def inc(self, instr, k=1):
        instr.then_inc(self.h, k)
        self.cnt = self.cnt + k
        return self.cnt

    def wait(self, eng, v):
        eng.wait_ge(self.h, self._base + v)


def make_consts(nc, es):
    C = {}
    def sb(name, shape, dt):
        h = es.enter_context(nc.sbuf_tensor(name, shape, dt))
        return h
    C["ones_col"] = sb("c_ones_col", [128, 1], BF16)
    C["ones_col_f"] = sb("c_ones_col_f", [128, 1], F32)
    C["ones_row"] = sb("c_ones_row", [1, 128], BF16)
    C["ones_row_f"] = sb("c_ones_row_f", [1, 128], F32)
    C["ones2_f"] = sb("c_ones2_f", [128, 128], F32)
    C["ident"] = sb("c_ident", [128, 128], BF16)
    C["ident_f"] = sb("c_ident_f", [128, 128], F32)
    C["triu"] = sb("c_triu", [128, 128], F32)
    C["E8"] = sb("c_E8", [8, 8 * 128], BF16)
    iota_c = sb("c_iota_c", [128, 128], F32)
    iota_p = sb("c_iota_p", [128, 1], F32)
    iota_p8 = sb("c_iota_p8", [8, 128], F32)

    with nc.Block() as blk:
        @blk.gpsimd
        def _(g):
            g.iota(iota_c[:, :], [[1, 128]], channel_multiplier=0,
                   allow_small_or_imprecise_dtypes=True)
            g.iota(iota_p[:, :], [[0, 1]], channel_multiplier=1,
                   allow_small_or_imprecise_dtypes=True)
            g.iota(iota_p8[:, :], [[0, 128]], channel_multiplier=1,
                   allow_small_or_imprecise_dtypes=True)

        @blk.vector
        def _(v):
            v.memset(C["ones_col"][:, :], 1.0)
            v.memset(C["ones_col_f"][:, :], 1.0)
            v.memset(C["ones_row"][:, :], 1.0)
            v.memset(C["ones_row_f"][:, :], 1.0)
            v.memset(C["ones2_f"][:, :], 1.0)

    with nc.Block() as blk:
        @blk.vector
        def _(v):
            v.tensor_scalar(out=C["ident"][:, :], in0=iota_c[:, :],
                            scalar1=iota_p[:, 0:1], scalar2=None, op0=ALU.is_equal)
            v.tensor_scalar(out=C["ident_f"][:, :], in0=iota_c[:, :],
                            scalar1=iota_p[:, 0:1], scalar2=None, op0=ALU.is_equal)
            v.tensor_scalar(out=C["triu"][:, :], in0=iota_c[:, :],
                            scalar1=iota_p[:, 0:1], scalar2=None, op0=ALU.is_ge)
            for e in range(8):
                v.tensor_scalar(out=C["E8"][:, e * 128:(e + 1) * 128],
                                in0=iota_p8[:, :], scalar1=float(e), scalar2=None,
                                op0=ALU.is_equal)
    return C


def emit_ln(nc, tag, x_dram, xnT_dram, C, router=None):
    with ExitStack() as es:
        def sb(name, shape, dt=F32):
            return es.enter_context(nc.sbuf_tensor(f"{tag}_{name}", shape, dt))
        buf = [sb("xbufA", [128, TOK]), sb("xbufB", [128, TOK])]
        xsq = [sb("xsqA", [128, TOK]), sb("xsqB", [128, TOK])]
        ssum = sb("ssum", [1, TOK]); ssq = sb("ssq", [1, TOK])
        rstd = sb("rstd", [1, TOK]); negmr = sb("negmr", [1, TOK])
        epsc = sb("epsc", [1, 1])
        rstd_bc = sb("rstd_bc", [128, TOK]); negmr_bc = sb("negmr_bc", [128, TOK])
        xnb = [sb("xnbA", [128, TOK], BF16), sb("xnbB", [128, TOK], BF16)]
        if router is not None:
            wr_dram, lg = router
            xnf = sb("xnf", [128, TOK])
            wr_sb = sb("wr", [128, NCH * 8])
        es8 = ExitStack()
        ps8 = es8.enter_context(nc.psum_tensor(f"{tag}_ps8", [1, TOK], F32))
        s_ld = Sem(nc, es, f"{tag}_ld")
        s_pr = Sem(nc, es, f"{tag}_pr")
        s_pe = Sem(nc, es, f"{tag}_pe")
        s_bm = Sem(nc, es, f"{tag}_bm")
        s_be = Sem(nc, es, f"{tag}_be")
        s_n = Sem(nc, es, f"{tag}_n")
        s_wr = Sem(nc, es, f"{tag}_wr")
        s_rm = Sem(nc, es, f"{tag}_rm")

        with nc.Block() as blk:
            @blk.vector
            def _(v):
                v.memset(epsc[0:1, 0:1], 1e-5)
            if router is not None:
                @blk.sync
                def _(sp):
                    for c in range(NCH):
                        sp.dma_start(out=wr_sb[:, c * 8:(c + 1) * 8],
                                     in_=wr_dram[c * 128:(c + 1) * 128, :]).then_inc(s_ld.h, 16)
                    s_ld.wait(sp, 16 * NCH)
                s_ld.cnt += 16 * NCH

        nld0 = s_ld._rel
        with nc.Block() as blk:
            @blk.sync
            def _(sp):
                for c in range(NCH):
                    if c >= 2:
                        s_pe.wait(sp, c - 1)
                    sp.dma_start(out=buf[c % 2][:, :],
                                 in_=x_dram[c * 128:(c + 1) * 128, :]).then_inc(s_ld.h, 16)
            s_ld.cnt += 16 * NCH

            @blk.tensor
            def _(t):
                for c in range(NCH):
                    s_ld.wait(t, nld0 + 16 * (c + 1))
                    for g in range(NCG):
                        ins = t.matmul(ps8[0:1, g * 512:(g + 1) * 512],
                                       C["ones_col_f"][:, :],
                                       buf[c % 2][:, g * 512:(g + 1) * 512],
                                       start=(c == 0), stop=(c == NCH - 1))
                    s_pe.inc(ins)

            @blk.scalar
            def _(s):
                s_pe.wait(s, NCH)
                for g in range(NCG):
                    s.copy(ssum[0:1, g * 512:(g + 1) * 512],
                           ps8[0:1, g * 512:(g + 1) * 512])

        with nc.Block() as blk:
            @blk.sync
            def _(sp):
                for c in range(NCH):
                    if c >= 2:
                        s_pr.wait(sp, c - 1)
                    sp.dma_start(out=buf[c % 2][:, :],
                                 in_=x_dram[c * 128:(c + 1) * 128, :]).then_inc(s_ld.h, 16)
            s_ld.cnt += 16 * NCH

            @blk.scalar
            def _(s):
                for c in range(NCH):
                    s_ld.wait(s, nld0 + 16 * (NCH + c + 1))
                    if c >= 2:
                        s_pe.wait(s, NCH + c - 1)
                    s_pr.inc(s.square(xsq[c % 2][:, :], buf[c % 2][:, :]))
                s_pe.wait(s, 2 * NCH)
                for g in range(NCG):
                    s.copy(ssq[0:1, g * 512:(g + 1) * 512],
                           ps8[0:1, g * 512:(g + 1) * 512])

            @blk.tensor
            def _(t):
                for c in range(NCH):
                    s_pr.wait(t, c + 1)
                    for g in range(NCG):
                        ins = t.matmul(ps8[0:1, g * 512:(g + 1) * 512],
                                       C["ones_col_f"][:, :],
                                       xsq[c % 2][:, g * 512:(g + 1) * 512],
                                       start=(c == 0), stop=(c == NCH - 1))
                    s_pe.inc(ins)

        with nc.Block() as blk:
            @blk.scalar
            def _(s):
                s.mul(ssum[0:1, :], ssum[0:1, :], 1.0 / D)
                s.mul(ssq[0:1, :], ssq[0:1, :], 1.0 / D)
                s_rm.inc(s.square(rstd[0:1, :], ssum[0:1, :]))
            @blk.vector
            def _(v):
                v.wait_ge(s_rm.h, s_rm.cnt)
                v.tensor_sub(ssq[0:1, :], ssq[0:1, :], rstd[0:1, :])
        with nc.Block() as blk:
            @blk.scalar
            def _(s):
                s_rm.inc(s.activation(out=ssq[0:1, :], in_=ssq[0:1, :], func=AF.Sqrt,
                                      bias=epsc[0:1, 0:1]))
            @blk.vector
            def _(v):
                v.wait_ge(s_rm.h, s_rm.cnt)
                s_rm.inc(v.reciprocal(rstd[0:1, :], ssq[0:1, :]))
                v.wait_ge(s_rm.h, s_rm.cnt)
                s_rm.inc(v.tensor_mul(negmr[0:1, :], ssum[0:1, :], rstd[0:1, :]))
                v.wait_ge(s_rm.h, s_rm.cnt)
                v.tensor_scalar(out=negmr[0:1, :], in0=negmr[0:1, :],
                                scalar1=-1.0, scalar2=None, op0=ALU.mult)

        es8.close()
        psb = es.enter_context(nc.psum_tensor(f"{tag}_psb", [128, 1024], F32))
        if router is not None:
            psr = es.enter_context(nc.psum_tensor(f"{tag}_psr", [128, NTT * 8], F32))
        with nc.Block() as blk:
            @blk.tensor
            def _(t):
                for i in range(2 * NCG):
                    g, which = i % NCG, i // NCG
                    src = rstd if which == 0 else negmr
                    if i >= 2:
                        s_be.wait(t, i - 1)
                    ins = t.matmul(psb[:, (i % 2) * 512:(i % 2) * 512 + 512],
                                   C["ones_row_f"][0:1, :],
                                   src[0:1, g * 512:(g + 1) * 512],
                                   start=True, stop=True)
                    s_bm.inc(ins)

            @blk.scalar
            def _(s):
                for i in range(2 * NCG):
                    g, which = i % NCG, i // NCG
                    dst = rstd_bc if which == 0 else negmr_bc
                    s_bm.wait(s, i + 1)
                    s_be.inc(s.copy(dst[:, g * 512:(g + 1) * 512],
                                    psb[:, (i % 2) * 512:(i % 2) * 512 + 512]))

        nld1 = s_ld._rel
        kst = 2
        with nc.Block() as blk:
            @blk.sync
            def _(sp):
                def store(c):
                    s_n.wait(sp, kst * (c + 1))
                    sp.dma_start(out=xnT_dram[c * 128:(c + 1) * 128, :],
                                 in_=xnb[c % 2][:, :]).then_inc(s_wr.h, 16)
                for c in range(NCH):
                    if c >= 2:
                        s_n.wait(sp, kst * (c - 1))
                    sp.dma_start(out=buf[c % 2][:, :],
                                 in_=x_dram[c * 128:(c + 1) * 128, :]).then_inc(s_ld.h, 16)
                    if c >= 2:
                        store(c - 2)
                store(NCH - 2)
                store(NCH - 1)
                s_wr.wait(sp, 16 * NCH)
            s_ld.cnt += 16 * NCH
            s_wr.cnt += 16 * NCH

            @blk.vector
            def _(v):
                for c in range(NCH):
                    s_ld.wait(v, nld1 + 16 * (c + 1))
                    if c >= 2:
                        s_wr.wait(v, 16 * (c - 1))
                    if router is not None and c >= 1:
                        s_pe.wait(v, 2 * NCH + c)
                    v.tensor_mul(buf[c % 2][:, :], buf[c % 2][:, :], rstd_bc[:, :])
                    if router is not None:
                        s_rm.inc(v.tensor_add(xnf[:, :], buf[c % 2][:, :],
                                              negmr_bc[:, :]))
                    s_n.inc(v.tensor_add(xnb[c % 2][:, :], buf[c % 2][:, :],
                                         negmr_bc[:, :]), 2)

            if router is not None:
                @blk.tensor
                def _(t):
                    nrm0 = s_rm.cnt - NCH
                    for c in range(NCH):
                        s_rm.wait(t, nrm0 + c + 1)
                        for tt in range(NTT):
                            ins = t.matmul(psr[:, tt * 8:(tt + 1) * 8],
                                           xnf[:, tt * 128:(tt + 1) * 128],
                                           wr_sb[:, c * 8:(c + 1) * 8],
                                           start=(c == 0 and tt == 0),
                                           stop=(c == NCH - 1 and tt == NTT - 1),
                                           skip_group_check=True)
                        s_pe.inc(ins)

                @blk.scalar
                def _(s):
                    s_pe.wait(s, 2 * NCH + NCH)
                    s.copy(lg[:, :], psr[:, :])


def emit_precast(nc, tag, pairs):
    with ExitStack() as es:
        s_ld = Sem(nc, es, f"{tag}_cld")
        s_cv = Sem(nc, es, f"{tag}_ccv")
        s_st = Sem(nc, es, f"{tag}_cst")
        wmax = max(src.shape[-1] for src, dst in pairs)
        stg = [es.enter_context(nc.sbuf_tensor(f"{tag}_cstg{i}", [128, wmax], F32))
               for i in range(2)]
        stb = [es.enter_context(nc.sbuf_tensor(f"{tag}_cstb{i}", [128, wmax], BF16))
               for i in range(2)]
        jobs = []
        for src, dst in pairs:
            R, W = src.shape
            assert R % 128 == 0
            for r in range(R // 128):
                jobs.append((src[r * 128:(r + 1) * 128, :],
                             dst[r * 128:(r + 1) * 128, :], W))
        with nc.Block() as blk:
            @blk.sync
            def _(sp):
                n = len(jobs)
                for i in range(n + 1):
                    if i < n:
                        s_ap, d_ap, W = jobs[i]
                        if i >= 2:
                            s_cv.wait(sp, i - 1)
                        sp.dma_start(out=stg[i % 2][:, 0:W], in_=s_ap).then_inc(s_ld.h, 16)
                    if i >= 1:
                        s_ap, d_ap, W = jobs[i - 1]
                        s_cv.wait(sp, i)
                        sp.dma_start(out=d_ap, in_=stb[(i - 1) % 2][:, 0:W]).then_inc(s_st.h, 16)
                s_st.wait(sp, 16 * len(jobs))
            s_ld.cnt += 16 * len(jobs)
            s_st.cnt += 16 * len(jobs)

            @blk.gpsimd
            def _(g):
                for i, (s_ap, d_ap, W) in enumerate(jobs):
                    s_ld.wait(g, 16 * (i + 1))
                    if i >= 2:
                        s_st.wait(g, 16 * (i - 1))
                    s_cv.inc(g.tensor_copy(out=stb[i % 2][:, 0:W],
                                           in_=stg[i % 2][:, 0:W]))


class PoolPrecast:
    def __init__(self, nc, es_buf, es_sem, tag, pairs, maxw=2048):
        self.nc = nc
        self.jobs = []
        for src, dst in pairs:
            R, Wd = src.shape
            assert R % 128 == 0
            for r in range(R // 128):
                for w0 in range(0, Wd, maxw):
                    w1 = min(Wd, w0 + maxw)
                    self.jobs.append((src[r * 128:(r + 1) * 128, w0:w1],
                                      dst[r * 128:(r + 1) * 128, w0:w1], w1 - w0))
        self.stg = [es_buf.enter_context(nc.sbuf_tensor(f"{tag}_pstg{i}", [128, maxw], F32))
                    for i in range(2)]
        self.stb = [es_buf.enter_context(nc.sbuf_tensor(f"{tag}_pstb{i}", [128, maxw], BF16))
                    for i in range(2)]
        self.s_in = Sem(nc, es_sem, f"{tag}_pin")
        self.s_out = Sem(nc, es_sem, f"{tag}_pout")
        self.pos = 0
        self.issued = 0

    def emit(self, blk, n=None):
        left = len(self.jobs) - self.pos
        n = left if n is None else min(n, left)
        if n <= 0:
            return
        i0, i1 = self.pos, self.pos + n
        self.pos = i1

        @blk.gpsimd
        def _(g):
            for i in range(i0, i1):
                while self.issued < min(i + 2, i1):
                    j = self.issued
                    src, dst, W = self.jobs[j]
                    g.dma_start(out=self.stg[j % 2][:, 0:W], in_=src).then_inc(self.s_in.h, 16)
                    self.s_in.cnt += 16
                    self.issued += 1
                src, dst, W = self.jobs[i]
                self.s_in.wait(g, 16 * (i + 1))
                if i >= 2:
                    self.s_out.wait(g, 16 * (i - 1))
                g.tensor_copy(out=self.stb[i % 2][:, 0:W], in_=self.stg[i % 2][:, 0:W])
                g.dma_start(out=dst, in_=self.stb[i % 2][:, 0:W]).then_inc(self.s_out.h, 16)
                self.s_out.cnt += 16

    def dep(self):
        return (self.s_out, 16 * len(self.jobs))



def emit_router(nc, tag, lg, gatesT_dram, C, eps, dbg=None):
    with ExitStack() as es:
        def sb(name, shape, dt=F32):
            return es.enter_context(nc.sbuf_tensor(f"{tag}_{name}", shape, dt))
        m = sb("m", [128, NTT]); d0 = sb("d0", [128, NTT * 8])
        er = sb("er", [128, NTT * 8]); ssm = sb("ssm", [128, NTT])
        rs = sb("rs", [128, NTT]); P = sb("P", [128, NTT * 8])
        msk1 = sb("msk1", [128, NTT * 8]); t1 = sb("t1", [128, NTT * 8])
        Pm = sb("Pm", [128, NTT * 8]); m2 = sb("m2", [128, NTT])
        mskK = sb("mskK", [128, NTT * 8]); Ptop = sb("Ptop", [128, NTT * 8])
        den = sb("den", [128, NTT]); rden = sb("rden", [128, NTT])
        gat = sb("gat", [128, NTT * 8])
        pst = es.enter_context(nc.psum_tensor(f"{tag}_pst", [8, 2048], F32))
        s_tr = Sem(nc, es, f"{tag}_tr")
        s_te = Sem(nc, es, f"{tag}_te")
        s_ch = Sem(nc, es, f"{tag}_ch")
        s_gw = Sem(nc, es, f"{tag}_gw")

        def r3(apx):
            return apx[:, :].rearrange("p (t e) -> p t e", e=8)

        with nc.Block() as blk:
            @blk.vector
            def _(v):
                def step(ins):
                    s_ch.inc(ins)
                    v.wait_ge(s_ch.h, s_ch.cnt)
                step(v.tensor_reduce(out=m[:, :], in_=r3(lg), axis=AX.X, op=ALU.max))
                v.tensor_tensor(out=r3(d0), in0=r3(lg), in1=bfree(m[:, :], 8),
                                op=ALU.subtract)
        with nc.Block() as blk:
            @blk.scalar
            def _(s):
                s.activation(out=er[:, :], in_=d0[:, :], func=AF.Exp)
        with nc.Block() as blk:
            @blk.vector
            def _(v):
                def step(ins):
                    s_ch.inc(ins)
                    v.wait_ge(s_ch.h, s_ch.cnt)
                step(v.tensor_reduce(out=ssm[:, :], in_=r3(er), axis=AX.X, op=ALU.add))
                step(v.reciprocal(rs[:, :], ssm[:, :]))
                step(v.tensor_tensor(out=r3(P), in0=r3(er), in1=bfree(rs[:, :], 8),
                                     op=ALU.mult))
                step(v.tensor_reduce(out=m[:, :], in_=r3(P), axis=AX.X, op=ALU.max))
                step(v.tensor_tensor(out=r3(msk1), in0=r3(P), in1=bfree(m[:, :], 8),
                                     op=ALU.is_equal))
                step(v.tensor_mul(t1[:, :], P[:, :], msk1[:, :]))
                step(v.tensor_sub(Pm[:, :], P[:, :], t1[:, :]))
                step(v.tensor_reduce(out=m2[:, :], in_=r3(Pm), axis=AX.X, op=ALU.max))
                step(v.tensor_tensor(out=r3(mskK), in0=r3(P), in1=bfree(m2[:, :], 8),
                                     op=ALU.is_ge))
                step(v.tensor_mul(Ptop[:, :], P[:, :], mskK[:, :]))
                step(v.tensor_reduce(out=den[:, :], in_=r3(Ptop), axis=AX.X, op=ALU.add))
                if eps:
                    step(v.tensor_scalar(out=den[:, :], in0=den[:, :], scalar1=float(eps),
                                         scalar2=None, op0=ALU.add))
                step(v.reciprocal(rden[:, :], den[:, :]))
                step(v.tensor_tensor(out=r3(gat), in0=r3(Ptop), in1=bfree(rden[:, :], 8),
                                     op=ALU.mult))

        gtt = [es.enter_context(nc.sbuf_tensor(f"{tag}_gtt{i}", [8, 128], BF16))
               for i in range(2)]
        with nc.Block() as blk:
            @blk.tensor
            def _(t):
                for tt in range(NTT):
                    if tt >= 4:
                        s_te.wait(t, tt - 3)
                    ins = t.transpose(pst[:, (tt % 4) * 512:(tt % 4) * 512 + 128],
                                      gat[:, tt * 8:(tt + 1) * 8], C["ident_f"][:, :])
                    s_tr.inc(ins)

            @blk.scalar
            def _(s):
                for tt in range(NTT):
                    s_tr.wait(s, tt + 1)
                    if tt >= 2:
                        s_gw.wait(s, 16 * (tt - 1))
                    s_te.inc(s.copy(gtt[tt % 2][:, :],
                                    pst[0:8, (tt % 4) * 512:(tt % 4) * 512 + 128]))

            @blk.sync
            def _(sp):
                for tt in range(NTT):
                    s_te.wait(sp, tt + 1)
                    sp.dma_start(out=gatesT_dram[0:8, tt * 128:(tt + 1) * 128],
                                 in_=gtt[tt % 2][:, :]).then_inc(s_gw.h, 16)
                s_gw.wait(sp, 16 * NTT)
            s_gw.cnt += 16 * NTT
        if dbg is not None:
            with nc.Block() as blk:
                @blk.sync
                def _(sp):
                    for i, (name, buf) in enumerate([("lg", lg), ("er", er), ("P", P),
                                                     ("mskK", mskK), ("den", den), ("gat", gat)]):
                        if name in dbg:
                            sp.dma_start(out=dbg[name], in_=buf[:, :]).then_inc(s_gw.h, 16)
                            s_gw.cnt += 16
                    sp.wait_ge(s_gw.h, s_gw.cnt)


def emit_load_cast_w(nc, tag, jobs):
    with ExitStack() as es:
        wmax = max(j[0].shape[-1] for j in jobs)
        stg = [es.enter_context(nc.sbuf_tensor(f"{tag}_wstg{i}", [128, wmax], F32))
               for i in range(2)]
        s_ld = Sem(nc, es, f"{tag}_wld")
        s_cv = Sem(nc, es, f"{tag}_wcv")
        with nc.Block() as blk:
            @blk.sync
            def _(sp):
                for i, (s_ap, d_ap) in enumerate(jobs):
                    W = s_ap.shape[-1]
                    if i >= 2:
                        s_cv.wait(sp, i - 1)
                    sp.dma_start(out=stg[i % 2][:, 0:W], in_=s_ap).then_inc(s_ld.h, 16)
            s_ld.cnt += 16 * len(jobs)

            @blk.gpsimd
            def _(g):
                for i, (s_ap, d_ap) in enumerate(jobs):
                    W = s_ap.shape[-1]
                    s_ld.wait(g, 16 * (i + 1))
                    s_cv.inc(g.tensor_copy(out=d_ap, in_=stg[i % 2][:, 0:W]))


def emit_l0_attn(nc, tag, x_in, xnT_dram, W, x_out, C, aux=None):
    with ExitStack() as es:
        def sb(name, shape, dt=F32):
            return es.enter_context(nc.sbuf_tensor(f"{tag}_{name}", shape, dt))
        wq = sb("wq", [128, NCH * 1024], BF16)
        wk = sb("wk", [128, NCH * 1024], BF16)
        wv = sb("wv", [128, NCH * 1024], BF16)
        jobs = []
        for name, dst in (("wq", wq), ("wk", wk), ("wv", wv)):
            for c in range(NCH):
                jobs.append((W[name][c * 128:(c + 1) * 128, :],
                             dst[:, c * 1024:(c + 1) * 1024]))
        emit_load_cast_w(nc, tag, jobs)

        qT = sb("qT", [128, TOK], BF16)
        kT = sb("kT", [128, TOK], BF16)
        v_aug = sb("vaug", [128, NTT * 130], BF16)
        PT = [sb(f"PT{i}", [128, 1024], BF16) for i in range(2)]
        oT = sb("oT", [128, TOK], BF16)
        oB_sb = [sb(f"oB{i}", [64, 1024], BF16) for i in range(2)]
        l2 = sb("l2", [128, TOK])
        recl_bc = sb("reclbc", [128, TOK])
        xg = [sb(f"xg{i}", [128, NCH * 512], BF16) for i in range(2)]
        x_sb = sb("x0", [128, TOK])
        tmpf = sb("tmpf", [128, TOK])
        psA = es.enter_context(nc.psum_tensor(f"{tag}_psA", [128, 2048], F32))
        psB = es.enter_context(nc.psum_tensor(f"{tag}_psB", [128, 1024], F32))
        pp = psA
        st_ps = psA
        av_ps = psB
        pb = psB
        s_xg = Sem(nc, es, f"{tag}_xg")
        s_pm = Sem(nc, es, f"{tag}_pm")
        s_pv = Sem(nc, es, f"{tag}_pv")
        s_ms = Sem(nc, es, f"{tag}_ms")
        s_sm = Sem(nc, es, f"{tag}_sm")
        s_ex = Sem(nc, es, f"{tag}_ex")
        s_am = Sem(nc, es, f"{tag}_am")
        s_ov = Sem(nc, es, f"{tag}_ov")
        s_ob = Sem(nc, es, f"{tag}_ob")
        s_bm = Sem(nc, es, f"{tag}_bm")
        s_be = Sem(nc, es, f"{tag}_be")
        s_xl = Sem(nc, es, f"{tag}_xl")
        s_xu = Sem(nc, es, f"{tag}_xu")
        s_xs = Sem(nc, es, f"{tag}_xs")

        for g in range(NEXP):
            with nc.Block() as blk:
                @blk.sync
                def _(sp):
                    for cg in range(NCG):
                        if g * 48 + cg * 6 >= 12:
                            s_pm.wait(sp, g * 48 + (cg - 1) * 6)
                        for c in range(NCH):
                            sp.dma_start(out=xg[cg % 2][:, c * 512:(c + 1) * 512],
                                         in_=xnT_dram[c * 128:(c + 1) * 128,
                                                      cg * 512:(cg + 1) * 512]
                                         ).then_inc(s_xg.h, 16)
                s_xg.cnt += 16 * NCH * NCG

                @blk.vector
                def _(v):
                    s_ms.inc(v.memset(v_aug[:, :], 1.0))

                if aux is not None:
                    aux(blk, g)

                @blk.tensor
                def _(t):
                    item = g * 48
                    for cg in range(NCG):
                        s_xg.wait(t, 16 * NCH * (g * NCG + cg + 1))
                        for wname, wsb in (("q", wq), ("k", wk)):
                            if item >= 2:
                                s_pv.wait(t, item - 1)
                            for c in range(NCH):
                                ins = t.matmul(
                                    pp[:, (item % 2) * 512:(item % 2) * 512 + 512],
                                    wsb[:, c * 1024 + g * 128:c * 1024 + (g + 1) * 128],
                                    xg[cg % 2][:, c * 512:(c + 1) * 512],
                                    start=(c == 0), stop=(c == NCH - 1))
                            s_pm.inc(ins)
                            item += 1
                        for i in range(4):
                            tt = cg * 4 + i
                            if item >= 2:
                                s_pv.wait(t, item - 1)
                            for c in range(NCH):
                                ins = t.matmul(
                                    pp[:, (item % 2) * 512:(item % 2) * 512 + 128],
                                    xg[cg % 2][:, c * 512 + i * 128:c * 512 + (i + 1) * 128],
                                    wv[:, c * 1024 + g * 128:c * 1024 + (g + 1) * 128],
                                    start=(c == 0), stop=(c == NCH - 1))
                            s_pm.inc(ins)
                            item += 1

                @blk.scalar
                def _(s):
                    item = g * 48
                    for cg in range(NCG):
                        s_pm.wait(s, item + 1)
                        s_pv.inc(s.copy(qT[:, cg * 512:(cg + 1) * 512],
                                        pp[:, (item % 2) * 512:(item % 2) * 512 + 512]))
                        item += 1
                        s_pm.wait(s, item + 1)
                        s_pv.inc(s.copy(kT[:, cg * 512:(cg + 1) * 512],
                                        pp[:, (item % 2) * 512:(item % 2) * 512 + 512]))
                        item += 1
                        for i in range(4):
                            tt = cg * 4 + i
                            s_pm.wait(s, item + 1)
                            if tt == 0:
                                s_ms.wait(s, g + 1)
                            dst = v_aug[:, tt * 130:(tt + 1) * 130].rearrange(
                                "p (h x) -> p h x", x=65)[:, :, 0:64]
                            s_pv.inc(s.copy(dst,
                                            pp[:, (item % 2) * 512:(item % 2) * 512 + 128]
                                            .rearrange("p (h x) -> p h x", x=64)))
                            item += 1

            for hi in range(2):
                hb = hi * 64
                with nc.Block() as blk:
                    @blk.tensor
                    def _(t):
                        for qq in range(4):
                            item = (g * 2 + hi) * 4 + qq
                            kb0 = item * NTT
                            for kb in range(NTT):
                                ki = kb0 + kb
                                if ki >= 2:
                                    s_ex.wait(t, ki - 1)
                                p = kb % 2
                                for half in range(2):
                                    ins = t.matmul(
                                        st_ps[:, p * 1024 + half * 512:p * 1024 + half * 512 + 512],
                                        kT[hb:hb + 64, kb * 128:(kb + 1) * 128],
                                        qT[hb:hb + 64,
                                           qq * 1024 + half * 512:qq * 1024 + half * 512 + 512],
                                        start=True, stop=True)
                                s_sm.inc(ins)
                                s_ex.wait(t, ki + 1)
                                if kb == 0:
                                    s_ov.wait(t, 2 * item)
                                for half in range(2):
                                    ins = t.matmul(
                                        av_ps[0:65, half * 512:half * 512 + 512],
                                        v_aug[:, kb * 130 + hi * 65:kb * 130 + (hi + 1) * 65],
                                        PT[p][:, half * 512:half * 512 + 512],
                                        start=(kb == 0), stop=(kb == NTT - 1))
                                s_am.inc(ins)

                    @blk.scalar
                    def _(s):
                        for qq in range(4):
                            item = (g * 2 + hi) * 4 + qq
                            kb0 = item * NTT
                            for kb in range(NTT):
                                ki = kb0 + kb
                                p = kb % 2
                                s_sm.wait(s, ki + 1)
                                if ki >= 2:
                                    s_am.wait(s, ki - 1)
                                s_ex.inc(s.activation(out=PT[p][:, :],
                                                      in_=st_ps[:, p * 1024:p * 1024 + 1024],
                                                      func=AF.Exp, scale=0.125))
                            s_am.wait(s, kb0 + NTT)
                            if hi == 0:
                                s.copy(oT[0:64, qq * 1024:(qq + 1) * 1024],
                                       av_ps[0:64, :])
                            else:
                                gq = g * 4 + qq
                                if gq >= 2:
                                    s_ob.wait(s, 16 * (gq - 1))
                                s.copy(oB_sb[qq % 2][:, :], av_ps[0:64, :])
                            s_ov.inc(s.copy(l2[64:65, qq * 1024:(qq + 1) * 1024],
                                            av_ps[64:65, :]), 2)

                    if hi == 1:
                        @blk.sync
                        def _(sp):
                            for qq in range(4):
                                item = (g * 2 + 1) * 4 + qq
                                s_ov.wait(sp, 2 * (item + 1))
                                sp.dma_start(out=oT[64:128, qq * 1024:(qq + 1) * 1024],
                                             in_=oB_sb[qq % 2][:, :]).then_inc(s_ob.h, 16)
                        s_ob.cnt += 16 * 4

                with nc.Block() as blk:
                    @blk.vector
                    def _(v):
                        v.reciprocal(l2[64:65, :], l2[64:65, :])

                with nc.Block() as blk:
                    @blk.tensor
                    def _(t):
                        for cg in range(NCG):
                            gi = (g * 2 + hi) * NCG + cg
                            if gi >= 2:
                                s_be.wait(t, gi - 1)
                            ins = t.matmul(
                                pb[hb:hb + 64, (cg % 2) * 512:(cg % 2) * 512 + 512],
                                C["ones2_f"][64:65, 0:64],
                                l2[64:65, cg * 512:(cg + 1) * 512],
                                start=True, stop=True)
                            s_bm.inc(ins)

                    @blk.scalar
                    def _(s):
                        for cg in range(NCG):
                            gi = (g * 2 + hi) * NCG + cg
                            s_bm.wait(s, gi + 1)
                            s_be.inc(s.copy(recl_bc[hb:hb + 64, cg * 512:(cg + 1) * 512],
                                            pb[hb:hb + 64, (cg % 2) * 512:(cg % 2) * 512 + 512]))

            with nc.Block() as blk:
                @blk.sync
                def _(sp):
                    if g >= 1:
                        s_xs.wait(sp, 16 * NCH * g)
                    for c in range(NCH):
                        sp.dma_start(out=x_sb[:, c * 512:(c + 1) * 512],
                                     in_=x_in[g * 128:(g + 1) * 128,
                                              c * 512:(c + 1) * 512]).then_inc(s_xl.h, 16)
                    s_xu.wait(sp, 2 * (g + 1))
                    for c in range(NCH):
                        sp.dma_start(out=x_out[g * 128:(g + 1) * 128,
                                               c * 512:(c + 1) * 512],
                                     in_=x_sb[:, c * 512:(c + 1) * 512]).then_inc(s_xs.h, 16)
                    if g == NEXP - 1:
                        s_xs.wait(sp, 16 * NCH * NEXP)
                s_xl.cnt += 16 * NCH
                s_xs.cnt += 16 * NCH

                @blk.vector
                def _(v):
                    s_xl.wait(v, 16 * NCH * (g + 1))
                    s_ob.wait(v, 16 * 4 * (g + 1))
                    v.tensor_mul(tmpf[:, :], oT[:, :], recl_bc[:, :])
                    s_xu.inc(v.tensor_add(x_sb[:, :], x_sb[:, :], tmpf[:, :]), 2)


def emit_moe_ffn(nc, tag, kind, xnT_dram, x_in, x_out, C, W, aux=None, deps=()):
    F = W["F"]
    NF = F // 128
    swi = (kind == "swiglu")
    with ExitStack() as es:
        def sb(name, shape, dt=F32):
            return es.enter_context(nc.sbuf_tensor(f"{tag}_{name}", shape, dt))
        gatesT_dram = W["gatesT_dram"]

        wup = sb("wup", [128, NCH * F], BF16)
        wup2 = sb("wup2", [128, NCH * F], BF16) if swi else None
        wdn = sb("wdn", [128, NF * D], BF16)
        hg = sb("hg", [128, NF * 512], BF16)
        htmp = sb("htmp", [128, 512], BF16) if swi else None
        xn_sb = [sb(f"xn{i}", [128, NCH * 512], BF16) for i in range(2)]
        x_sb = sb("x0", [128, NCH * 512])
        gate_sb = [sb(f"gate{i}", [128, 512], BF16) for i in range(2)]
        gate_rhs = [sb(f"gr{i}", [8, 512], BF16) for i in range(2)]
        b1T = sb("b1T", [128, 8 * NF]) if not swi else None
        b2bf = sb("b2bf", [8, D], BF16) if not swi else None
        ph = es.enter_context(nc.psum_tensor(f"{tag}_ph", [128, 1024], F32))
        phu = es.enter_context(nc.psum_tensor(f"{tag}_phu", [128, 1024], F32)) if swi else None
        py = es.enter_context(nc.psum_tensor(f"{tag}_py", [128, 1024], F32))
        pg = es.enter_context(nc.psum_tensor(f"{tag}_pg", [128, 1024], F32))
        s_w = Sem(nc, es, f"{tag}_w")
        s_xn = Sem(nc, es, f"{tag}_xn")
        s_xl = Sem(nc, es, f"{tag}_xl")
        s_xs = Sem(nc, es, f"{tag}_xs")
        s_gm = Sem(nc, es, f"{tag}_gm")
        s_gr = Sem(nc, es, f"{tag}_gr")
        s_ge = Sem(nc, es, f"{tag}_ge")
        s_hm = Sem(nc, es, f"{tag}_hm")
        s_hu = Sem(nc, es, f"{tag}_hu")
        s_hg = Sem(nc, es, f"{tag}_hg")
        s_hs = Sem(nc, es, f"{tag}_hs")
        s_ym = Sem(nc, es, f"{tag}_ym")
        s_ya = Sem(nc, es, f"{tag}_ya")

        if not swi:
            with nc.Block() as blk:
                @blk.sync
                def _(sp):
                    sp.dma_start(out=b1T[:, :], in_=W["b1T"]).then_inc(s_w.h, 16)
                    sp.dma_start(out=x_sb[0:8, 0:D], in_=W["b2"]).then_inc(s_w.h, 16)
                    s_w.wait(sp, 32)
                s_w.cnt += 32
            with nc.Block() as blk:
                @blk.vector
                def _(v):
                    v.tensor_copy(out=b2bf[:, :], in_=x_sb[0:8, 0:D])

        nup = 2 if swi else 1
        nwj = 16 * (nup * NCH + NF)
        nw_base = s_w.cnt
        with nc.Block() as blk:
            if aux is not None:
                aux(blk)

            @blk.sync
            def _(sp):
                for sem, val in deps:
                    sem.wait(sp, val)
                for e in range(NEXP):
                    if e >= 1:
                        s_hm.wait(sp, e * NCG * NF)
                    for c in range(NCH):
                        sp.dma_start(out=wup[:, c * F:(c + 1) * F],
                                     in_=W["w1bf" if not swi else "wgbf"][e, c * 128:(c + 1) * 128, :]
                                     ).then_inc(s_w.h, 16)
                        if swi:
                            sp.dma_start(out=wup2[:, c * F:(c + 1) * F],
                                         in_=W["wubf"][e, c * 128:(c + 1) * 128, :]
                                         ).then_inc(s_w.h, 16)
                    if e >= 1:
                        s_ym.wait(sp, e * NCG * NCH)
                    for f in range(NF):
                        sp.dma_start(out=wdn[:, f * D:(f + 1) * D],
                                     in_=W["w2bf" if not swi else "wdbf"][e, f * 128:(f + 1) * 128, :]
                                     ).then_inc(s_w.h, 16)

                    def store(tb):
                        s_ya.wait(sp, (e * NCG + tb + 1) * NCH)
                        for c in range(NCH):
                            sp.dma_start(out=x_out[c * 128:(c + 1) * 128,
                                                   tb * 512:(tb + 1) * 512],
                                         in_=x_sb[:, c * 512:(c + 1) * 512]
                                         ).then_inc(s_xs.h, 16)
                    for tb in range(NCG):
                        if e * NCG + tb >= 2:
                            s_ym.wait(sp, (e * NCG + tb - 1) * NCH)
                        sp.dma_start(out=gate_rhs[tb % 2][:, :],
                                     in_=gatesT_dram[0:8, tb * 512:(tb + 1) * 512]
                                     ).then_inc(s_gr.h, 16)
                        if e * NCG + tb >= 2:
                            s_hm.wait(sp, (e * NCG + tb - 1) * NF)
                        for c in range(NCH):
                            sp.dma_start(out=xn_sb[tb % 2][:, c * 512:(c + 1) * 512],
                                         in_=xnT_dram[c * 128:(c + 1) * 128,
                                                      tb * 512:(tb + 1) * 512]).then_inc(s_xn.h, 16)
                        if tb >= 1:
                            store(tb - 1)
                        if e * NCG + tb >= 1:
                            s_xs.wait(sp, 16 * NCH * (e * NCG + tb))
                        if e >= 1:
                            s_xs.wait(sp, 16 * NCH * ((e - 1) * NCG + tb + 1))
                        xsrc = x_in if e == 0 else x_out
                        for c in range(NCH):
                            sp.dma_start(out=x_sb[:, c * 512:(c + 1) * 512],
                                         in_=xsrc[c * 128:(c + 1) * 128,
                                                  tb * 512:(tb + 1) * 512]).then_inc(s_xl.h, 16)
                    store(NCG - 1)
                    s_w.cnt += nwj
                    s_xn.cnt += 16 * NCH * NCG
                    s_xl.cnt += 16 * NCH * NCG
                    s_xs.cnt += 16 * NCH * NCG
                    s_gr.cnt += 16 * NCG
                sp.wait_ge(s_xs.h, s_xs.cnt)

            @blk.tensor
            def _(t):
                for e in range(NEXP):
                    nw0 = nw_base + e * nwj
                    t.wait_ge(s_w.h, nw0 + 16 * nup * NCH)
                    for tb in range(NCG):
                        if e * NCG + tb >= 2:
                            s_ge.wait(t, e * NCG + tb - 1)
                        s_gr.wait(t, 16 * (e * NCG + tb + 1))
                        ins = t.matmul(pg[:, (tb % 2) * 512:(tb % 2) * 512 + 512],
                                       C["E8"][:, e * 128:(e + 1) * 128],
                                       gate_rhs[tb % 2][:, :],
                                       start=True, stop=True)
                        s_gm.inc(ins)
                        s_xn.wait(t, 16 * NCH * (e * NCG + tb + 1))
                        fbase = (e * NCG + tb) * NF
                        for f in range(NF):
                            if fbase + f >= 2:
                                s_hg.wait(t, fbase + f - 1)
                            for c in range(NCH):
                                ins = t.matmul(ph[:, (f % 2) * 512:(f % 2) * 512 + 512],
                                               wup[:, c * F + f * 128:c * F + (f + 1) * 128],
                                               xn_sb[tb % 2][:, c * 512:(c + 1) * 512],
                                               start=(c == 0), stop=(c == NCH - 1))
                            if swi:
                                if fbase + f >= 2:
                                    s_hu.wait(t, fbase + f - 1)
                                for c in range(NCH):
                                    ins = t.matmul(phu[:, (f % 2) * 512:(f % 2) * 512 + 512],
                                                   wup2[:, c * F + f * 128:c * F + (f + 1) * 128],
                                                   xn_sb[tb % 2][:, c * 512:(c + 1) * 512],
                                                   start=(c == 0), stop=(c == NCH - 1))
                            s_hm.inc(ins)
                        if tb == 0:
                            t.wait_ge(s_w.h, nw0 + nwj)
                        s_hs.wait(t, fbase + NF)
                        for j in range(NCH):
                            s_ya.wait(t, max(0, (e * NCG + tb) * NCH + j - 1))
                            for f in range(NF):
                                ins = t.matmul(py[:, (j % 2) * 512:(j % 2) * 512 + 512],
                                               wdn[:, f * D + j * 128:f * D + (j + 1) * 128],
                                               hg[:, f * 512:(f + 1) * 512],
                                               start=(f == 0), stop=(f == NF - 1 and not
                                                                     (not swi and e == 0)))
                            if not swi and e == 0:
                                ins = t.matmul(py[:, (j % 2) * 512:(j % 2) * 512 + 512],
                                               b2bf[0:8, j * 128:(j + 1) * 128],
                                               gate_rhs[tb % 2][:, :],
                                               start=False, stop=True)
                            s_ym.inc(ins)

            @blk.scalar
            def _(s):
                for e in range(NEXP):
                    for tb in range(NCG):
                        s_gm.wait(s, e * NCG + tb + 1)
                        s_ge.inc(s.copy(gate_sb[tb % 2][:, :],
                                        pg[:, (tb % 2) * 512:(tb % 2) * 512 + 512]))
                        fbase = (e * NCG + tb) * NF
                        for f in range(NF):
                            s_hm.wait(s, fbase + f + 1)
                            if f == 0 and e * NCG + tb >= 1:
                                s_ym.wait(s, (e * NCG + tb) * NCH)
                            if swi:
                                ins = s.activation(out=hg[:, f * 512:(f + 1) * 512],
                                                   in_=ph[:, (f % 2) * 512:(f % 2) * 512 + 512],
                                                   func=AF.Silu)
                            else:
                                ins = s.activation(out=hg[:, f * 512:(f + 1) * 512],
                                                   in_=ph[:, (f % 2) * 512:(f % 2) * 512 + 512],
                                                   func=AF.Gelu,
                                                   bias=b1T[:, e * NF + f:e * NF + f + 1])
                            s_hg.inc(ins)

            @blk.vector
            def _(v):
                for e in range(NEXP):
                    for tb in range(NCG):
                        fbase = (e * NCG + tb) * NF
                        s_ge.wait(v, e * NCG + tb + 1)
                        for f in range(NF):
                            s_hg.wait(v, fbase + f + 1)
                            if e * NCG + tb >= 1:
                                s_ym.wait(v, (e * NCG + tb) * NCH)
                            if swi:
                                s_hu.inc(v.tensor_mul(htmp[:, :],
                                                      hg[:, f * 512:(f + 1) * 512],
                                                      phu[:, (f % 2) * 512:(f % 2) * 512 + 512]))
                                s_hs.inc(v.tensor_mul(hg[:, f * 512:(f + 1) * 512],
                                                      htmp[:, :], gate_sb[tb % 2][:, :]))
                            else:
                                ins = v.tensor_mul(hg[:, f * 512:(f + 1) * 512],
                                                   hg[:, f * 512:(f + 1) * 512],
                                                   gate_sb[tb % 2][:, :])
                                s_hs.inc(ins)
                        s_xl.wait(v, 16 * NCH * (e * NCG + tb + 1))
                        for j in range(NCH):
                            s_ym.wait(v, (e * NCG + tb) * NCH + j + 1)
                            ins = v.scalar_tensor_tensor(
                                out=x_sb[:, j * 512:(j + 1) * 512],
                                in0=py[:, (j % 2) * 512:(j % 2) * 512 + 512],
                                scalar=0.0, in1=x_sb[:, j * 512:(j + 1) * 512],
                                op0=ALU.add, op1=ALU.add)
                            s_ya.inc(ins)


def build_ffn_launch(kind, debug=False):
    nc = bass.Bass("TRN2", target_bir_lowering=False)
    F = 4096 if kind == "gelu" else 2816
    x_in = nc.dram_tensor("x_in", [D, TOK], F32, kind="ExternalInput").ap()
    wr = nc.dram_tensor("wr", [D, 8], F32, kind="ExternalInput").ap()
    x_out = nc.dram_tensor("x_out", [D, TOK], F32, kind="ExternalOutput").ap()
    W = {"F": F, "wr": wr}
    if kind == "gelu":
        w1 = nc.dram_tensor("w1", [8, D, F], F32, kind="ExternalInput").ap()
        w2 = nc.dram_tensor("w2", [8, F, D], F32, kind="ExternalInput").ap()
        W["b1T"] = nc.dram_tensor("b1T", [128, 8 * (F // 128)], F32, kind="ExternalInput").ap()
        W["b2"] = nc.dram_tensor("b2", [8, D], F32, kind="ExternalInput").ap()
        w1bf = nc.dram_tensor("w1bf", [8, D, F], BF16).ap()
        w2bf = nc.dram_tensor("w2bf", [8, F, D], BF16).ap()
        W["w1bf"], W["w2bf"] = w1bf, w2bf
        cast_pairs = [(w1.rearrange("e d f -> (e d) f"), w1bf.rearrange("e d f -> (e d) f")),
                      (w2.rearrange("e f d -> (e f) d"), w2bf.rearrange("e f d -> (e f) d"))]
    else:
        wg = nc.dram_tensor("wg", [8, D, F], F32, kind="ExternalInput").ap()
        wu = nc.dram_tensor("wu", [8, D, F], F32, kind="ExternalInput").ap()
        wd = nc.dram_tensor("wd", [8, F, D], F32, kind="ExternalInput").ap()
        wgbf = nc.dram_tensor("wgbf", [8, D, F], BF16).ap()
        wubf = nc.dram_tensor("wubf", [8, D, F], BF16).ap()
        wdbf = nc.dram_tensor("wdbf", [8, F, D], BF16).ap()
        W["wgbf"], W["wubf"], W["wdbf"] = wgbf, wubf, wdbf
        cast_pairs = [(wg.rearrange("e d f -> (e d) f"), wgbf.rearrange("e d f -> (e d) f")),
                      (wu.rearrange("e d f -> (e d) f"), wubf.rearrange("e d f -> (e d) f")),
                      (wd.rearrange("e f d -> (e f) d"), wdbf.rearrange("e f d -> (e f) d"))]
    xnT = nc.dram_tensor("xnT", [D, TOK], BF16).ap()
    W["gatesT_dram"] = nc.dram_tensor("gatesT", [8, TOK], BF16).ap()
    if debug:
        xn_dbg = nc.dram_tensor("xn_dbg", [D, TOK], BF16, kind="ExternalOutput").ap()
        g_dbg = nc.dram_tensor("g_dbg", [8, TOK], BF16, kind="ExternalOutput").ap()
        W["rdbg"] = {name: nc.dram_tensor(f"r_{name}", [128, NTT * 8] if name != "den"
                                          else [128, NTT], F32, kind="ExternalOutput").ap()
                     for name in ("lg", "er", "P", "mskK", "den", "gat")}
    with ExitStack() as es:
        C = make_consts(nc, es)
        emit_precast(nc, "pc", cast_pairs)
        lg = es.enter_context(nc.sbuf_tensor("lgr", [128, NTT * 8], F32))
        emit_ln(nc, "ln", x_in, xnT, C, router=(wr, lg))
        emit_router(nc, "rt", lg, W["gatesT_dram"], C,
                    eps=(1e-9 if kind == "gelu" else 0.0), dbg=W.get("rdbg"))
        emit_moe_ffn(nc, "ffn", kind, xnT, x_in, x_out, C, W)
        if debug:
            with nc.Block() as blk:
                @blk.sync
                def _(sp):
                    s_dbg = nc.semaphore("s_dbg").__enter__()
                    sp.dma_start(out=xn_dbg, in_=xnT).then_inc(s_dbg, 16)
                    sp.dma_start(out=g_dbg, in_=W["gatesT_dram"]).then_inc(s_dbg, 16)
                    sp.wait_ge(s_dbg, 32)
    return nc


def host_ffn_inputs(kind, x_bct, inp):
    maps = []
    for c in range(NCORES):
        m = {"x_in": np.ascontiguousarray(x_bct[c])}
        if kind == "gelu":
            F = 4096
            m["wr"] = inp["L0_ffn_Wg"]
            m["w1"] = inp["L0_ffn_W1"]
            m["w2"] = inp["L0_ffn_W2"]
            m["b1T"] = np.ascontiguousarray(
                inp["L0_ffn_b1"].reshape(8, F // 128, 128).transpose(2, 0, 1).reshape(128, -1))
            m["b2"] = inp["L0_ffn_b2"]
        else:
            m["wr"] = inp["L1_ffn_Wr"]
            m["wg"] = inp["L1_ffn_Wg"]
            m["wu"] = inp["L1_ffn_Wu"]
            m["wd"] = inp["L1_ffn_Wd"]
        maps.append(m)
    return maps


def build_l0_attn_launch(debug=False):
    nc = bass.Bass("TRN2", target_bir_lowering=False)
    x_in = nc.dram_tensor("x_in", [D, TOK], F32, kind="ExternalInput").ap()
    wq = nc.dram_tensor("wq", [D, 1024], F32, kind="ExternalInput").ap()
    wk = nc.dram_tensor("wk", [D, 1024], F32, kind="ExternalInput").ap()
    wv = nc.dram_tensor("wv", [D, 1024], F32, kind="ExternalInput").ap()
    x_out = nc.dram_tensor("x_out", [D, TOK], F32, kind="ExternalOutput").ap()
    xnT = nc.dram_tensor("xnT", [D, TOK], BF16).ap()
    with ExitStack() as es:
        C = make_consts(nc, es)
        emit_ln(nc, "ln", x_in, xnT, C)
        emit_l0_attn(nc, "at", x_in, xnT, {"wq": wq, "wk": wk, "wv": wv}, x_out, C)
    return nc


def host_l0_attn_inputs(x_bct, inp):
    W = inp["L0_attn_W"].reshape(1024, 16, 192)
    wq = np.ascontiguousarray(W[:, :, 0:64].reshape(1024, 1024))
    wk = np.ascontiguousarray(W[:, :, 64:128].reshape(1024, 1024))
    wv = np.ascontiguousarray(W[:, :, 128:192].reshape(1024, 1024))
    return [{"x_in": x_bct[c], "wq": wq, "wk": wk, "wv": wv} for c in range(NCORES)]


def emit_l1_attn(nc, tag, x_in, xnT_dram, W, x_out, C, aux=None):
    with ExitStack() as es:
        def sb(name, shape, dt=F32):
            return es.enter_context(nc.sbuf_tensor(f"{tag}_{name}", shape, dt))
        oT_sb = sb("oT", [128, NEXP * TOK], BF16)
        bqT = sb("bqT", [128, 8]); bkT = sb("bkT", [128, 8]); boT = sb("boT", [128, 8])
        bvf = sb("bvf", [1, D]); bvb = sb("bvb", [1, D], BF16)
        s_bl = Sem(nc, es, f"{tag}_bl")
        with nc.Block() as blk:
            @blk.sync
            def _(sp):
                sp.dma_start(out=bqT[:, :], in_=W["bqT"]).then_inc(s_bl.h, 16)
                sp.dma_start(out=bkT[:, :], in_=W["bkT"]).then_inc(s_bl.h, 16)
                sp.dma_start(out=boT[:, :], in_=W["boT"]).then_inc(s_bl.h, 16)
                sp.dma_start(out=bvf[:, :], in_=W["bv"]).then_inc(s_bl.h, 16)
                s_bl.wait(sp, 64)
        with nc.Block() as blk:
            @blk.vector
            def _(v):
                v.tensor_copy(out=bvb[:, :], in_=bvf[:, :])

        with ExitStack() as es2:
            def sb2(name, shape, dt=F32):
                return es2.enter_context(nc.sbuf_tensor(f"{tag}_{name}", shape, dt))
            wq = sb2("wq", [128, NCH * 1024], BF16)
            wk = sb2("wk", [128, NCH * 1024], BF16)
            wv = sb2("wv", [128, NCH * 1024], BF16)
            jobs = []
            for name, dst in (("wqT", wq), ("wkT", wk), ("wvT", wv)):
                for c in range(NCH):
                    jobs.append((W[name][c * 128:(c + 1) * 128, :],
                                 dst[:, c * 1024:(c + 1) * 1024]))
            emit_load_cast_w(nc, tag, jobs)

            qT = sb2("qT", [128, TOK], BF16)
            kT = sb2("kT", [128, TOK], BF16)
            v_g = sb2("v_g", [128, TOK], BF16)
            ktok = sb2("ktok", [128, TOK], BF16)
            xg = [sb2(f"xg{i}", [128, NCH * 512], BF16) for i in range(2)]
            qpre = [sb2(f"qpre{i}", [128, 512]) for i in range(2)]
            t0 = sb2("t0", [128, 512]); t1 = sb2("t1", [128, 512])
            eE = sb2("eE", [128, 512])
            ATm = [sb2(f"ATm{i}", [128, 128], BF16) for i in range(2)]
            S_bf = sb2("S_bf", [128, 64], BF16)
            psP = es2.enter_context(nc.psum_tensor(f"{tag}_psP", [128, 1024], F32))
            psAT = es2.enter_context(nc.psum_tensor(f"{tag}_psAT", [128, 1024], F32))
            psO = es2.enter_context(nc.psum_tensor(f"{tag}_psO", [128, 1024], F32))
            psS = es2.enter_context(nc.psum_tensor(f"{tag}_psS", [128, 1024], F32))
            s_xg = Sem(nc, es2, f"{tag}_xg")
            s_pm = Sem(nc, es2, f"{tag}_pm")
            s_pe1 = Sem(nc, es2, f"{tag}_pe1")
            s_mn = Sem(nc, es2, f"{tag}_mn")
            s_eE = Sem(nc, es2, f"{tag}_eE")
            s_fm = Sem(nc, es2, f"{tag}_fm")
            s_kt = Sem(nc, es2, f"{tag}_kt")
            s_at = Sem(nc, es2, f"{tag}_at")
            s_tm = Sem(nc, es2, f"{tag}_tm")
            s_om = Sem(nc, es2, f"{tag}_om")
            s_sm2 = Sem(nc, es2, f"{tag}_sm2")
            s_se = Sem(nc, es2, f"{tag}_se")
            s_oe = Sem(nc, es2, f"{tag}_oe")

            for g in range(NEXP):
                with nc.Block() as blk:
                    if aux is not None:
                        aux(blk, g)

                    @blk.sync
                    def _(sp):
                        for cg in range(NCG):
                            if g * 48 + cg * 6 >= 12:
                                s_pm.wait(sp, g * 48 + (cg - 1) * 6)
                            for c in range(NCH):
                                sp.dma_start(out=xg[cg % 2][:, c * 512:(c + 1) * 512],
                                             in_=xnT_dram[c * 128:(c + 1) * 128,
                                                          cg * 512:(cg + 1) * 512]
                                             ).then_inc(s_xg.h, 16)
                    s_xg.cnt += 16 * NCH * NCG

                    @blk.tensor
                    def _(t):
                        item = g * 48
                        for cg in range(NCG):
                            s_xg.wait(t, 16 * NCH * (g * NCG + cg + 1))
                            for wsb in (wq, wk):
                                if item >= 2:
                                    s_pe1.wait(t, item - 1)
                                for c in range(NCH):
                                    ins = t.matmul(
                                        psP[:, (item % 2) * 512:(item % 2) * 512 + 512],
                                        wsb[:, c * 1024 + g * 128:c * 1024 + (g + 1) * 128],
                                        xg[cg % 2][:, c * 512:(c + 1) * 512],
                                        start=(c == 0), stop=(c == NCH - 1))
                                s_pm.inc(ins)
                                item += 1
                            for i in range(4):
                                tt = cg * 4 + i
                                if item >= 2:
                                    s_pe1.wait(t, item - 1)
                                for c in range(NCH):
                                    t.matmul(
                                        psP[:, (item % 2) * 512:(item % 2) * 512 + 128],
                                        xg[cg % 2][:, c * 512 + i * 128:c * 512 + (i + 1) * 128],
                                        wv[:, c * 1024 + g * 128:c * 1024 + (g + 1) * 128],
                                        start=(c == 0), stop=False)
                                ins = t.matmul(
                                    psP[:, (item % 2) * 512:(item % 2) * 512 + 128],
                                    C["ones_row"][0:1, :],
                                    bvb[0:1, g * 128:(g + 1) * 128],
                                    start=False, stop=True)
                                s_pm.inc(ins)
                                item += 1

                    @blk.scalar
                    def _(s):
                        item = g * 48
                        for cg in range(NCG):
                            for which in range(2):
                                dstT = qT if which == 0 else kT
                                bias = bqT if which == 0 else bkT
                                s_pm.wait(s, item + 1)
                                s_pe1.inc(s.activation(
                                    out=qpre[item % 2][:, :],
                                    in_=psP[:, (item % 2) * 512:(item % 2) * 512 + 512],
                                    func=AF.Identity, bias=bias[:, g:g + 1]))
                                qk = g * 16 + cg * 2 + which
                                s_mn.wait(s, qk + 1)
                                if qk >= 1:
                                    s_fm.wait(s, qk)
                                s_eE.inc(s.activation(out=eE[:, :], in_=t0[:, :],
                                                      func=AF.Exp))
                                item += 1
                            for i in range(4):
                                s_pm.wait(s, item + 1)
                                tt = cg * 4 + i
                                s_pe1.inc(s.copy(v_g[:, tt * 128:(tt + 1) * 128],
                                                 psP[:, (item % 2) * 512:(item % 2) * 512 + 128]))
                                item += 1

                    @blk.vector
                    def _(v):
                        qk = g * 16
                        for cg in range(NCG):
                            for which in range(2):
                                item = g * 48 + cg * 6 + which
                                s_pe1.wait(v, item + 1)
                                if qk >= 1:
                                    s_eE.wait(v, qk)
                                v.tensor_scalar(out=t0[:, :], in0=qpre[item % 2][:, :],
                                                scalar1=0.0, scalar2=None, op0=ALU.min)
                                s_mn.inc(v.tensor_scalar(
                                    out=t1[:, :], in0=qpre[item % 2][:, :],
                                    scalar1=0.0, scalar2=None, op0=ALU.max))
                                dstT = qT if which == 0 else kT
                                s_eE.wait(v, qk + 1)
                                s_fm.inc(v.tensor_add(
                                    dstT[:, cg * 512:(cg + 1) * 512], eE[:, :], t1[:, :]))
                                qk += 1

                with nc.Block() as blk:
                    @blk.sync
                    def _(sp):
                        for tt in range(NTT):
                            sp.dma_start(out=ktok[:, tt * 128:(tt + 1) * 128],
                                         in_=kT[:, tt * 128:(tt + 1) * 128],
                                         transpose=True).then_inc(s_kt.h, 16)
                        sp.wait_ge(s_kt.h, s_kt.cnt + 16 * NTT)
                    s_kt.cnt += 16 * NTT

                with nc.Block() as blk:
                    @blk.tensor
                    def _(t):
                        for c in range(NTT):
                            for hi in range(2):
                                hb = hi * 64
                                i = (g * NTT + c) * 2 + hi
                                ib = (g * NTT * 2)
                                if i - ib >= 2:
                                    s_tm.wait(t, i - 1)
                                ins = t.matmul(psAT[:, (i % 2) * 512:(i % 2) * 512 + 128],
                                               kT[hb:hb + 64, c * 128:(c + 1) * 128],
                                               qT[hb:hb + 64, c * 128:(c + 1) * 128],
                                               start=True, stop=True)
                                s_at.inc(ins)
                                if i - ib >= 2:
                                    s_oe.wait(t, i - 1)
                                s_tm.wait(t, i + 1)
                                if c >= 1:
                                    s_se.wait(t, i - 1)
                                    t.matmul(psO[hb:hb + 64, hi * 512:hi * 512 + 128],
                                             S_bf[hb:hb + 64, :],
                                             qT[hb:hb + 64, c * 128:(c + 1) * 128],
                                             start=True, stop=False)
                                ins = t.matmul(psO[hb:hb + 64, hi * 512:hi * 512 + 128],
                                               v_g[:, c * 128 + hb:c * 128 + hb + 64],
                                               ATm[i % 2][:, :],
                                               start=(c == 0), stop=True)
                                s_om.inc(ins)
                                ins = t.matmul(psS[hb:hb + 64, hi * 512:hi * 512 + 64],
                                               ktok[:, c * 128 + hb:c * 128 + hb + 64],
                                               v_g[:, c * 128 + hb:c * 128 + hb + 64],
                                               start=(c == 0), stop=(c == NTT - 1))
                                s_sm2.inc(ins)

                    @blk.vector
                    def _(v):
                        for c in range(NTT):
                            for hi in range(2):
                                i = (g * NTT + c) * 2 + hi
                                ib = (g * NTT * 2)
                                s_at.wait(v, i + 1)
                                if i - ib >= 2:
                                    s_om.wait(v, i - 1)
                                s_tm.inc(v.tensor_tensor(out=ATm[i % 2][:, :],
                                                         in0=psAT[:, (i % 2) * 512:(i % 2) * 512 + 128],
                                                         in1=C["triu"][:, :], op=ALU.mult))

                    @blk.scalar
                    def _(s):
                        for c in range(NTT):
                            for hi in range(2):
                                hb = hi * 64
                                i = (g * NTT + c) * 2 + hi
                                s_om.wait(s, i + 1)
                                s_oe.inc(s.copy(
                                    oT_sb[hb:hb + 64, g * TOK + c * 128:g * TOK + (c + 1) * 128],
                                    psO[hb:hb + 64, hi * 512:hi * 512 + 128]))
                                s_sm2.wait(s, i + 1)
                                s_se.inc(s.copy(S_bf[hb:hb + 64, :],
                                                psS[hb:hb + 64, hi * 512:hi * 512 + 64]))

        with ExitStack() as es3:
            def sb3(name, shape, dt=F32):
                return es3.enter_context(nc.sbuf_tensor(f"{tag}_{name}", shape, dt))
            wo = sb3("wo", [128, NCH * 1024], BF16)
            jobs = [(W["woT"][c * 128:(c + 1) * 128, :], wo[:, c * 1024:(c + 1) * 1024])
                    for c in range(NCH)]
            emit_load_cast_w(nc, f"{tag}o", jobs)
            x_sb = sb3("x0", [128, TOK])
            psW = es3.enter_context(nc.psum_tensor(f"{tag}_psW", [128, 1024], F32))
            s_wm = Sem(nc, es3, f"{tag}_wm")
            s_wa = Sem(nc, es3, f"{tag}_wa")
            s_xl = Sem(nc, es3, f"{tag}_xl")
            s_xs = Sem(nc, es3, f"{tag}_xs")
            with nc.Block() as blk:
                @blk.sync
                def _(sp):
                    for dout in range(NCH):
                        if dout >= 1:
                            s_xs.wait(sp, 16 * NCH * dout)
                        for c in range(NCH):
                            sp.dma_start(out=x_sb[:, c * 512:(c + 1) * 512],
                                         in_=x_in[dout * 128:(dout + 1) * 128,
                                                  c * 512:(c + 1) * 512]).then_inc(s_xl.h, 16)
                        s_wa.wait(sp, NCG * (dout + 1))
                        for c in range(NCH):
                            sp.dma_start(out=x_out[dout * 128:(dout + 1) * 128,
                                                   c * 512:(c + 1) * 512],
                                         in_=x_sb[:, c * 512:(c + 1) * 512]).then_inc(s_xs.h, 16)
                    s_xs.wait(sp, 16 * NCH * NCH)
                s_xl.cnt += 16 * NCH * NCH
                s_xs.cnt += 16 * NCH * NCH

                @blk.tensor
                def _(t):
                    for dout in range(NCH):
                        for cg in range(NCG):
                            gi = dout * NCG + cg
                            if gi >= 2:
                                s_wa.wait(t, gi - 1)
                            for c in range(NCH):
                                ins = t.matmul(psW[:, (gi % 2) * 512:(gi % 2) * 512 + 512],
                                               wo[:, c * 1024 + dout * 128:c * 1024 + (dout + 1) * 128],
                                               oT_sb[:, c * TOK + cg * 512:c * TOK + (cg + 1) * 512],
                                               start=(c == 0), stop=(c == NCH - 1))
                            s_wm.inc(ins)

                @blk.vector
                def _(v):
                    for dout in range(NCH):
                        s_xl.wait(v, 16 * NCH * (dout + 1))
                        for cg in range(NCG):
                            gi = dout * NCG + cg
                            s_wm.wait(v, gi + 1)
                            ins = v.scalar_tensor_tensor(
                                out=x_sb[:, cg * 512:(cg + 1) * 512],
                                in0=psW[:, (gi % 2) * 512:(gi % 2) * 512 + 512],
                                scalar=boT[:, dout:dout + 1],
                                in1=x_sb[:, cg * 512:(cg + 1) * 512],
                                op0=ALU.add, op1=ALU.add)
                            s_wa.inc(ins)


def build_l1_attn_launch(debug=False):
    nc = bass.Bass("TRN2", target_bir_lowering=False)
    x_in = nc.dram_tensor("x_in", [D, TOK], F32, kind="ExternalInput").ap()
    names = ["wqT", "wkT", "wvT", "woT"]
    W = {n: nc.dram_tensor(n, [D, D], F32, kind="ExternalInput").ap() for n in names}
    for n in ("bqT", "bkT", "boT"):
        W[n] = nc.dram_tensor(n, [128, 8], F32, kind="ExternalInput").ap()
    W["bv"] = nc.dram_tensor("bv", [1, D], F32, kind="ExternalInput").ap()
    x_out = nc.dram_tensor("x_out", [D, TOK], F32, kind="ExternalOutput").ap()
    xnT = nc.dram_tensor("xnT", [D, TOK], BF16).ap()
    with ExitStack() as es:
        C = make_consts(nc, es)
        emit_ln(nc, "ln", x_in, xnT, C)
        emit_l1_attn(nc, "la", x_in, xnT, W, x_out, C)
    return nc


def host_l1_attn_inputs(x_bct, inp):
    m = {}
    m["wqT"] = np.ascontiguousarray(inp["L1_attn_Wq"].T)
    m["wkT"] = np.ascontiguousarray(inp["L1_attn_Wk"].T)
    m["wvT"] = np.ascontiguousarray(inp["L1_attn_Wv"].T)
    m["woT"] = np.ascontiguousarray(inp["L1_attn_Wo"].T)
    m["bqT"] = np.ascontiguousarray(inp["L1_attn_bq"].reshape(8, 128).T)
    m["bkT"] = np.ascontiguousarray(inp["L1_attn_bk"].reshape(8, 128).T)
    m["boT"] = np.ascontiguousarray(inp["L1_attn_bo"].reshape(8, 128).T)
    m["bv"] = inp["L1_attn_bv"].reshape(1, 1024)
    return [dict(m, x_in=x_bct[c]) for c in range(NCORES)]


_BUILD_CACHE = {}


def _get(name, fn):
    if name not in _BUILD_CACHE:
        _BUILD_CACHE[name] = fn()
    return _BUILD_CACHE[name]


def build_fused():
    nc = bass.Bass("TRN2", target_bir_lowering=False)
    x0 = nc.dram_tensor("x0", [D, TOK], F32, kind="ExternalInput").ap()
    out = nc.dram_tensor("out", [D, TOK], F32, kind="ExternalOutput").ap()
    wq = nc.dram_tensor("wq", [D, 1024], F32, kind="ExternalInput").ap()
    wk = nc.dram_tensor("wk", [D, 1024], F32, kind="ExternalInput").ap()
    wv = nc.dram_tensor("wv", [D, 1024], F32, kind="ExternalInput").ap()
    f0_wr = nc.dram_tensor("f0_wr", [D, 8], F32, kind="ExternalInput").ap()
    f0_w1 = nc.dram_tensor("f0_w1", [8, D, 4096], F32, kind="ExternalInput").ap()
    f0_w2 = nc.dram_tensor("f0_w2", [8, 4096, D], F32, kind="ExternalInput").ap()
    f0_b1T = nc.dram_tensor("f0_b1T", [128, 8 * 32], F32, kind="ExternalInput").ap()
    f0_b2 = nc.dram_tensor("f0_b2", [8, D], F32, kind="ExternalInput").ap()
    WA = {}
    for n in ("wqT", "wkT", "wvT", "woT"):
        WA[n] = nc.dram_tensor(n, [D, D], F32, kind="ExternalInput").ap()
    for n in ("bqT", "bkT", "boT"):
        WA[n] = nc.dram_tensor(n, [128, 8], F32, kind="ExternalInput").ap()
    WA["bv"] = nc.dram_tensor("bv", [1, D], F32, kind="ExternalInput").ap()
    f1_wr = nc.dram_tensor("f1_wr", [D, 8], F32, kind="ExternalInput").ap()
    f1_wg = nc.dram_tensor("f1_wg", [8, D, 2816], F32, kind="ExternalInput").ap()
    f1_wu = nc.dram_tensor("f1_wu", [8, D, 2816], F32, kind="ExternalInput").ap()
    f1_wd = nc.dram_tensor("f1_wd", [8, 2816, D], F32, kind="ExternalInput").ap()
    x1 = nc.dram_tensor("x1", [D, TOK], F32).ap()
    x2 = nc.dram_tensor("x2", [D, TOK], F32).ap()
    x3 = nc.dram_tensor("x3", [D, TOK], F32).ap()
    xnT = nc.dram_tensor("xnT", [D, TOK], BF16).ap()
    gatesT = nc.dram_tensor("gatesT", [8, TOK], BF16).ap()
    w1bf = nc.dram_tensor("w1bf", [8, D, 4096], BF16).ap()
    w2bf = nc.dram_tensor("w2bf", [8, 4096, D], BF16).ap()
    wgbf = nc.dram_tensor("wgbf", [8, D, 2816], BF16).ap()
    wubf = nc.dram_tensor("wubf", [8, D, 2816], BF16).ap()
    wdbf = nc.dram_tensor("wdbf", [8, 2816, D], BF16).ap()

    W0 = {"F": 4096, "wr": f0_wr, "w1bf": w1bf, "w2bf": w2bf,
          "b1T": f0_b1T, "b2": f0_b2, "gatesT_dram": gatesT}
    W1 = {"F": 2816, "wr": f1_wr, "wgbf": wgbf, "wubf": wubf, "wdbf": wdbf,
          "gatesT_dram": gatesT}

    with ExitStack() as es:
        C = make_consts(nc, es)
        lg = es.enter_context(nc.sbuf_tensor("lgr", [128, NTT * 8], F32))
        emit_ln(nc, "ln0", x0, xnT, C)
        with ExitStack() as esA:
            pc0 = PoolPrecast(nc, esA, es, "pc0", [
                (f0_w1.rearrange("e d f -> (e d) f"), w1bf.rearrange("e d f -> (e d) f")),
                (f0_w2.rearrange("e f d -> (e f) d"), w2bf.rearrange("e f d -> (e f) d"))],
                maxw=2048)
            npg = (len(pc0.jobs) + NEXP - 1) // NEXP
            emit_l0_attn(nc, "at0", x0, xnT, {"wq": wq, "wk": wk, "wv": wv}, x1, C,
                         aux=lambda blk, g: pc0.emit(blk, npg))
            dep0 = pc0.dep()
        emit_ln(nc, "ln1", x1, xnT, C, router=(f0_wr, lg))
        emit_router(nc, "rt1", lg, gatesT, C, eps=1e-9)
        emit_moe_ffn(nc, "ff1", "gelu", xnT, x1, x2, C, W0, deps=[dep0])
        emit_ln(nc, "ln2", x2, xnT, C)
        with ExitStack() as esB:
            pc1 = PoolPrecast(nc, esB, es, "pc1", [
                (f1_wg.rearrange("e d f -> (e d) f"), wgbf.rearrange("e d f -> (e d) f")),
                (f1_wu.rearrange("e d f -> (e d) f"), wubf.rearrange("e d f -> (e d) f")),
                (f1_wd.rearrange("e f d -> (e f) d"), wdbf.rearrange("e f d -> (e f) d"))],
                maxw=1024)
            npg1 = (len(pc1.jobs) + NEXP - 1) // NEXP
            emit_l1_attn(nc, "la2", x2, xnT, WA, x3, C,
                         aux=lambda blk, g: pc1.emit(blk, npg1))
            dep1 = pc1.dep()
        emit_ln(nc, "ln3", x3, xnT, C, router=(f1_wr, lg))
        emit_router(nc, "rt3", lg, gatesT, C, eps=0.0)
        emit_moe_ffn(nc, "ff3", "swiglu", xnT, x3, out, C, W1, deps=[dep1])
    return nc


def host_fused_inputs(x_bct, inp):
    W = inp["L0_attn_W"].reshape(1024, 16, 192)
    base = {
        "wq": np.ascontiguousarray(W[:, :, 0:64].reshape(1024, 1024)),
        "wk": np.ascontiguousarray(W[:, :, 64:128].reshape(1024, 1024)),
        "wv": np.ascontiguousarray(W[:, :, 128:192].reshape(1024, 1024)),
        "f0_wr": inp["L0_ffn_Wg"],
        "f0_w1": inp["L0_ffn_W1"],
        "f0_w2": inp["L0_ffn_W2"],
        "f0_b1T": np.ascontiguousarray(
            inp["L0_ffn_b1"].reshape(8, 32, 128).transpose(2, 0, 1).reshape(128, 256)),
        "f0_b2": inp["L0_ffn_b2"],
        "wqT": np.ascontiguousarray(inp["L1_attn_Wq"].T),
        "wkT": np.ascontiguousarray(inp["L1_attn_Wk"].T),
        "wvT": np.ascontiguousarray(inp["L1_attn_Wv"].T),
        "woT": np.ascontiguousarray(inp["L1_attn_Wo"].T),
        "bqT": np.ascontiguousarray(inp["L1_attn_bq"].reshape(8, 128).T),
        "bkT": np.ascontiguousarray(inp["L1_attn_bk"].reshape(8, 128).T),
        "boT": np.ascontiguousarray(inp["L1_attn_bo"].reshape(8, 128).T),
        "bv": inp["L1_attn_bv"].reshape(1, 1024),
        "f1_wr": inp["L1_ffn_Wr"],
        "f1_wg": inp["L1_ffn_Wg"],
        "f1_wu": inp["L1_ffn_Wu"],
        "f1_wd": inp["L1_ffn_Wd"],
    }
    return [dict(base, x0=np.ascontiguousarray(x_bct[c])) for c in range(NCORES)]


def kernel_fused(**inputs):
    x = inputs["L0_attn_x"]
    x_bct = np.ascontiguousarray(x.transpose(0, 2, 1))
    nc = _get("fused", build_fused)
    maps = host_fused_inputs(x_bct, inputs)
    res = run_bass_kernel_spmd(nc, maps, core_ids=list(range(NCORES)))
    out = np.stack([np.asarray(res.results[c]["out"]).T for c in range(NCORES)])
    return np.ascontiguousarray(out)


def kernel(**inputs):
    return kernel_fused(**inputs)


def kernel_multilaunch(**inputs):
    x = inputs["L0_attn_x"]
    x_bct = np.ascontiguousarray(x.transpose(0, 2, 1))

    core_ids = list(range(NCORES))

    nc1 = _get("l0a", build_l0_attn_launch)
    res = run_bass_kernel_spmd(nc1, host_l0_attn_inputs(x_bct, inputs), core_ids=core_ids)
    x_bct = [np.asarray(res.results[c]["x_out"]) for c in range(NCORES)]

    nc2 = _get("l0f", lambda: build_ffn_launch("gelu"))
    maps = host_ffn_inputs("gelu", x_bct, inputs)
    res = run_bass_kernel_spmd(nc2, maps, core_ids=core_ids)
    x_bct = [np.asarray(res.results[c]["x_out"]) for c in range(NCORES)]

    nc3 = _get("l1a", build_l1_attn_launch)
    res = run_bass_kernel_spmd(nc3, host_l1_attn_inputs(x_bct, inputs), core_ids=core_ids)
    x_bct = [np.asarray(res.results[c]["x_out"]) for c in range(NCORES)]

    nc4 = _get("l1f", lambda: build_ffn_launch("swiglu"))
    maps = host_ffn_inputs("swiglu", x_bct, inputs)
    res = run_bass_kernel_spmd(nc4, maps, core_ids=core_ids)
    out = np.stack([np.asarray(res.results[c]["x_out"]).T for c in range(NCORES)])
    return np.ascontiguousarray(out)


# version 04
# speedup vs baseline: 1.1230x; 1.1230x over previous
import numpy as np
from contextlib import ExitStack
import concourse.bass as bass
import concourse.mybir as mybir
from concourse.ap import AP
from concourse.bass_utils import run_bass_kernel_spmd

F32 = mybir.dt.float32
BF16 = mybir.dt.bfloat16
AF = mybir.ActivationFunctionType
ALU = mybir.AluOpType
AX = mybir.AxisListType

TOK = 4096
D = 1024
NCH = D // 128
NCG = TOK // 512
NTT = TOK // 128
NCORES = 8
NEXP = 8


def bfree(ap, n):
    return AP(ap.tensor, ap.offset, [list(p) for p in ap.ap] + [[0, n]])


_SEMCNT = {}


class Sem:
    def __init__(self, nc, es, name):
        self.h = es.enter_context(nc.semaphore(name))
        self._key = (id(nc), int(self.h.num))
        self._base = _SEMCNT.get(self._key, 0)
        self._rel = 0

    @property
    def cnt(self):
        return self._base + self._rel

    @cnt.setter
    def cnt(self, v):
        self._rel = v - self._base
        _SEMCNT[self._key] = self._base + self._rel

    def inc(self, instr, k=1):
        instr.then_inc(self.h, k)
        self.cnt = self.cnt + k
        return self.cnt

    def wait(self, eng, v):
        eng.wait_ge(self.h, self._base + v)


def make_consts(nc, es):
    C = {}
    def sb(name, shape, dt):
        h = es.enter_context(nc.sbuf_tensor(name, shape, dt))
        return h
    C["ones_col"] = sb("c_ones_col", [128, 1], BF16)
    C["ones_col_f"] = sb("c_ones_col_f", [128, 1], F32)
    C["ones_row"] = sb("c_ones_row", [1, 128], BF16)
    C["ones_row_f"] = sb("c_ones_row_f", [1, 128], F32)
    C["ones2_f"] = sb("c_ones2_f", [128, 128], F32)
    C["ident"] = sb("c_ident", [128, 128], BF16)
    C["ident_f"] = sb("c_ident_f", [128, 128], F32)
    C["triu"] = sb("c_triu", [128, 128], F32)
    C["E8"] = sb("c_E8", [8, 8 * 128], BF16)
    iota_c = sb("c_iota_c", [128, 128], F32)
    iota_p = sb("c_iota_p", [128, 1], F32)
    iota_p8 = sb("c_iota_p8", [8, 128], F32)

    with nc.Block() as blk:
        @blk.gpsimd
        def _(g):
            g.iota(iota_c[:, :], [[1, 128]], channel_multiplier=0,
                   allow_small_or_imprecise_dtypes=True)
            g.iota(iota_p[:, :], [[0, 1]], channel_multiplier=1,
                   allow_small_or_imprecise_dtypes=True)
            g.iota(iota_p8[:, :], [[0, 128]], channel_multiplier=1,
                   allow_small_or_imprecise_dtypes=True)

        @blk.vector
        def _(v):
            v.memset(C["ones_col"][:, :], 1.0)
            v.memset(C["ones_col_f"][:, :], 1.0)
            v.memset(C["ones_row"][:, :], 1.0)
            v.memset(C["ones_row_f"][:, :], 1.0)
            v.memset(C["ones2_f"][:, :], 1.0)

    with nc.Block() as blk:
        @blk.vector
        def _(v):
            v.tensor_scalar(out=C["ident"][:, :], in0=iota_c[:, :],
                            scalar1=iota_p[:, 0:1], scalar2=None, op0=ALU.is_equal)
            v.tensor_scalar(out=C["ident_f"][:, :], in0=iota_c[:, :],
                            scalar1=iota_p[:, 0:1], scalar2=None, op0=ALU.is_equal)
            v.tensor_scalar(out=C["triu"][:, :], in0=iota_c[:, :],
                            scalar1=iota_p[:, 0:1], scalar2=None, op0=ALU.is_ge)
            for e in range(8):
                v.tensor_scalar(out=C["E8"][:, e * 128:(e + 1) * 128],
                                in0=iota_p8[:, :], scalar1=float(e), scalar2=None,
                                op0=ALU.is_equal)
    return C


def emit_ln(nc, tag, x_dram, xnT_dram, C, router=None):
    with ExitStack() as es:
        def sb(name, shape, dt=F32):
            return es.enter_context(nc.sbuf_tensor(f"{tag}_{name}", shape, dt))
        buf = [sb("xbufA", [128, TOK]), sb("xbufB", [128, TOK])]
        xsq = [sb("xsqA", [128, TOK]), sb("xsqB", [128, TOK])]
        ssum = sb("ssum", [1, TOK]); ssq = sb("ssq", [1, TOK])
        rstd = sb("rstd", [1, TOK]); negmr = sb("negmr", [1, TOK])
        epsc = sb("epsc", [1, 1])
        rstd_bc = sb("rstd_bc", [128, TOK]); negmr_bc = sb("negmr_bc", [128, TOK])
        xnb = [sb("xnbA", [128, TOK], BF16), sb("xnbB", [128, TOK], BF16)]
        if router is not None:
            wr_dram, lg = router
            xnf = sb("xnf", [128, TOK])
            wr_sb = sb("wr", [128, NCH * 8])
        es8 = ExitStack()
        ps8 = es8.enter_context(nc.psum_tensor(f"{tag}_ps8", [1, TOK], F32))
        s_ld = Sem(nc, es, f"{tag}_ld")
        s_pr = Sem(nc, es, f"{tag}_pr")
        s_pe = Sem(nc, es, f"{tag}_pe")
        s_bm = Sem(nc, es, f"{tag}_bm")
        s_be = Sem(nc, es, f"{tag}_be")
        s_n = Sem(nc, es, f"{tag}_n")
        s_wr = Sem(nc, es, f"{tag}_wr")
        s_rm = Sem(nc, es, f"{tag}_rm")

        with nc.Block() as blk:
            @blk.vector
            def _(v):
                v.memset(epsc[0:1, 0:1], 1e-5)
            if router is not None:
                @blk.sync
                def _(sp):
                    for c in range(NCH):
                        sp.dma_start(out=wr_sb[:, c * 8:(c + 1) * 8],
                                     in_=wr_dram[c * 128:(c + 1) * 128, :]).then_inc(s_ld.h, 16)
                    s_ld.wait(sp, 16 * NCH)
                s_ld.cnt += 16 * NCH

        nld0 = s_ld._rel
        with nc.Block() as blk:
            @blk.sync
            def _(sp):
                for c in range(NCH):
                    if c >= 2:
                        s_pe.wait(sp, c - 1)
                    sp.dma_start(out=buf[c % 2][:, :],
                                 in_=x_dram[c * 128:(c + 1) * 128, :]).then_inc(s_ld.h, 16)
            s_ld.cnt += 16 * NCH

            @blk.tensor
            def _(t):
                for c in range(NCH):
                    s_ld.wait(t, nld0 + 16 * (c + 1))
                    for g in range(NCG):
                        ins = t.matmul(ps8[0:1, g * 512:(g + 1) * 512],
                                       C["ones_col_f"][:, :],
                                       buf[c % 2][:, g * 512:(g + 1) * 512],
                                       start=(c == 0), stop=(c == NCH - 1))
                    s_pe.inc(ins)

            @blk.scalar
            def _(s):
                s_pe.wait(s, NCH)
                for g in range(NCG):
                    s.copy(ssum[0:1, g * 512:(g + 1) * 512],
                           ps8[0:1, g * 512:(g + 1) * 512])

        with nc.Block() as blk:
            @blk.sync
            def _(sp):
                for c in range(NCH):
                    if c >= 2:
                        s_pr.wait(sp, c - 1)
                    sp.dma_start(out=buf[c % 2][:, :],
                                 in_=x_dram[c * 128:(c + 1) * 128, :]).then_inc(s_ld.h, 16)
            s_ld.cnt += 16 * NCH

            @blk.scalar
            def _(s):
                for c in range(NCH):
                    s_ld.wait(s, nld0 + 16 * (NCH + c + 1))
                    if c >= 2:
                        s_pe.wait(s, NCH + c - 1)
                    s_pr.inc(s.square(xsq[c % 2][:, :], buf[c % 2][:, :]))
                s_pe.wait(s, 2 * NCH)
                for g in range(NCG):
                    s.copy(ssq[0:1, g * 512:(g + 1) * 512],
                           ps8[0:1, g * 512:(g + 1) * 512])

            @blk.tensor
            def _(t):
                for c in range(NCH):
                    s_pr.wait(t, c + 1)
                    for g in range(NCG):
                        ins = t.matmul(ps8[0:1, g * 512:(g + 1) * 512],
                                       C["ones_col_f"][:, :],
                                       xsq[c % 2][:, g * 512:(g + 1) * 512],
                                       start=(c == 0), stop=(c == NCH - 1))
                    s_pe.inc(ins)

        with nc.Block() as blk:
            @blk.scalar
            def _(s):
                s.mul(ssum[0:1, :], ssum[0:1, :], 1.0 / D)
                s.mul(ssq[0:1, :], ssq[0:1, :], 1.0 / D)
                s_rm.inc(s.square(rstd[0:1, :], ssum[0:1, :]))
            @blk.vector
            def _(v):
                v.wait_ge(s_rm.h, s_rm.cnt)
                v.tensor_sub(ssq[0:1, :], ssq[0:1, :], rstd[0:1, :])
        with nc.Block() as blk:
            @blk.scalar
            def _(s):
                s_rm.inc(s.activation(out=ssq[0:1, :], in_=ssq[0:1, :], func=AF.Sqrt,
                                      bias=epsc[0:1, 0:1]))
            @blk.vector
            def _(v):
                v.wait_ge(s_rm.h, s_rm.cnt)
                s_rm.inc(v.reciprocal(rstd[0:1, :], ssq[0:1, :]))
                v.wait_ge(s_rm.h, s_rm.cnt)
                s_rm.inc(v.tensor_mul(negmr[0:1, :], ssum[0:1, :], rstd[0:1, :]))
                v.wait_ge(s_rm.h, s_rm.cnt)
                v.tensor_scalar(out=negmr[0:1, :], in0=negmr[0:1, :],
                                scalar1=-1.0, scalar2=None, op0=ALU.mult)

        es8.close()
        psb = es.enter_context(nc.psum_tensor(f"{tag}_psb", [128, 1024], F32))
        if router is not None:
            psr = es.enter_context(nc.psum_tensor(f"{tag}_psr", [128, NTT * 8], F32))
        with nc.Block() as blk:
            @blk.tensor
            def _(t):
                for i in range(2 * NCG):
                    g, which = i % NCG, i // NCG
                    src = rstd if which == 0 else negmr
                    if i >= 2:
                        s_be.wait(t, i - 1)
                    ins = t.matmul(psb[:, (i % 2) * 512:(i % 2) * 512 + 512],
                                   C["ones_row_f"][0:1, :],
                                   src[0:1, g * 512:(g + 1) * 512],
                                   start=True, stop=True)
                    s_bm.inc(ins)

            @blk.scalar
            def _(s):
                for i in range(2 * NCG):
                    g, which = i % NCG, i // NCG
                    dst = rstd_bc if which == 0 else negmr_bc
                    s_bm.wait(s, i + 1)
                    s_be.inc(s.copy(dst[:, g * 512:(g + 1) * 512],
                                    psb[:, (i % 2) * 512:(i % 2) * 512 + 512]))

        nld1 = s_ld._rel
        kst = 2
        with nc.Block() as blk:
            @blk.sync
            def _(sp):
                def store(c):
                    s_n.wait(sp, kst * (c + 1))
                    sp.dma_start(out=xnT_dram[c * 128:(c + 1) * 128, :],
                                 in_=xnb[c % 2][:, :]).then_inc(s_wr.h, 16)
                for c in range(NCH):
                    if c >= 2:
                        s_n.wait(sp, kst * (c - 1))
                    sp.dma_start(out=buf[c % 2][:, :],
                                 in_=x_dram[c * 128:(c + 1) * 128, :]).then_inc(s_ld.h, 16)
                    if c >= 2:
                        store(c - 2)
                store(NCH - 2)
                store(NCH - 1)
                s_wr.wait(sp, 16 * NCH)
            s_ld.cnt += 16 * NCH
            s_wr.cnt += 16 * NCH

            @blk.vector
            def _(v):
                for c in range(NCH):
                    s_ld.wait(v, nld1 + 16 * (c + 1))
                    if c >= 2:
                        s_wr.wait(v, 16 * (c - 1))
                    if router is not None and c >= 1:
                        s_pe.wait(v, 2 * NCH + c)
                    v.tensor_mul(buf[c % 2][:, :], buf[c % 2][:, :], rstd_bc[:, :])
                    if router is not None:
                        s_rm.inc(v.tensor_add(xnf[:, :], buf[c % 2][:, :],
                                              negmr_bc[:, :]))
                    s_n.inc(v.tensor_add(xnb[c % 2][:, :], buf[c % 2][:, :],
                                         negmr_bc[:, :]), 2)

            if router is not None:
                @blk.tensor
                def _(t):
                    nrm0 = s_rm.cnt - NCH
                    for c in range(NCH):
                        s_rm.wait(t, nrm0 + c + 1)
                        for tt in range(NTT):
                            ins = t.matmul(psr[:, tt * 8:(tt + 1) * 8],
                                           xnf[:, tt * 128:(tt + 1) * 128],
                                           wr_sb[:, c * 8:(c + 1) * 8],
                                           start=(c == 0 and tt == 0),
                                           stop=(c == NCH - 1 and tt == NTT - 1),
                                           skip_group_check=True)
                        s_pe.inc(ins)

                @blk.scalar
                def _(s):
                    s_pe.wait(s, 2 * NCH + NCH)
                    s.copy(lg[:, :], psr[:, :])


def emit_precast(nc, tag, pairs):
    with ExitStack() as es:
        s_ld = Sem(nc, es, f"{tag}_cld")
        s_cv = Sem(nc, es, f"{tag}_ccv")
        s_st = Sem(nc, es, f"{tag}_cst")
        wmax = max(src.shape[-1] for src, dst in pairs)
        stg = [es.enter_context(nc.sbuf_tensor(f"{tag}_cstg{i}", [128, wmax], F32))
               for i in range(2)]
        stb = [es.enter_context(nc.sbuf_tensor(f"{tag}_cstb{i}", [128, wmax], BF16))
               for i in range(2)]
        jobs = []
        for src, dst in pairs:
            R, W = src.shape
            assert R % 128 == 0
            for r in range(R // 128):
                jobs.append((src[r * 128:(r + 1) * 128, :],
                             dst[r * 128:(r + 1) * 128, :], W))
        with nc.Block() as blk:
            @blk.sync
            def _(sp):
                n = len(jobs)
                for i in range(n + 1):
                    if i < n:
                        s_ap, d_ap, W = jobs[i]
                        if i >= 2:
                            s_cv.wait(sp, i - 1)
                        sp.dma_start(out=stg[i % 2][:, 0:W], in_=s_ap).then_inc(s_ld.h, 16)
                    if i >= 1:
                        s_ap, d_ap, W = jobs[i - 1]
                        s_cv.wait(sp, i)
                        sp.dma_start(out=d_ap, in_=stb[(i - 1) % 2][:, 0:W]).then_inc(s_st.h, 16)
                s_st.wait(sp, 16 * len(jobs))
            s_ld.cnt += 16 * len(jobs)
            s_st.cnt += 16 * len(jobs)

            @blk.gpsimd
            def _(g):
                for i, (s_ap, d_ap, W) in enumerate(jobs):
                    s_ld.wait(g, 16 * (i + 1))
                    if i >= 2:
                        s_st.wait(g, 16 * (i - 1))
                    s_cv.inc(g.tensor_copy(out=stb[i % 2][:, 0:W],
                                           in_=stg[i % 2][:, 0:W]))


class PoolPrecast:
    def __init__(self, nc, es_buf, es_sem, tag, pairs, maxw=2048):
        self.nc = nc
        self.jobs = []
        for src, dst in pairs:
            R, Wd = src.shape
            assert R % 128 == 0
            for r in range(R // 128):
                for w0 in range(0, Wd, maxw):
                    w1 = min(Wd, w0 + maxw)
                    self.jobs.append((src[r * 128:(r + 1) * 128, w0:w1],
                                      dst[r * 128:(r + 1) * 128, w0:w1], w1 - w0))
        self.stg = [es_buf.enter_context(nc.sbuf_tensor(f"{tag}_pstg{i}", [128, maxw], F32))
                    for i in range(2)]
        self.stb = [es_buf.enter_context(nc.sbuf_tensor(f"{tag}_pstb{i}", [128, maxw], BF16))
                    for i in range(2)]
        self.s_in = Sem(nc, es_sem, f"{tag}_pin")
        self.s_out = Sem(nc, es_sem, f"{tag}_pout")
        self.pos = 0
        self.issued = 0

    def emit(self, blk, n=None):
        left = len(self.jobs) - self.pos
        n = left if n is None else min(n, left)
        if n <= 0:
            return
        i0, i1 = self.pos, self.pos + n
        self.pos = i1

        @blk.gpsimd
        def _(g):
            for i in range(i0, i1):
                while self.issued < min(i + 2, i1):
                    j = self.issued
                    src, dst, W = self.jobs[j]
                    g.dma_start(out=self.stg[j % 2][:, 0:W], in_=src).then_inc(self.s_in.h, 16)
                    self.s_in.cnt += 16
                    self.issued += 1
                src, dst, W = self.jobs[i]
                self.s_in.wait(g, 16 * (i + 1))
                if i >= 2:
                    self.s_out.wait(g, 16 * (i - 1))
                g.tensor_copy(out=self.stb[i % 2][:, 0:W], in_=self.stg[i % 2][:, 0:W])
                g.dma_start(out=dst, in_=self.stb[i % 2][:, 0:W]).then_inc(self.s_out.h, 16)
                self.s_out.cnt += 16

    def dep(self):
        return (self.s_out, 16 * len(self.jobs))



def emit_router(nc, tag, lg, gatesT_dram, C, eps, dbg=None):
    with ExitStack() as es:
        def sb(name, shape, dt=F32):
            return es.enter_context(nc.sbuf_tensor(f"{tag}_{name}", shape, dt))
        m = sb("m", [128, NTT]); d0 = sb("d0", [128, NTT * 8])
        er = sb("er", [128, NTT * 8]); ssm = sb("ssm", [128, NTT])
        rs = sb("rs", [128, NTT]); P = sb("P", [128, NTT * 8])
        msk1 = sb("msk1", [128, NTT * 8]); t1 = sb("t1", [128, NTT * 8])
        Pm = sb("Pm", [128, NTT * 8]); m2 = sb("m2", [128, NTT])
        mskK = sb("mskK", [128, NTT * 8]); Ptop = sb("Ptop", [128, NTT * 8])
        den = sb("den", [128, NTT]); rden = sb("rden", [128, NTT])
        gat = sb("gat", [128, NTT * 8])
        pst = es.enter_context(nc.psum_tensor(f"{tag}_pst", [8, 2048], F32))
        s_tr = Sem(nc, es, f"{tag}_tr")
        s_te = Sem(nc, es, f"{tag}_te")
        s_ch = Sem(nc, es, f"{tag}_ch")
        s_gw = Sem(nc, es, f"{tag}_gw")

        def r3(apx):
            return apx[:, :].rearrange("p (t e) -> p t e", e=8)

        with nc.Block() as blk:
            @blk.vector
            def _(v):
                def step(ins):
                    s_ch.inc(ins)
                    v.wait_ge(s_ch.h, s_ch.cnt)
                step(v.tensor_reduce(out=m[:, :], in_=r3(lg), axis=AX.X, op=ALU.max))
                v.tensor_tensor(out=r3(d0), in0=r3(lg), in1=bfree(m[:, :], 8),
                                op=ALU.subtract)
        with nc.Block() as blk:
            @blk.scalar
            def _(s):
                s.activation(out=er[:, :], in_=d0[:, :], func=AF.Exp)
        with nc.Block() as blk:
            @blk.vector
            def _(v):
                def step(ins):
                    s_ch.inc(ins)
                    v.wait_ge(s_ch.h, s_ch.cnt)
                step(v.tensor_reduce(out=ssm[:, :], in_=r3(er), axis=AX.X, op=ALU.add))
                step(v.reciprocal(rs[:, :], ssm[:, :]))
                step(v.tensor_tensor(out=r3(P), in0=r3(er), in1=bfree(rs[:, :], 8),
                                     op=ALU.mult))
                step(v.tensor_reduce(out=m[:, :], in_=r3(P), axis=AX.X, op=ALU.max))
                step(v.tensor_tensor(out=r3(msk1), in0=r3(P), in1=bfree(m[:, :], 8),
                                     op=ALU.is_equal))
                step(v.tensor_mul(t1[:, :], P[:, :], msk1[:, :]))
                step(v.tensor_sub(Pm[:, :], P[:, :], t1[:, :]))
                step(v.tensor_reduce(out=m2[:, :], in_=r3(Pm), axis=AX.X, op=ALU.max))
                step(v.tensor_tensor(out=r3(mskK), in0=r3(P), in1=bfree(m2[:, :], 8),
                                     op=ALU.is_ge))
                step(v.tensor_mul(Ptop[:, :], P[:, :], mskK[:, :]))
                step(v.tensor_reduce(out=den[:, :], in_=r3(Ptop), axis=AX.X, op=ALU.add))
                if eps:
                    step(v.tensor_scalar(out=den[:, :], in0=den[:, :], scalar1=float(eps),
                                         scalar2=None, op0=ALU.add))
                step(v.reciprocal(rden[:, :], den[:, :]))
                step(v.tensor_tensor(out=r3(gat), in0=r3(Ptop), in1=bfree(rden[:, :], 8),
                                     op=ALU.mult))

        gtt = [es.enter_context(nc.sbuf_tensor(f"{tag}_gtt{i}", [8, 128], BF16))
               for i in range(2)]
        with nc.Block() as blk:
            @blk.tensor
            def _(t):
                for tt in range(NTT):
                    if tt >= 4:
                        s_te.wait(t, tt - 3)
                    ins = t.transpose(pst[:, (tt % 4) * 512:(tt % 4) * 512 + 128],
                                      gat[:, tt * 8:(tt + 1) * 8], C["ident_f"][:, :])
                    s_tr.inc(ins)

            @blk.scalar
            def _(s):
                for tt in range(NTT):
                    s_tr.wait(s, tt + 1)
                    if tt >= 2:
                        s_gw.wait(s, 16 * (tt - 1))
                    s_te.inc(s.copy(gtt[tt % 2][:, :],
                                    pst[0:8, (tt % 4) * 512:(tt % 4) * 512 + 128]))

            @blk.sync
            def _(sp):
                for tt in range(NTT):
                    s_te.wait(sp, tt + 1)
                    sp.dma_start(out=gatesT_dram[0:8, tt * 128:(tt + 1) * 128],
                                 in_=gtt[tt % 2][:, :]).then_inc(s_gw.h, 16)
                s_gw.wait(sp, 16 * NTT)
            s_gw.cnt += 16 * NTT
        if dbg is not None:
            with nc.Block() as blk:
                @blk.sync
                def _(sp):
                    for i, (name, buf) in enumerate([("lg", lg), ("er", er), ("P", P),
                                                     ("mskK", mskK), ("den", den), ("gat", gat)]):
                        if name in dbg:
                            sp.dma_start(out=dbg[name], in_=buf[:, :]).then_inc(s_gw.h, 16)
                            s_gw.cnt += 16
                    sp.wait_ge(s_gw.h, s_gw.cnt)


def emit_load_cast_w(nc, tag, jobs):
    with ExitStack() as es:
        wmax = max(j[0].shape[-1] for j in jobs)
        stg = [es.enter_context(nc.sbuf_tensor(f"{tag}_wstg{i}", [128, wmax], F32))
               for i in range(2)]
        s_ld = Sem(nc, es, f"{tag}_wld")
        s_cv = Sem(nc, es, f"{tag}_wcv")
        with nc.Block() as blk:
            @blk.sync
            def _(sp):
                for i, (s_ap, d_ap) in enumerate(jobs):
                    W = s_ap.shape[-1]
                    if i >= 2:
                        s_cv.wait(sp, i - 1)
                    sp.dma_start(out=stg[i % 2][:, 0:W], in_=s_ap).then_inc(s_ld.h, 16)
            s_ld.cnt += 16 * len(jobs)

            @blk.gpsimd
            def _(g):
                for i, (s_ap, d_ap) in enumerate(jobs):
                    W = s_ap.shape[-1]
                    s_ld.wait(g, 16 * (i + 1))
                    s_cv.inc(g.tensor_copy(out=d_ap, in_=stg[i % 2][:, 0:W]))


def emit_l0_attn(nc, tag, x_in, xnT_dram, W, x_out, C, aux=None):
    with ExitStack() as es:
        def sb(name, shape, dt=F32):
            return es.enter_context(nc.sbuf_tensor(f"{tag}_{name}", shape, dt))
        wq = sb("wq", [128, NCH * 1024], BF16)
        wk = sb("wk", [128, NCH * 1024], BF16)
        wv = sb("wv", [128, NCH * 1024], BF16)
        jobs = []
        for name, dst in (("wq", wq), ("wk", wk), ("wv", wv)):
            for c in range(NCH):
                jobs.append((W[name][c * 128:(c + 1) * 128, :],
                             dst[:, c * 1024:(c + 1) * 1024]))
        emit_load_cast_w(nc, tag, jobs)

        qT = sb("qT", [128, TOK], BF16)
        kT = sb("kT", [128, TOK], BF16)
        v_aug = sb("vaug", [128, NTT * 130], BF16)
        PT = [sb(f"PT{i}", [128, 1024], BF16) for i in range(2)]
        oT = sb("oT", [128, TOK], BF16)
        oB_sb = [sb(f"oB{i}", [64, 1024], BF16) for i in range(2)]
        l2 = sb("l2", [128, TOK])
        recl_bc = sb("reclbc", [128, TOK])
        xg = [sb(f"xg{i}", [128, NCH * 512], BF16) for i in range(2)]
        x_sb = sb("x0", [128, TOK])
        tmpf = sb("tmpf", [128, TOK])
        psA = es.enter_context(nc.psum_tensor(f"{tag}_psA", [128, 2048], F32))
        psB = es.enter_context(nc.psum_tensor(f"{tag}_psB", [128, 2048], F32))
        pp = psA
        st_ps = psA
        av_ps = psB
        pb = psA
        s_xg = Sem(nc, es, f"{tag}_xg")
        s_pm = Sem(nc, es, f"{tag}_pm")
        s_pv = Sem(nc, es, f"{tag}_pv")
        s_ms = Sem(nc, es, f"{tag}_ms")
        s_sm = Sem(nc, es, f"{tag}_sm")
        s_ex = Sem(nc, es, f"{tag}_ex")
        s_am = Sem(nc, es, f"{tag}_am")
        s_ov = Sem(nc, es, f"{tag}_ov")
        s_ob = Sem(nc, es, f"{tag}_ob")
        s_rc = Sem(nc, es, f"{tag}_rc")
        s_bm = Sem(nc, es, f"{tag}_bm")
        s_be = Sem(nc, es, f"{tag}_be")
        s_xl = Sem(nc, es, f"{tag}_xl")
        s_xu = Sem(nc, es, f"{tag}_xu")
        s_xs = Sem(nc, es, f"{tag}_xs")

        for g in range(NEXP):
            with nc.Block() as blk:
                @blk.sync
                def _(sp):
                    for cg in range(NCG):
                        if g * 48 + cg * 6 >= 12:
                            s_pm.wait(sp, g * 48 + (cg - 1) * 6)
                        for c in range(NCH):
                            sp.dma_start(out=xg[cg % 2][:, c * 512:(c + 1) * 512],
                                         in_=xnT_dram[c * 128:(c + 1) * 128,
                                                      cg * 512:(cg + 1) * 512]
                                         ).then_inc(s_xg.h, 16)
                s_xg.cnt += 16 * NCH * NCG

                @blk.vector
                def _(v):
                    s_ms.inc(v.memset(v_aug[:, :], 1.0))

                if aux is not None:
                    aux(blk, g)

                @blk.tensor
                def _(t):
                    item = g * 48
                    for cg in range(NCG):
                        s_xg.wait(t, 16 * NCH * (g * NCG + cg + 1))
                        for wname, wsb in (("q", wq), ("k", wk)):
                            if item >= 2:
                                s_pv.wait(t, item - 1)
                            for c in range(NCH):
                                ins = t.matmul(
                                    pp[:, (item % 2) * 512:(item % 2) * 512 + 512],
                                    wsb[:, c * 1024 + g * 128:c * 1024 + (g + 1) * 128],
                                    xg[cg % 2][:, c * 512:(c + 1) * 512],
                                    start=(c == 0), stop=(c == NCH - 1))
                            s_pm.inc(ins)
                            item += 1
                        for i in range(4):
                            tt = cg * 4 + i
                            if item >= 2:
                                s_pv.wait(t, item - 1)
                            for c in range(NCH):
                                ins = t.matmul(
                                    pp[:, (item % 2) * 512:(item % 2) * 512 + 128],
                                    xg[cg % 2][:, c * 512 + i * 128:c * 512 + (i + 1) * 128],
                                    wv[:, c * 1024 + g * 128:c * 1024 + (g + 1) * 128],
                                    start=(c == 0), stop=(c == NCH - 1))
                            s_pm.inc(ins)
                            item += 1

                @blk.scalar
                def _(s):
                    item = g * 48
                    for cg in range(NCG):
                        s_pm.wait(s, item + 1)
                        s_pv.inc(s.copy(qT[:, cg * 512:(cg + 1) * 512],
                                        pp[:, (item % 2) * 512:(item % 2) * 512 + 512]))
                        item += 1
                        s_pm.wait(s, item + 1)
                        s_pv.inc(s.copy(kT[:, cg * 512:(cg + 1) * 512],
                                        pp[:, (item % 2) * 512:(item % 2) * 512 + 512]))
                        item += 1
                        for i in range(4):
                            tt = cg * 4 + i
                            s_pm.wait(s, item + 1)
                            if tt == 0:
                                s_ms.wait(s, g + 1)
                            dst = v_aug[:, tt * 130:(tt + 1) * 130].rearrange(
                                "p (h x) -> p h x", x=65)[:, :, 0:64]
                            s_pv.inc(s.copy(dst,
                                            pp[:, (item % 2) * 512:(item % 2) * 512 + 128]
                                            .rearrange("p (h x) -> p h x", x=64)))
                            item += 1

            for hi in range(2):
                hb = hi * 64
                B = g * 2 + hi
                with nc.Block() as blk:
                    @blk.tensor
                    def _(t):
                        for pair in range(2):
                            for kb in range(NTT):
                                for sub in range(2):
                                    qq = pair * 2 + sub
                                    item = B * 4 + qq
                                    gidx = B * 128 + pair * 64 + kb * 2 + sub
                                    if pair * 64 + kb * 2 + sub >= 2:
                                        s_ex.wait(t, gidx - 1)
                                    ins = t.matmul(
                                        st_ps[:, sub * 1024:sub * 1024 + 512],
                                        kT[hb:hb + 64, kb * 128:(kb + 1) * 128],
                                        qT[hb:hb + 64, qq * 1024:qq * 1024 + 512],
                                        start=True, stop=True)
                                    ins = t.matmul(
                                        st_ps[:, sub * 1024 + 512:sub * 1024 + 1024],
                                        kT[hb:hb + 64, kb * 128:(kb + 1) * 128],
                                        qT[hb:hb + 64, qq * 1024 + 512:qq * 1024 + 1024],
                                        start=True, stop=True)
                                    s_sm.inc(ins)
                                for sub in range(2):
                                    qq = pair * 2 + sub
                                    item = B * 4 + qq
                                    gidx = B * 128 + pair * 64 + kb * 2 + sub
                                    s_ex.wait(t, gidx + 1)
                                    if kb == 0:
                                        s_ov.wait(t, max(0, 2 * (item - 1)))
                                    for half in range(2):
                                        ins = t.matmul(
                                            av_ps[0:65, sub * 1024 + half * 512:
                                                  sub * 1024 + half * 512 + 512],
                                            v_aug[:, kb * 130 + hi * 65:kb * 130 + (hi + 1) * 65],
                                            PT[sub][:, half * 512:half * 512 + 512],
                                            start=(kb == 0), stop=(kb == NTT - 1))
                                    s_am.inc(ins)
                        for cg in range(NCG):
                            gi = B * NCG + cg
                            s_rc.wait(t, B + 1)
                            if gi >= 2:
                                s_be.wait(t, gi - 1)
                            ins = t.matmul(
                                pb[hb:hb + 64, (cg % 2) * 512:(cg % 2) * 512 + 512],
                                C["ones2_f"][64:65, 0:64],
                                l2[64:65, cg * 512:(cg + 1) * 512],
                                start=True, stop=True)
                            s_bm.inc(ins)

                    @blk.scalar
                    def _(s):
                        for pair in range(2):
                            for kb in range(NTT):
                                for sub in range(2):
                                    qq = pair * 2 + sub
                                    gidx = B * 128 + pair * 64 + kb * 2 + sub
                                    s_sm.wait(s, gidx + 1)
                                    if pair * 64 + kb * 2 + sub >= 2:
                                        s_am.wait(s, gidx - 1)
                                    s_ex.inc(s.activation(
                                        out=PT[sub][:, :],
                                        in_=st_ps[:, sub * 1024:sub * 1024 + 1024],
                                        func=AF.Exp, scale=0.125))
                            for sub in range(2):
                                qq = pair * 2 + sub
                                item = B * 4 + qq
                                s_am.wait(s, B * 128 + pair * 64 + 31 * 2 + sub + 1)
                                if hi == 0:
                                    s.copy(oT[0:64, qq * 1024:(qq + 1) * 1024],
                                           av_ps[0:64, sub * 1024:sub * 1024 + 1024])
                                else:
                                    gq = g * 4 + qq
                                    if gq >= 2:
                                        s_ob.wait(s, 16 * (gq - 1))
                                    s.copy(oB_sb[qq % 2][:, :],
                                           av_ps[0:64, sub * 1024:sub * 1024 + 1024])
                                s_ov.inc(s.copy(l2[64:65, qq * 1024:(qq + 1) * 1024],
                                                av_ps[64:65, sub * 1024:sub * 1024 + 1024]),
                                         2)
                        for cg in range(NCG):
                            gi = B * NCG + cg
                            s_bm.wait(s, gi + 1)
                            s_be.inc(s.copy(recl_bc[hb:hb + 64, cg * 512:(cg + 1) * 512],
                                            pb[hb:hb + 64, (cg % 2) * 512:(cg % 2) * 512 + 512]))

                    @blk.vector
                    def _(v):
                        s_ov.wait(v, 2 * (B * 4 + 4))
                        s_rc.inc(v.reciprocal(l2[64:65, :], l2[64:65, :]))

                    if hi == 1:
                        @blk.sync
                        def _(sp):
                            for qq in range(4):
                                item = B * 4 + qq
                                s_ov.wait(sp, 2 * (item + 1))
                                sp.dma_start(out=oT[64:128, qq * 1024:(qq + 1) * 1024],
                                             in_=oB_sb[qq % 2][:, :]).then_inc(s_ob.h, 16)
                        s_ob.cnt += 16 * 4

            with nc.Block() as blk:
                @blk.sync
                def _(sp):
                    if g >= 1:
                        s_xs.wait(sp, 16 * NCH * g)
                    for c in range(NCH):
                        sp.dma_start(out=x_sb[:, c * 512:(c + 1) * 512],
                                     in_=x_in[g * 128:(g + 1) * 128,
                                              c * 512:(c + 1) * 512]).then_inc(s_xl.h, 16)
                    s_xu.wait(sp, 2 * (g + 1))
                    for c in range(NCH):
                        sp.dma_start(out=x_out[g * 128:(g + 1) * 128,
                                               c * 512:(c + 1) * 512],
                                     in_=x_sb[:, c * 512:(c + 1) * 512]).then_inc(s_xs.h, 16)
                    if g == NEXP - 1:
                        s_xs.wait(sp, 16 * NCH * NEXP)
                s_xl.cnt += 16 * NCH
                s_xs.cnt += 16 * NCH

                @blk.vector
                def _(v):
                    s_xl.wait(v, 16 * NCH * (g + 1))
                    s_ob.wait(v, 16 * 4 * (g + 1))
                    v.tensor_mul(tmpf[:, :], oT[:, :], recl_bc[:, :])
                    s_xu.inc(v.tensor_add(x_sb[:, :], x_sb[:, :], tmpf[:, :]), 2)


def emit_moe_ffn(nc, tag, kind, xnT_dram, x_in, x_out, C, W, aux=None, deps=()):
    F = W["F"]
    NF = F // 128
    swi = (kind == "swiglu")
    with ExitStack() as es:
        def sb(name, shape, dt=F32):
            return es.enter_context(nc.sbuf_tensor(f"{tag}_{name}", shape, dt))
        gatesT_dram = W["gatesT_dram"]

        wup = sb("wup", [128, NCH * F], BF16)
        wup2 = sb("wup2", [128, NCH * F], BF16) if swi else None
        wdn = sb("wdn", [128, NF * D], BF16)
        hg = sb("hg", [128, NF * 512], BF16)
        htmp = sb("htmp", [128, 512], BF16) if swi else None
        xn_sb = [sb(f"xn{i}", [128, NCH * 512], BF16) for i in range(2)]
        x_sb = sb("x0", [128, NCH * 512])
        gate_sb = [sb(f"gate{i}", [128, 512], BF16) for i in range(2)]
        gate_rhs = [sb(f"gr{i}", [8, 512], BF16) for i in range(2)]
        b1T = sb("b1T", [128, 8 * NF]) if not swi else None
        b2bf = sb("b2bf", [8, D], BF16) if not swi else None
        ph = es.enter_context(nc.psum_tensor(f"{tag}_ph", [128, 1024], F32))
        phu = es.enter_context(nc.psum_tensor(f"{tag}_phu", [128, 1024], F32)) if swi else None
        py = es.enter_context(nc.psum_tensor(f"{tag}_py", [128, 1024], F32))
        pg = es.enter_context(nc.psum_tensor(f"{tag}_pg", [128, 1024], F32))
        s_w = Sem(nc, es, f"{tag}_w")
        s_xn = Sem(nc, es, f"{tag}_xn")
        s_xl = Sem(nc, es, f"{tag}_xl")
        s_xs = Sem(nc, es, f"{tag}_xs")
        s_gm = Sem(nc, es, f"{tag}_gm")
        s_gr = Sem(nc, es, f"{tag}_gr")
        s_ge = Sem(nc, es, f"{tag}_ge")
        s_hm = Sem(nc, es, f"{tag}_hm")
        s_hu = Sem(nc, es, f"{tag}_hu")
        s_hg = Sem(nc, es, f"{tag}_hg")
        s_hs = Sem(nc, es, f"{tag}_hs")
        s_ym = Sem(nc, es, f"{tag}_ym")
        s_ya = Sem(nc, es, f"{tag}_ya")

        if not swi:
            with nc.Block() as blk:
                @blk.sync
                def _(sp):
                    sp.dma_start(out=b1T[:, :], in_=W["b1T"]).then_inc(s_w.h, 16)
                    sp.dma_start(out=x_sb[0:8, 0:D], in_=W["b2"]).then_inc(s_w.h, 16)
                    s_w.wait(sp, 32)
                s_w.cnt += 32
            with nc.Block() as blk:
                @blk.vector
                def _(v):
                    v.tensor_copy(out=b2bf[:, :], in_=x_sb[0:8, 0:D])

        nup = 2 if swi else 1
        nwj = 16 * (nup * NCH + NF)
        nw_base = s_w.cnt
        with nc.Block() as blk:
            if aux is not None:
                aux(blk)

            @blk.sync
            def _(sp):
                for sem, val in deps:
                    sem.wait(sp, val)
                for e in range(NEXP):
                    if e >= 1:
                        s_hm.wait(sp, e * NCG * NF)
                    for c in range(NCH):
                        sp.dma_start(out=wup[:, c * F:(c + 1) * F],
                                     in_=W["w1bf" if not swi else "wgbf"][e, c * 128:(c + 1) * 128, :]
                                     ).then_inc(s_w.h, 16)
                        if swi:
                            sp.dma_start(out=wup2[:, c * F:(c + 1) * F],
                                         in_=W["wubf"][e, c * 128:(c + 1) * 128, :]
                                         ).then_inc(s_w.h, 16)
                    if e >= 1:
                        s_ym.wait(sp, e * NCG * NCH)
                    for f in range(NF):
                        sp.dma_start(out=wdn[:, f * D:(f + 1) * D],
                                     in_=W["w2bf" if not swi else "wdbf"][e, f * 128:(f + 1) * 128, :]
                                     ).then_inc(s_w.h, 16)

                    def store(tb):
                        s_ya.wait(sp, (e * NCG + tb + 1) * NCH)
                        for c in range(NCH):
                            sp.dma_start(out=x_out[c * 128:(c + 1) * 128,
                                                   tb * 512:(tb + 1) * 512],
                                         in_=x_sb[:, c * 512:(c + 1) * 512]
                                         ).then_inc(s_xs.h, 16)
                    for tb in range(NCG):
                        if e * NCG + tb >= 2:
                            s_ym.wait(sp, (e * NCG + tb - 1) * NCH)
                        sp.dma_start(out=gate_rhs[tb % 2][:, :],
                                     in_=gatesT_dram[0:8, tb * 512:(tb + 1) * 512]
                                     ).then_inc(s_gr.h, 16)
                        if e * NCG + tb >= 2:
                            s_hm.wait(sp, (e * NCG + tb - 1) * NF)
                        for c in range(NCH):
                            sp.dma_start(out=xn_sb[tb % 2][:, c * 512:(c + 1) * 512],
                                         in_=xnT_dram[c * 128:(c + 1) * 128,
                                                      tb * 512:(tb + 1) * 512]).then_inc(s_xn.h, 16)
                        if tb >= 1:
                            store(tb - 1)
                        if e * NCG + tb >= 1:
                            s_xs.wait(sp, 16 * NCH * (e * NCG + tb))
                        if e >= 1:
                            s_xs.wait(sp, 16 * NCH * ((e - 1) * NCG + tb + 1))
                        xsrc = x_in if e == 0 else x_out
                        for c in range(NCH):
                            sp.dma_start(out=x_sb[:, c * 512:(c + 1) * 512],
                                         in_=xsrc[c * 128:(c + 1) * 128,
                                                  tb * 512:(tb + 1) * 512]).then_inc(s_xl.h, 16)
                    store(NCG - 1)
                    s_w.cnt += nwj
                    s_xn.cnt += 16 * NCH * NCG
                    s_xl.cnt += 16 * NCH * NCG
                    s_xs.cnt += 16 * NCH * NCG
                    s_gr.cnt += 16 * NCG
                sp.wait_ge(s_xs.h, s_xs.cnt)

            @blk.tensor
            def _(t):
                for e in range(NEXP):
                    nw0 = nw_base + e * nwj
                    t.wait_ge(s_w.h, nw0 + 16 * nup * NCH)
                    for tb in range(NCG):
                        if e * NCG + tb >= 2:
                            s_ge.wait(t, e * NCG + tb - 1)
                        s_gr.wait(t, 16 * (e * NCG + tb + 1))
                        ins = t.matmul(pg[:, (tb % 2) * 512:(tb % 2) * 512 + 512],
                                       C["E8"][:, e * 128:(e + 1) * 128],
                                       gate_rhs[tb % 2][:, :],
                                       start=True, stop=True)
                        s_gm.inc(ins)
                        s_xn.wait(t, 16 * NCH * (e * NCG + tb + 1))
                        fbase = (e * NCG + tb) * NF
                        for f in range(NF):
                            if fbase + f >= 2:
                                s_hg.wait(t, fbase + f - 1)
                            for c in range(NCH):
                                ins = t.matmul(ph[:, (f % 2) * 512:(f % 2) * 512 + 512],
                                               wup[:, c * F + f * 128:c * F + (f + 1) * 128],
                                               xn_sb[tb % 2][:, c * 512:(c + 1) * 512],
                                               start=(c == 0), stop=(c == NCH - 1))
                            if swi:
                                if fbase + f >= 2:
                                    s_hu.wait(t, fbase + f - 1)
                                for c in range(NCH):
                                    ins = t.matmul(phu[:, (f % 2) * 512:(f % 2) * 512 + 512],
                                                   wup2[:, c * F + f * 128:c * F + (f + 1) * 128],
                                                   xn_sb[tb % 2][:, c * 512:(c + 1) * 512],
                                                   start=(c == 0), stop=(c == NCH - 1))
                            s_hm.inc(ins)
                        if tb == 0:
                            t.wait_ge(s_w.h, nw0 + nwj)
                        s_hs.wait(t, fbase + NF)
                        for j in range(NCH):
                            s_ya.wait(t, max(0, (e * NCG + tb) * NCH + j - 1))
                            for f in range(NF):
                                ins = t.matmul(py[:, (j % 2) * 512:(j % 2) * 512 + 512],
                                               wdn[:, f * D + j * 128:f * D + (j + 1) * 128],
                                               hg[:, f * 512:(f + 1) * 512],
                                               start=(f == 0), stop=(f == NF - 1 and not
                                                                     (not swi and e == 0)))
                            if not swi and e == 0:
                                ins = t.matmul(py[:, (j % 2) * 512:(j % 2) * 512 + 512],
                                               b2bf[0:8, j * 128:(j + 1) * 128],
                                               gate_rhs[tb % 2][:, :],
                                               start=False, stop=True)
                            s_ym.inc(ins)

            @blk.scalar
            def _(s):
                for e in range(NEXP):
                    for tb in range(NCG):
                        s_gm.wait(s, e * NCG + tb + 1)
                        s_ge.inc(s.copy(gate_sb[tb % 2][:, :],
                                        pg[:, (tb % 2) * 512:(tb % 2) * 512 + 512]))
                        fbase = (e * NCG + tb) * NF
                        for f in range(NF):
                            s_hm.wait(s, fbase + f + 1)
                            if f == 0 and e * NCG + tb >= 1:
                                s_ym.wait(s, (e * NCG + tb) * NCH)
                            if swi:
                                ins = s.activation(out=hg[:, f * 512:(f + 1) * 512],
                                                   in_=ph[:, (f % 2) * 512:(f % 2) * 512 + 512],
                                                   func=AF.Silu)
                            else:
                                ins = s.activation(out=hg[:, f * 512:(f + 1) * 512],
                                                   in_=ph[:, (f % 2) * 512:(f % 2) * 512 + 512],
                                                   func=AF.Gelu,
                                                   bias=b1T[:, e * NF + f:e * NF + f + 1])
                            s_hg.inc(ins)

            @blk.vector
            def _(v):
                for e in range(NEXP):
                    for tb in range(NCG):
                        fbase = (e * NCG + tb) * NF
                        s_ge.wait(v, e * NCG + tb + 1)
                        for f in range(NF):
                            s_hg.wait(v, fbase + f + 1)
                            if e * NCG + tb >= 1:
                                s_ym.wait(v, (e * NCG + tb) * NCH)
                            if swi:
                                s_hu.inc(v.tensor_mul(htmp[:, :],
                                                      hg[:, f * 512:(f + 1) * 512],
                                                      phu[:, (f % 2) * 512:(f % 2) * 512 + 512]))
                                s_hs.inc(v.tensor_mul(hg[:, f * 512:(f + 1) * 512],
                                                      htmp[:, :], gate_sb[tb % 2][:, :]))
                            else:
                                ins = v.tensor_mul(hg[:, f * 512:(f + 1) * 512],
                                                   hg[:, f * 512:(f + 1) * 512],
                                                   gate_sb[tb % 2][:, :])
                                s_hs.inc(ins)
                        s_xl.wait(v, 16 * NCH * (e * NCG + tb + 1))
                        for j in range(NCH):
                            s_ym.wait(v, (e * NCG + tb) * NCH + j + 1)
                            ins = v.scalar_tensor_tensor(
                                out=x_sb[:, j * 512:(j + 1) * 512],
                                in0=py[:, (j % 2) * 512:(j % 2) * 512 + 512],
                                scalar=0.0, in1=x_sb[:, j * 512:(j + 1) * 512],
                                op0=ALU.add, op1=ALU.add)
                            s_ya.inc(ins)


def build_ffn_launch(kind, debug=False):
    nc = bass.Bass("TRN2", target_bir_lowering=False)
    F = 4096 if kind == "gelu" else 2816
    x_in = nc.dram_tensor("x_in", [D, TOK], F32, kind="ExternalInput").ap()
    wr = nc.dram_tensor("wr", [D, 8], F32, kind="ExternalInput").ap()
    x_out = nc.dram_tensor("x_out", [D, TOK], F32, kind="ExternalOutput").ap()
    W = {"F": F, "wr": wr}
    if kind == "gelu":
        w1 = nc.dram_tensor("w1", [8, D, F], F32, kind="ExternalInput").ap()
        w2 = nc.dram_tensor("w2", [8, F, D], F32, kind="ExternalInput").ap()
        W["b1T"] = nc.dram_tensor("b1T", [128, 8 * (F // 128)], F32, kind="ExternalInput").ap()
        W["b2"] = nc.dram_tensor("b2", [8, D], F32, kind="ExternalInput").ap()
        w1bf = nc.dram_tensor("w1bf", [8, D, F], BF16).ap()
        w2bf = nc.dram_tensor("w2bf", [8, F, D], BF16).ap()
        W["w1bf"], W["w2bf"] = w1bf, w2bf
        cast_pairs = [(w1.rearrange("e d f -> (e d) f"), w1bf.rearrange("e d f -> (e d) f")),
                      (w2.rearrange("e f d -> (e f) d"), w2bf.rearrange("e f d -> (e f) d"))]
    else:
        wg = nc.dram_tensor("wg", [8, D, F], F32, kind="ExternalInput").ap()
        wu = nc.dram_tensor("wu", [8, D, F], F32, kind="ExternalInput").ap()
        wd = nc.dram_tensor("wd", [8, F, D], F32, kind="ExternalInput").ap()
        wgbf = nc.dram_tensor("wgbf", [8, D, F], BF16).ap()
        wubf = nc.dram_tensor("wubf", [8, D, F], BF16).ap()
        wdbf = nc.dram_tensor("wdbf", [8, F, D], BF16).ap()
        W["wgbf"], W["wubf"], W["wdbf"] = wgbf, wubf, wdbf
        cast_pairs = [(wg.rearrange("e d f -> (e d) f"), wgbf.rearrange("e d f -> (e d) f")),
                      (wu.rearrange("e d f -> (e d) f"), wubf.rearrange("e d f -> (e d) f")),
                      (wd.rearrange("e f d -> (e f) d"), wdbf.rearrange("e f d -> (e f) d"))]
    xnT = nc.dram_tensor("xnT", [D, TOK], BF16).ap()
    W["gatesT_dram"] = nc.dram_tensor("gatesT", [8, TOK], BF16).ap()
    if debug:
        xn_dbg = nc.dram_tensor("xn_dbg", [D, TOK], BF16, kind="ExternalOutput").ap()
        g_dbg = nc.dram_tensor("g_dbg", [8, TOK], BF16, kind="ExternalOutput").ap()
        W["rdbg"] = {name: nc.dram_tensor(f"r_{name}", [128, NTT * 8] if name != "den"
                                          else [128, NTT], F32, kind="ExternalOutput").ap()
                     for name in ("lg", "er", "P", "mskK", "den", "gat")}
    with ExitStack() as es:
        C = make_consts(nc, es)
        emit_precast(nc, "pc", cast_pairs)
        lg = es.enter_context(nc.sbuf_tensor("lgr", [128, NTT * 8], F32))
        emit_ln(nc, "ln", x_in, xnT, C, router=(wr, lg))
        emit_router(nc, "rt", lg, W["gatesT_dram"], C,
                    eps=(1e-9 if kind == "gelu" else 0.0), dbg=W.get("rdbg"))
        emit_moe_ffn(nc, "ffn", kind, xnT, x_in, x_out, C, W)
        if debug:
            with nc.Block() as blk:
                @blk.sync
                def _(sp):
                    s_dbg = nc.semaphore("s_dbg").__enter__()
                    sp.dma_start(out=xn_dbg, in_=xnT).then_inc(s_dbg, 16)
                    sp.dma_start(out=g_dbg, in_=W["gatesT_dram"]).then_inc(s_dbg, 16)
                    sp.wait_ge(s_dbg, 32)
    return nc


def host_ffn_inputs(kind, x_bct, inp):
    maps = []
    for c in range(NCORES):
        m = {"x_in": np.ascontiguousarray(x_bct[c])}
        if kind == "gelu":
            F = 4096
            m["wr"] = inp["L0_ffn_Wg"]
            m["w1"] = inp["L0_ffn_W1"]
            m["w2"] = inp["L0_ffn_W2"]
            m["b1T"] = np.ascontiguousarray(
                inp["L0_ffn_b1"].reshape(8, F // 128, 128).transpose(2, 0, 1).reshape(128, -1))
            m["b2"] = inp["L0_ffn_b2"]
        else:
            m["wr"] = inp["L1_ffn_Wr"]
            m["wg"] = inp["L1_ffn_Wg"]
            m["wu"] = inp["L1_ffn_Wu"]
            m["wd"] = inp["L1_ffn_Wd"]
        maps.append(m)
    return maps


def build_l0_attn_launch(debug=False):
    nc = bass.Bass("TRN2", target_bir_lowering=False)
    x_in = nc.dram_tensor("x_in", [D, TOK], F32, kind="ExternalInput").ap()
    wq = nc.dram_tensor("wq", [D, 1024], F32, kind="ExternalInput").ap()
    wk = nc.dram_tensor("wk", [D, 1024], F32, kind="ExternalInput").ap()
    wv = nc.dram_tensor("wv", [D, 1024], F32, kind="ExternalInput").ap()
    x_out = nc.dram_tensor("x_out", [D, TOK], F32, kind="ExternalOutput").ap()
    xnT = nc.dram_tensor("xnT", [D, TOK], BF16).ap()
    with ExitStack() as es:
        C = make_consts(nc, es)
        emit_ln(nc, "ln", x_in, xnT, C)
        emit_l0_attn(nc, "at", x_in, xnT, {"wq": wq, "wk": wk, "wv": wv}, x_out, C)
    return nc


def host_l0_attn_inputs(x_bct, inp):
    W = inp["L0_attn_W"].reshape(1024, 16, 192)
    wq = np.ascontiguousarray(W[:, :, 0:64].reshape(1024, 1024))
    wk = np.ascontiguousarray(W[:, :, 64:128].reshape(1024, 1024))
    wv = np.ascontiguousarray(W[:, :, 128:192].reshape(1024, 1024))
    return [{"x_in": x_bct[c], "wq": wq, "wk": wk, "wv": wv} for c in range(NCORES)]


def emit_l1_attn(nc, tag, x_in, xnT_dram, W, x_out, C, aux=None):
    with ExitStack() as es:
        def sb(name, shape, dt=F32):
            return es.enter_context(nc.sbuf_tensor(f"{tag}_{name}", shape, dt))
        oT_sb = sb("oT", [128, NEXP * TOK], BF16)
        bqT = sb("bqT", [128, 8]); bkT = sb("bkT", [128, 8]); boT = sb("boT", [128, 8])
        bvf = sb("bvf", [1, D]); bvb = sb("bvb", [1, D], BF16)
        s_bl = Sem(nc, es, f"{tag}_bl")
        with nc.Block() as blk:
            @blk.sync
            def _(sp):
                sp.dma_start(out=bqT[:, :], in_=W["bqT"]).then_inc(s_bl.h, 16)
                sp.dma_start(out=bkT[:, :], in_=W["bkT"]).then_inc(s_bl.h, 16)
                sp.dma_start(out=boT[:, :], in_=W["boT"]).then_inc(s_bl.h, 16)
                sp.dma_start(out=bvf[:, :], in_=W["bv"]).then_inc(s_bl.h, 16)
                s_bl.wait(sp, 64)
        with nc.Block() as blk:
            @blk.vector
            def _(v):
                v.tensor_copy(out=bvb[:, :], in_=bvf[:, :])

        with ExitStack() as es2:
            def sb2(name, shape, dt=F32):
                return es2.enter_context(nc.sbuf_tensor(f"{tag}_{name}", shape, dt))
            wq = sb2("wq", [128, NCH * 1024], BF16)
            wk = sb2("wk", [128, NCH * 1024], BF16)
            wv = sb2("wv", [128, NCH * 1024], BF16)
            jobs = []
            for name, dst in (("wqT", wq), ("wkT", wk), ("wvT", wv)):
                for c in range(NCH):
                    jobs.append((W[name][c * 128:(c + 1) * 128, :],
                                 dst[:, c * 1024:(c + 1) * 1024]))
            emit_load_cast_w(nc, tag, jobs)

            qT = sb2("qT", [128, TOK], BF16)
            kT = sb2("kT", [128, TOK], BF16)
            v_g = sb2("v_g", [128, TOK], BF16)
            ktok = sb2("ktok", [128, TOK], BF16)
            xg = [sb2(f"xg{i}", [128, NCH * 512], BF16) for i in range(2)]
            qpre = [sb2(f"qpre{i}", [128, 512]) for i in range(2)]
            t0 = sb2("t0", [128, 512]); t1 = sb2("t1", [128, 512])
            eE = sb2("eE", [128, 512])
            ATm = [sb2(f"ATm{i}", [128, 128], BF16) for i in range(2)]
            S_bf = sb2("S_bf", [128, 64], BF16)
            psP = es2.enter_context(nc.psum_tensor(f"{tag}_psP", [128, 1024], F32))
            psAT = es2.enter_context(nc.psum_tensor(f"{tag}_psAT", [128, 1024], F32))
            psO = es2.enter_context(nc.psum_tensor(f"{tag}_psO", [128, 1024], F32))
            psS = es2.enter_context(nc.psum_tensor(f"{tag}_psS", [128, 1024], F32))
            s_xg = Sem(nc, es2, f"{tag}_xg")
            s_pm = Sem(nc, es2, f"{tag}_pm")
            s_pe1 = Sem(nc, es2, f"{tag}_pe1")
            s_mn = Sem(nc, es2, f"{tag}_mn")
            s_eE = Sem(nc, es2, f"{tag}_eE")
            s_fm = Sem(nc, es2, f"{tag}_fm")
            s_kt = Sem(nc, es2, f"{tag}_kt")
            s_at = Sem(nc, es2, f"{tag}_at")
            s_tm = Sem(nc, es2, f"{tag}_tm")
            s_om = Sem(nc, es2, f"{tag}_om")
            s_sm2 = Sem(nc, es2, f"{tag}_sm2")
            s_se = Sem(nc, es2, f"{tag}_se")
            s_oe = Sem(nc, es2, f"{tag}_oe")

            for g in range(NEXP):
                with nc.Block() as blk:
                    if aux is not None:
                        aux(blk, g)

                    @blk.sync
                    def _(sp):
                        for cg in range(NCG):
                            if g * 48 + cg * 6 >= 12:
                                s_pm.wait(sp, g * 48 + (cg - 1) * 6)
                            for c in range(NCH):
                                sp.dma_start(out=xg[cg % 2][:, c * 512:(c + 1) * 512],
                                             in_=xnT_dram[c * 128:(c + 1) * 128,
                                                          cg * 512:(cg + 1) * 512]
                                             ).then_inc(s_xg.h, 16)
                    s_xg.cnt += 16 * NCH * NCG

                    @blk.tensor
                    def _(t):
                        item = g * 48
                        for cg in range(NCG):
                            s_xg.wait(t, 16 * NCH * (g * NCG + cg + 1))
                            for wsb in (wq, wk):
                                if item >= 2:
                                    s_pe1.wait(t, item - 1)
                                for c in range(NCH):
                                    ins = t.matmul(
                                        psP[:, (item % 2) * 512:(item % 2) * 512 + 512],
                                        wsb[:, c * 1024 + g * 128:c * 1024 + (g + 1) * 128],
                                        xg[cg % 2][:, c * 512:(c + 1) * 512],
                                        start=(c == 0), stop=(c == NCH - 1))
                                s_pm.inc(ins)
                                item += 1
                            for i in range(4):
                                tt = cg * 4 + i
                                if item >= 2:
                                    s_pe1.wait(t, item - 1)
                                for c in range(NCH):
                                    t.matmul(
                                        psP[:, (item % 2) * 512:(item % 2) * 512 + 128],
                                        xg[cg % 2][:, c * 512 + i * 128:c * 512 + (i + 1) * 128],
                                        wv[:, c * 1024 + g * 128:c * 1024 + (g + 1) * 128],
                                        start=(c == 0), stop=False)
                                ins = t.matmul(
                                    psP[:, (item % 2) * 512:(item % 2) * 512 + 128],
                                    C["ones_row"][0:1, :],
                                    bvb[0:1, g * 128:(g + 1) * 128],
                                    start=False, stop=True)
                                s_pm.inc(ins)
                                item += 1

                    @blk.scalar
                    def _(s):
                        item = g * 48
                        for cg in range(NCG):
                            for which in range(2):
                                dstT = qT if which == 0 else kT
                                bias = bqT if which == 0 else bkT
                                s_pm.wait(s, item + 1)
                                s_pe1.inc(s.activation(
                                    out=qpre[item % 2][:, :],
                                    in_=psP[:, (item % 2) * 512:(item % 2) * 512 + 512],
                                    func=AF.Identity, bias=bias[:, g:g + 1]))
                                qk = g * 16 + cg * 2 + which
                                s_mn.wait(s, qk + 1)
                                if qk >= 1:
                                    s_fm.wait(s, qk)
                                s_eE.inc(s.activation(out=eE[:, :], in_=t0[:, :],
                                                      func=AF.Exp))
                                item += 1
                            for i in range(4):
                                s_pm.wait(s, item + 1)
                                tt = cg * 4 + i
                                s_pe1.inc(s.copy(v_g[:, tt * 128:(tt + 1) * 128],
                                                 psP[:, (item % 2) * 512:(item % 2) * 512 + 128]))
                                item += 1

                    @blk.vector
                    def _(v):
                        qk = g * 16
                        for cg in range(NCG):
                            for which in range(2):
                                item = g * 48 + cg * 6 + which
                                s_pe1.wait(v, item + 1)
                                if qk >= 1:
                                    s_eE.wait(v, qk)
                                v.tensor_scalar(out=t0[:, :], in0=qpre[item % 2][:, :],
                                                scalar1=0.0, scalar2=None, op0=ALU.min)
                                s_mn.inc(v.tensor_scalar(
                                    out=t1[:, :], in0=qpre[item % 2][:, :],
                                    scalar1=0.0, scalar2=None, op0=ALU.max))
                                dstT = qT if which == 0 else kT
                                s_eE.wait(v, qk + 1)
                                s_fm.inc(v.tensor_add(
                                    dstT[:, cg * 512:(cg + 1) * 512], eE[:, :], t1[:, :]))
                                qk += 1

                with nc.Block() as blk:
                    @blk.sync
                    def _(sp):
                        for tt in range(NTT):
                            sp.dma_start(out=ktok[:, tt * 128:(tt + 1) * 128],
                                         in_=kT[:, tt * 128:(tt + 1) * 128],
                                         transpose=True).then_inc(s_kt.h, 16)
                        sp.wait_ge(s_kt.h, s_kt.cnt + 16 * NTT)
                    s_kt.cnt += 16 * NTT

                with nc.Block() as blk:
                    @blk.tensor
                    def _(t):
                        for c in range(NTT):
                            for hi in range(2):
                                hb = hi * 64
                                i = (g * NTT + c) * 2 + hi
                                ib = (g * NTT * 2)
                                if i - ib >= 2:
                                    s_tm.wait(t, i - 1)
                                ins = t.matmul(psAT[:, (i % 2) * 512:(i % 2) * 512 + 128],
                                               kT[hb:hb + 64, c * 128:(c + 1) * 128],
                                               qT[hb:hb + 64, c * 128:(c + 1) * 128],
                                               start=True, stop=True)
                                s_at.inc(ins)
                                if i - ib >= 2:
                                    s_oe.wait(t, i - 1)
                                s_tm.wait(t, i + 1)
                                if c >= 1:
                                    s_se.wait(t, i - 1)
                                    t.matmul(psO[hb:hb + 64, hi * 512:hi * 512 + 128],
                                             S_bf[hb:hb + 64, :],
                                             qT[hb:hb + 64, c * 128:(c + 1) * 128],
                                             start=True, stop=False)
                                ins = t.matmul(psO[hb:hb + 64, hi * 512:hi * 512 + 128],
                                               v_g[:, c * 128 + hb:c * 128 + hb + 64],
                                               ATm[i % 2][:, :],
                                               start=(c == 0), stop=True)
                                s_om.inc(ins)
                                ins = t.matmul(psS[hb:hb + 64, hi * 512:hi * 512 + 64],
                                               ktok[:, c * 128 + hb:c * 128 + hb + 64],
                                               v_g[:, c * 128 + hb:c * 128 + hb + 64],
                                               start=(c == 0), stop=(c == NTT - 1))
                                s_sm2.inc(ins)

                    @blk.vector
                    def _(v):
                        for c in range(NTT):
                            for hi in range(2):
                                i = (g * NTT + c) * 2 + hi
                                ib = (g * NTT * 2)
                                s_at.wait(v, i + 1)
                                if i - ib >= 2:
                                    s_om.wait(v, i - 1)
                                s_tm.inc(v.tensor_tensor(out=ATm[i % 2][:, :],
                                                         in0=psAT[:, (i % 2) * 512:(i % 2) * 512 + 128],
                                                         in1=C["triu"][:, :], op=ALU.mult))

                    @blk.scalar
                    def _(s):
                        for c in range(NTT):
                            for hi in range(2):
                                hb = hi * 64
                                i = (g * NTT + c) * 2 + hi
                                s_om.wait(s, i + 1)
                                s_oe.inc(s.copy(
                                    oT_sb[hb:hb + 64, g * TOK + c * 128:g * TOK + (c + 1) * 128],
                                    psO[hb:hb + 64, hi * 512:hi * 512 + 128]))
                                s_sm2.wait(s, i + 1)
                                s_se.inc(s.copy(S_bf[hb:hb + 64, :],
                                                psS[hb:hb + 64, hi * 512:hi * 512 + 64]))

        with ExitStack() as es3:
            def sb3(name, shape, dt=F32):
                return es3.enter_context(nc.sbuf_tensor(f"{tag}_{name}", shape, dt))
            wo = sb3("wo", [128, NCH * 1024], BF16)
            jobs = [(W["woT"][c * 128:(c + 1) * 128, :], wo[:, c * 1024:(c + 1) * 1024])
                    for c in range(NCH)]
            emit_load_cast_w(nc, f"{tag}o", jobs)
            x_sb = sb3("x0", [128, TOK])
            psW = es3.enter_context(nc.psum_tensor(f"{tag}_psW", [128, 1024], F32))
            s_wm = Sem(nc, es3, f"{tag}_wm")
            s_wa = Sem(nc, es3, f"{tag}_wa")
            s_xl = Sem(nc, es3, f"{tag}_xl")
            s_xs = Sem(nc, es3, f"{tag}_xs")
            with nc.Block() as blk:
                @blk.sync
                def _(sp):
                    for dout in range(NCH):
                        if dout >= 1:
                            s_xs.wait(sp, 16 * NCH * dout)
                        for c in range(NCH):
                            sp.dma_start(out=x_sb[:, c * 512:(c + 1) * 512],
                                         in_=x_in[dout * 128:(dout + 1) * 128,
                                                  c * 512:(c + 1) * 512]).then_inc(s_xl.h, 16)
                        s_wa.wait(sp, NCG * (dout + 1))
                        for c in range(NCH):
                            sp.dma_start(out=x_out[dout * 128:(dout + 1) * 128,
                                                   c * 512:(c + 1) * 512],
                                         in_=x_sb[:, c * 512:(c + 1) * 512]).then_inc(s_xs.h, 16)
                    s_xs.wait(sp, 16 * NCH * NCH)
                s_xl.cnt += 16 * NCH * NCH
                s_xs.cnt += 16 * NCH * NCH

                @blk.tensor
                def _(t):
                    for dout in range(NCH):
                        for cg in range(NCG):
                            gi = dout * NCG + cg
                            if gi >= 2:
                                s_wa.wait(t, gi - 1)
                            for c in range(NCH):
                                ins = t.matmul(psW[:, (gi % 2) * 512:(gi % 2) * 512 + 512],
                                               wo[:, c * 1024 + dout * 128:c * 1024 + (dout + 1) * 128],
                                               oT_sb[:, c * TOK + cg * 512:c * TOK + (cg + 1) * 512],
                                               start=(c == 0), stop=(c == NCH - 1))
                            s_wm.inc(ins)

                @blk.vector
                def _(v):
                    for dout in range(NCH):
                        s_xl.wait(v, 16 * NCH * (dout + 1))
                        for cg in range(NCG):
                            gi = dout * NCG + cg
                            s_wm.wait(v, gi + 1)
                            ins = v.scalar_tensor_tensor(
                                out=x_sb[:, cg * 512:(cg + 1) * 512],
                                in0=psW[:, (gi % 2) * 512:(gi % 2) * 512 + 512],
                                scalar=boT[:, dout:dout + 1],
                                in1=x_sb[:, cg * 512:(cg + 1) * 512],
                                op0=ALU.add, op1=ALU.add)
                            s_wa.inc(ins)


def build_l1_attn_launch(debug=False):
    nc = bass.Bass("TRN2", target_bir_lowering=False)
    x_in = nc.dram_tensor("x_in", [D, TOK], F32, kind="ExternalInput").ap()
    names = ["wqT", "wkT", "wvT", "woT"]
    W = {n: nc.dram_tensor(n, [D, D], F32, kind="ExternalInput").ap() for n in names}
    for n in ("bqT", "bkT", "boT"):
        W[n] = nc.dram_tensor(n, [128, 8], F32, kind="ExternalInput").ap()
    W["bv"] = nc.dram_tensor("bv", [1, D], F32, kind="ExternalInput").ap()
    x_out = nc.dram_tensor("x_out", [D, TOK], F32, kind="ExternalOutput").ap()
    xnT = nc.dram_tensor("xnT", [D, TOK], BF16).ap()
    with ExitStack() as es:
        C = make_consts(nc, es)
        emit_ln(nc, "ln", x_in, xnT, C)
        emit_l1_attn(nc, "la", x_in, xnT, W, x_out, C)
    return nc


def host_l1_attn_inputs(x_bct, inp):
    m = {}
    m["wqT"] = np.ascontiguousarray(inp["L1_attn_Wq"].T)
    m["wkT"] = np.ascontiguousarray(inp["L1_attn_Wk"].T)
    m["wvT"] = np.ascontiguousarray(inp["L1_attn_Wv"].T)
    m["woT"] = np.ascontiguousarray(inp["L1_attn_Wo"].T)
    m["bqT"] = np.ascontiguousarray(inp["L1_attn_bq"].reshape(8, 128).T)
    m["bkT"] = np.ascontiguousarray(inp["L1_attn_bk"].reshape(8, 128).T)
    m["boT"] = np.ascontiguousarray(inp["L1_attn_bo"].reshape(8, 128).T)
    m["bv"] = inp["L1_attn_bv"].reshape(1, 1024)
    return [dict(m, x_in=x_bct[c]) for c in range(NCORES)]


_BUILD_CACHE = {}


def _get(name, fn):
    if name not in _BUILD_CACHE:
        _BUILD_CACHE[name] = fn()
    return _BUILD_CACHE[name]


def build_fused():
    nc = bass.Bass("TRN2", target_bir_lowering=False)
    x0 = nc.dram_tensor("x0", [D, TOK], F32, kind="ExternalInput").ap()
    out = nc.dram_tensor("out", [D, TOK], F32, kind="ExternalOutput").ap()
    wq = nc.dram_tensor("wq", [D, 1024], F32, kind="ExternalInput").ap()
    wk = nc.dram_tensor("wk", [D, 1024], F32, kind="ExternalInput").ap()
    wv = nc.dram_tensor("wv", [D, 1024], F32, kind="ExternalInput").ap()
    f0_wr = nc.dram_tensor("f0_wr", [D, 8], F32, kind="ExternalInput").ap()
    f0_w1 = nc.dram_tensor("f0_w1", [8, D, 4096], F32, kind="ExternalInput").ap()
    f0_w2 = nc.dram_tensor("f0_w2", [8, 4096, D], F32, kind="ExternalInput").ap()
    f0_b1T = nc.dram_tensor("f0_b1T", [128, 8 * 32], F32, kind="ExternalInput").ap()
    f0_b2 = nc.dram_tensor("f0_b2", [8, D], F32, kind="ExternalInput").ap()
    WA = {}
    for n in ("wqT", "wkT", "wvT", "woT"):
        WA[n] = nc.dram_tensor(n, [D, D], F32, kind="ExternalInput").ap()
    for n in ("bqT", "bkT", "boT"):
        WA[n] = nc.dram_tensor(n, [128, 8], F32, kind="ExternalInput").ap()
    WA["bv"] = nc.dram_tensor("bv", [1, D], F32, kind="ExternalInput").ap()
    f1_wr = nc.dram_tensor("f1_wr", [D, 8], F32, kind="ExternalInput").ap()
    f1_wg = nc.dram_tensor("f1_wg", [8, D, 2816], F32, kind="ExternalInput").ap()
    f1_wu = nc.dram_tensor("f1_wu", [8, D, 2816], F32, kind="ExternalInput").ap()
    f1_wd = nc.dram_tensor("f1_wd", [8, 2816, D], F32, kind="ExternalInput").ap()
    x1 = nc.dram_tensor("x1", [D, TOK], F32).ap()
    x2 = nc.dram_tensor("x2", [D, TOK], F32).ap()
    x3 = nc.dram_tensor("x3", [D, TOK], F32).ap()
    xnT = nc.dram_tensor("xnT", [D, TOK], BF16).ap()
    gatesT = nc.dram_tensor("gatesT", [8, TOK], BF16).ap()
    w1bf = nc.dram_tensor("w1bf", [8, D, 4096], BF16).ap()
    w2bf = nc.dram_tensor("w2bf", [8, 4096, D], BF16).ap()
    wgbf = nc.dram_tensor("wgbf", [8, D, 2816], BF16).ap()
    wubf = nc.dram_tensor("wubf", [8, D, 2816], BF16).ap()
    wdbf = nc.dram_tensor("wdbf", [8, 2816, D], BF16).ap()

    W0 = {"F": 4096, "wr": f0_wr, "w1bf": w1bf, "w2bf": w2bf,
          "b1T": f0_b1T, "b2": f0_b2, "gatesT_dram": gatesT}
    W1 = {"F": 2816, "wr": f1_wr, "wgbf": wgbf, "wubf": wubf, "wdbf": wdbf,
          "gatesT_dram": gatesT}

    with ExitStack() as es:
        C = make_consts(nc, es)
        lg = es.enter_context(nc.sbuf_tensor("lgr", [128, NTT * 8], F32))
        emit_ln(nc, "ln0", x0, xnT, C)
        with ExitStack() as esA:
            pc0 = PoolPrecast(nc, esA, es, "pc0", [
                (f0_w1.rearrange("e d f -> (e d) f"), w1bf.rearrange("e d f -> (e d) f")),
                (f0_w2.rearrange("e f d -> (e f) d"), w2bf.rearrange("e f d -> (e f) d"))],
                maxw=2048)
            npg = (len(pc0.jobs) + NEXP - 1) // NEXP
            emit_l0_attn(nc, "at0", x0, xnT, {"wq": wq, "wk": wk, "wv": wv}, x1, C,
                         aux=lambda blk, g: pc0.emit(blk, npg))
            dep0 = pc0.dep()
        emit_ln(nc, "ln1", x1, xnT, C, router=(f0_wr, lg))
        emit_router(nc, "rt1", lg, gatesT, C, eps=1e-9)
        emit_moe_ffn(nc, "ff1", "gelu", xnT, x1, x2, C, W0, deps=[dep0])
        emit_ln(nc, "ln2", x2, xnT, C)
        with ExitStack() as esB:
            pc1 = PoolPrecast(nc, esB, es, "pc1", [
                (f1_wg.rearrange("e d f -> (e d) f"), wgbf.rearrange("e d f -> (e d) f")),
                (f1_wu.rearrange("e d f -> (e d) f"), wubf.rearrange("e d f -> (e d) f")),
                (f1_wd.rearrange("e f d -> (e f) d"), wdbf.rearrange("e f d -> (e f) d"))],
                maxw=1024)
            npg1 = (len(pc1.jobs) + NEXP - 1) // NEXP
            emit_l1_attn(nc, "la2", x2, xnT, WA, x3, C,
                         aux=lambda blk, g: pc1.emit(blk, npg1))
            dep1 = pc1.dep()
        emit_ln(nc, "ln3", x3, xnT, C, router=(f1_wr, lg))
        emit_router(nc, "rt3", lg, gatesT, C, eps=0.0)
        emit_moe_ffn(nc, "ff3", "swiglu", xnT, x3, out, C, W1, deps=[dep1])
    return nc


def host_fused_inputs(x_bct, inp):
    W = inp["L0_attn_W"].reshape(1024, 16, 192)
    base = {
        "wq": np.ascontiguousarray(W[:, :, 0:64].reshape(1024, 1024)),
        "wk": np.ascontiguousarray(W[:, :, 64:128].reshape(1024, 1024)),
        "wv": np.ascontiguousarray(W[:, :, 128:192].reshape(1024, 1024)),
        "f0_wr": inp["L0_ffn_Wg"],
        "f0_w1": inp["L0_ffn_W1"],
        "f0_w2": inp["L0_ffn_W2"],
        "f0_b1T": np.ascontiguousarray(
            inp["L0_ffn_b1"].reshape(8, 32, 128).transpose(2, 0, 1).reshape(128, 256)),
        "f0_b2": inp["L0_ffn_b2"],
        "wqT": np.ascontiguousarray(inp["L1_attn_Wq"].T),
        "wkT": np.ascontiguousarray(inp["L1_attn_Wk"].T),
        "wvT": np.ascontiguousarray(inp["L1_attn_Wv"].T),
        "woT": np.ascontiguousarray(inp["L1_attn_Wo"].T),
        "bqT": np.ascontiguousarray(inp["L1_attn_bq"].reshape(8, 128).T),
        "bkT": np.ascontiguousarray(inp["L1_attn_bk"].reshape(8, 128).T),
        "boT": np.ascontiguousarray(inp["L1_attn_bo"].reshape(8, 128).T),
        "bv": inp["L1_attn_bv"].reshape(1, 1024),
        "f1_wr": inp["L1_ffn_Wr"],
        "f1_wg": inp["L1_ffn_Wg"],
        "f1_wu": inp["L1_ffn_Wu"],
        "f1_wd": inp["L1_ffn_Wd"],
    }
    return [dict(base, x0=np.ascontiguousarray(x_bct[c])) for c in range(NCORES)]


def kernel_fused(**inputs):
    x = inputs["L0_attn_x"]
    x_bct = np.ascontiguousarray(x.transpose(0, 2, 1))
    nc = _get("fused", build_fused)
    maps = host_fused_inputs(x_bct, inputs)
    res = run_bass_kernel_spmd(nc, maps, core_ids=list(range(NCORES)))
    out = np.stack([np.asarray(res.results[c]["out"]).T for c in range(NCORES)])
    return np.ascontiguousarray(out)


def kernel(**inputs):
    return kernel_fused(**inputs)


def kernel_multilaunch(**inputs):
    x = inputs["L0_attn_x"]
    x_bct = np.ascontiguousarray(x.transpose(0, 2, 1))

    core_ids = list(range(NCORES))

    nc1 = _get("l0a", build_l0_attn_launch)
    res = run_bass_kernel_spmd(nc1, host_l0_attn_inputs(x_bct, inputs), core_ids=core_ids)
    x_bct = [np.asarray(res.results[c]["x_out"]) for c in range(NCORES)]

    nc2 = _get("l0f", lambda: build_ffn_launch("gelu"))
    maps = host_ffn_inputs("gelu", x_bct, inputs)
    res = run_bass_kernel_spmd(nc2, maps, core_ids=core_ids)
    x_bct = [np.asarray(res.results[c]["x_out"]) for c in range(NCORES)]

    nc3 = _get("l1a", build_l1_attn_launch)
    res = run_bass_kernel_spmd(nc3, host_l1_attn_inputs(x_bct, inputs), core_ids=core_ids)
    x_bct = [np.asarray(res.results[c]["x_out"]) for c in range(NCORES)]

    nc4 = _get("l1f", lambda: build_ffn_launch("swiglu"))
    maps = host_ffn_inputs("swiglu", x_bct, inputs)
    res = run_bass_kernel_spmd(nc4, maps, core_ids=core_ids)
    out = np.stack([np.asarray(res.results[c]["x_out"]).T for c in range(NCORES)])
    return np.ascontiguousarray(out)


# version 05
# speedup vs baseline: 1.1301x; 1.0063x over previous
import numpy as np
from contextlib import ExitStack
import concourse.bass as bass
import concourse.mybir as mybir
from concourse.ap import AP
from concourse.bass_utils import run_bass_kernel_spmd

F32 = mybir.dt.float32
BF16 = mybir.dt.bfloat16
AF = mybir.ActivationFunctionType
ALU = mybir.AluOpType
AX = mybir.AxisListType

TOK = 4096
D = 1024
NCH = D // 128
NCG = TOK // 512
NTT = TOK // 128
NCORES = 8
NEXP = 8


def bfree(ap, n):
    return AP(ap.tensor, ap.offset, [list(p) for p in ap.ap] + [[0, n]])


_SEMCNT = {}


class Sem:
    def __init__(self, nc, es, name):
        self.h = es.enter_context(nc.semaphore(name))
        self._key = (id(nc), int(self.h.num))
        self._base = _SEMCNT.get(self._key, 0)
        self._rel = 0

    @property
    def cnt(self):
        return self._base + self._rel

    @cnt.setter
    def cnt(self, v):
        self._rel = v - self._base
        _SEMCNT[self._key] = self._base + self._rel

    def inc(self, instr, k=1):
        instr.then_inc(self.h, k)
        self.cnt = self.cnt + k
        return self.cnt

    def wait(self, eng, v):
        eng.wait_ge(self.h, self._base + v)


def make_consts(nc, es):
    C = {}
    def sb(name, shape, dt):
        h = es.enter_context(nc.sbuf_tensor(name, shape, dt))
        return h
    C["ones_col"] = sb("c_ones_col", [128, 1], BF16)
    C["ones_col_f"] = sb("c_ones_col_f", [128, 1], F32)
    C["ones_row"] = sb("c_ones_row", [1, 128], BF16)
    C["ones_row_f"] = sb("c_ones_row_f", [1, 128], F32)
    C["ones2_f"] = sb("c_ones2_f", [128, 128], F32)
    C["ident"] = sb("c_ident", [128, 128], BF16)
    C["ident_f"] = sb("c_ident_f", [128, 128], F32)
    C["triu"] = sb("c_triu", [128, 128], F32)
    C["E8"] = sb("c_E8", [8, 8 * 128], BF16)
    iota_c = sb("c_iota_c", [128, 128], F32)
    iota_p = sb("c_iota_p", [128, 1], F32)
    iota_p8 = sb("c_iota_p8", [8, 128], F32)

    with nc.Block() as blk:
        @blk.gpsimd
        def _(g):
            g.iota(iota_c[:, :], [[1, 128]], channel_multiplier=0,
                   allow_small_or_imprecise_dtypes=True)
            g.iota(iota_p[:, :], [[0, 1]], channel_multiplier=1,
                   allow_small_or_imprecise_dtypes=True)
            g.iota(iota_p8[:, :], [[0, 128]], channel_multiplier=1,
                   allow_small_or_imprecise_dtypes=True)

        @blk.vector
        def _(v):
            v.memset(C["ones_col"][:, :], 1.0)
            v.memset(C["ones_col_f"][:, :], 1.0)
            v.memset(C["ones_row"][:, :], 1.0)
            v.memset(C["ones_row_f"][:, :], 1.0)
            v.memset(C["ones2_f"][:, :], 1.0)

    with nc.Block() as blk:
        @blk.vector
        def _(v):
            v.tensor_scalar(out=C["ident"][:, :], in0=iota_c[:, :],
                            scalar1=iota_p[:, 0:1], scalar2=None, op0=ALU.is_equal)
            v.tensor_scalar(out=C["ident_f"][:, :], in0=iota_c[:, :],
                            scalar1=iota_p[:, 0:1], scalar2=None, op0=ALU.is_equal)
            v.tensor_scalar(out=C["triu"][:, :], in0=iota_c[:, :],
                            scalar1=iota_p[:, 0:1], scalar2=None, op0=ALU.is_ge)
            for e in range(8):
                v.tensor_scalar(out=C["E8"][:, e * 128:(e + 1) * 128],
                                in0=iota_p8[:, :], scalar1=float(e), scalar2=None,
                                op0=ALU.is_equal)
    return C


def emit_ln(nc, tag, x_dram, xnT_dram, C, router=None):
    with ExitStack() as es:
        def sb(name, shape, dt=F32):
            return es.enter_context(nc.sbuf_tensor(f"{tag}_{name}", shape, dt))
        buf = [sb("xbufA", [128, TOK]), sb("xbufB", [128, TOK])]
        xsq = [sb("xsqA", [128, TOK]), sb("xsqB", [128, TOK])]
        ssum = sb("ssum", [1, TOK]); ssq = sb("ssq", [1, TOK])
        rstd = sb("rstd", [1, TOK]); negmr = sb("negmr", [1, TOK])
        epsc = sb("epsc", [1, 1])
        rstd_bc = sb("rstd_bc", [128, TOK]); negmr_bc = sb("negmr_bc", [128, TOK])
        xnb = [sb("xnbA", [128, TOK], BF16), sb("xnbB", [128, TOK], BF16)]
        if router is not None:
            wr_dram, lg = router
            xnf = sb("xnf", [128, TOK])
            wr_sb = sb("wr", [128, NCH * 8])
        es8 = ExitStack()
        ps8 = es8.enter_context(nc.psum_tensor(f"{tag}_ps8", [1, TOK], F32))
        s_ld = Sem(nc, es, f"{tag}_ld")
        s_pr = Sem(nc, es, f"{tag}_pr")
        s_pe = Sem(nc, es, f"{tag}_pe")
        s_bm = Sem(nc, es, f"{tag}_bm")
        s_be = Sem(nc, es, f"{tag}_be")
        s_n = Sem(nc, es, f"{tag}_n")
        s_wr = Sem(nc, es, f"{tag}_wr")
        s_rm = Sem(nc, es, f"{tag}_rm")

        with nc.Block() as blk:
            @blk.vector
            def _(v):
                v.memset(epsc[0:1, 0:1], 1e-5)
            if router is not None:
                @blk.sync
                def _(sp):
                    for c in range(NCH):
                        sp.dma_start(out=wr_sb[:, c * 8:(c + 1) * 8],
                                     in_=wr_dram[c * 128:(c + 1) * 128, :]).then_inc(s_ld.h, 16)
                    s_ld.wait(sp, 16 * NCH)
                s_ld.cnt += 16 * NCH

        nld0 = s_ld._rel
        with nc.Block() as blk:
            @blk.sync
            def _(sp):
                for c in range(NCH):
                    if c >= 2:
                        s_pe.wait(sp, c - 1)
                    sp.dma_start(out=buf[c % 2][:, :],
                                 in_=x_dram[c * 128:(c + 1) * 128, :]).then_inc(s_ld.h, 16)
            s_ld.cnt += 16 * NCH

            @blk.tensor
            def _(t):
                for c in range(NCH):
                    s_ld.wait(t, nld0 + 16 * (c + 1))
                    for g in range(NCG):
                        ins = t.matmul(ps8[0:1, g * 512:(g + 1) * 512],
                                       C["ones_col_f"][:, :],
                                       buf[c % 2][:, g * 512:(g + 1) * 512],
                                       start=(c == 0), stop=(c == NCH - 1))
                    s_pe.inc(ins)

            @blk.scalar
            def _(s):
                s_pe.wait(s, NCH)
                for g in range(NCG):
                    s.copy(ssum[0:1, g * 512:(g + 1) * 512],
                           ps8[0:1, g * 512:(g + 1) * 512])

        with nc.Block() as blk:
            @blk.sync
            def _(sp):
                for c in range(NCH):
                    if c >= 2:
                        s_pr.wait(sp, c - 1)
                    sp.dma_start(out=buf[c % 2][:, :],
                                 in_=x_dram[c * 128:(c + 1) * 128, :]).then_inc(s_ld.h, 16)
            s_ld.cnt += 16 * NCH

            @blk.scalar
            def _(s):
                for c in range(NCH):
                    s_ld.wait(s, nld0 + 16 * (NCH + c + 1))
                    if c >= 2:
                        s_pe.wait(s, NCH + c - 1)
                    s_pr.inc(s.square(xsq[c % 2][:, :], buf[c % 2][:, :]))
                s_pe.wait(s, 2 * NCH)
                for g in range(NCG):
                    s.copy(ssq[0:1, g * 512:(g + 1) * 512],
                           ps8[0:1, g * 512:(g + 1) * 512])

            @blk.tensor
            def _(t):
                for c in range(NCH):
                    s_pr.wait(t, c + 1)
                    for g in range(NCG):
                        ins = t.matmul(ps8[0:1, g * 512:(g + 1) * 512],
                                       C["ones_col_f"][:, :],
                                       xsq[c % 2][:, g * 512:(g + 1) * 512],
                                       start=(c == 0), stop=(c == NCH - 1))
                    s_pe.inc(ins)

        with nc.Block() as blk:
            @blk.scalar
            def _(s):
                s.mul(ssum[0:1, :], ssum[0:1, :], 1.0 / D)
                s.mul(ssq[0:1, :], ssq[0:1, :], 1.0 / D)
                s_rm.inc(s.square(rstd[0:1, :], ssum[0:1, :]))
            @blk.vector
            def _(v):
                v.wait_ge(s_rm.h, s_rm.cnt)
                v.tensor_sub(ssq[0:1, :], ssq[0:1, :], rstd[0:1, :])
        with nc.Block() as blk:
            @blk.scalar
            def _(s):
                s_rm.inc(s.activation(out=ssq[0:1, :], in_=ssq[0:1, :], func=AF.Sqrt,
                                      bias=epsc[0:1, 0:1]))
            @blk.vector
            def _(v):
                v.wait_ge(s_rm.h, s_rm.cnt)
                s_rm.inc(v.reciprocal(rstd[0:1, :], ssq[0:1, :]))
                v.wait_ge(s_rm.h, s_rm.cnt)
                s_rm.inc(v.tensor_mul(negmr[0:1, :], ssum[0:1, :], rstd[0:1, :]))
                v.wait_ge(s_rm.h, s_rm.cnt)
                v.tensor_scalar(out=negmr[0:1, :], in0=negmr[0:1, :],
                                scalar1=-1.0, scalar2=None, op0=ALU.mult)

        es8.close()
        psb = es.enter_context(nc.psum_tensor(f"{tag}_psb", [128, 1024], F32))
        if router is not None:
            psr = es.enter_context(nc.psum_tensor(f"{tag}_psr", [128, NTT * 8], F32))
        with nc.Block() as blk:
            @blk.tensor
            def _(t):
                for i in range(2 * NCG):
                    g, which = i % NCG, i // NCG
                    src = rstd if which == 0 else negmr
                    if i >= 2:
                        s_be.wait(t, i - 1)
                    ins = t.matmul(psb[:, (i % 2) * 512:(i % 2) * 512 + 512],
                                   C["ones_row_f"][0:1, :],
                                   src[0:1, g * 512:(g + 1) * 512],
                                   start=True, stop=True)
                    s_bm.inc(ins)

            @blk.scalar
            def _(s):
                for i in range(2 * NCG):
                    g, which = i % NCG, i // NCG
                    dst = rstd_bc if which == 0 else negmr_bc
                    s_bm.wait(s, i + 1)
                    s_be.inc(s.copy(dst[:, g * 512:(g + 1) * 512],
                                    psb[:, (i % 2) * 512:(i % 2) * 512 + 512]))

        nld1 = s_ld._rel
        kst = 2
        with nc.Block() as blk:
            @blk.sync
            def _(sp):
                def store(c):
                    s_n.wait(sp, kst * (c + 1))
                    sp.dma_start(out=xnT_dram[c * 128:(c + 1) * 128, :],
                                 in_=xnb[c % 2][:, :]).then_inc(s_wr.h, 16)
                for c in range(NCH):
                    if c >= 2:
                        s_n.wait(sp, kst * (c - 1))
                    sp.dma_start(out=buf[c % 2][:, :],
                                 in_=x_dram[c * 128:(c + 1) * 128, :]).then_inc(s_ld.h, 16)
                    if c >= 2:
                        store(c - 2)
                store(NCH - 2)
                store(NCH - 1)
                s_wr.wait(sp, 16 * NCH)
            s_ld.cnt += 16 * NCH
            s_wr.cnt += 16 * NCH

            @blk.vector
            def _(v):
                for c in range(NCH):
                    s_ld.wait(v, nld1 + 16 * (c + 1))
                    if c >= 2:
                        s_wr.wait(v, 16 * (c - 1))
                    if router is not None and c >= 1:
                        s_pe.wait(v, 2 * NCH + c)
                    v.tensor_mul(buf[c % 2][:, :], buf[c % 2][:, :], rstd_bc[:, :])
                    if router is not None:
                        s_rm.inc(v.tensor_add(xnf[:, :], buf[c % 2][:, :],
                                              negmr_bc[:, :]))
                    s_n.inc(v.tensor_add(xnb[c % 2][:, :], buf[c % 2][:, :],
                                         negmr_bc[:, :]), 2)

            if router is not None:
                @blk.tensor
                def _(t):
                    nrm0 = s_rm.cnt - NCH
                    for c in range(NCH):
                        s_rm.wait(t, nrm0 + c + 1)
                        for tt in range(NTT):
                            ins = t.matmul(psr[:, tt * 8:(tt + 1) * 8],
                                           xnf[:, tt * 128:(tt + 1) * 128],
                                           wr_sb[:, c * 8:(c + 1) * 8],
                                           start=(c == 0 and tt == 0),
                                           stop=(c == NCH - 1 and tt == NTT - 1),
                                           skip_group_check=True)
                        s_pe.inc(ins)

                @blk.scalar
                def _(s):
                    s_pe.wait(s, 2 * NCH + NCH)
                    s.copy(lg[:, :], psr[:, :])


def emit_precast(nc, tag, pairs):
    with ExitStack() as es:
        s_ld = Sem(nc, es, f"{tag}_cld")
        s_cv = Sem(nc, es, f"{tag}_ccv")
        s_st = Sem(nc, es, f"{tag}_cst")
        wmax = max(src.shape[-1] for src, dst in pairs)
        stg = [es.enter_context(nc.sbuf_tensor(f"{tag}_cstg{i}", [128, wmax], F32))
               for i in range(2)]
        stb = [es.enter_context(nc.sbuf_tensor(f"{tag}_cstb{i}", [128, wmax], BF16))
               for i in range(2)]
        jobs = []
        for src, dst in pairs:
            R, W = src.shape
            assert R % 128 == 0
            for r in range(R // 128):
                jobs.append((src[r * 128:(r + 1) * 128, :],
                             dst[r * 128:(r + 1) * 128, :], W))
        with nc.Block() as blk:
            @blk.sync
            def _(sp):
                n = len(jobs)
                for i in range(n + 1):
                    if i < n:
                        s_ap, d_ap, W = jobs[i]
                        if i >= 2:
                            s_cv.wait(sp, i - 1)
                        sp.dma_start(out=stg[i % 2][:, 0:W], in_=s_ap).then_inc(s_ld.h, 16)
                    if i >= 1:
                        s_ap, d_ap, W = jobs[i - 1]
                        s_cv.wait(sp, i)
                        sp.dma_start(out=d_ap, in_=stb[(i - 1) % 2][:, 0:W]).then_inc(s_st.h, 16)
                s_st.wait(sp, 16 * len(jobs))
            s_ld.cnt += 16 * len(jobs)
            s_st.cnt += 16 * len(jobs)

            @blk.gpsimd
            def _(g):
                for i, (s_ap, d_ap, W) in enumerate(jobs):
                    s_ld.wait(g, 16 * (i + 1))
                    if i >= 2:
                        s_st.wait(g, 16 * (i - 1))
                    s_cv.inc(g.tensor_copy(out=stb[i % 2][:, 0:W],
                                           in_=stg[i % 2][:, 0:W]))


class PoolPrecast:
    def __init__(self, nc, es_buf, es_sem, tag, pairs, maxw=2048):
        self.nc = nc
        self.jobs = []
        for src, dst in pairs:
            R, Wd = src.shape
            assert R % 128 == 0
            for r in range(R // 128):
                for w0 in range(0, Wd, maxw):
                    w1 = min(Wd, w0 + maxw)
                    self.jobs.append((src[r * 128:(r + 1) * 128, w0:w1],
                                      dst[r * 128:(r + 1) * 128, w0:w1], w1 - w0))
        self.stg = [es_buf.enter_context(nc.sbuf_tensor(f"{tag}_pstg{i}", [128, maxw], F32))
                    for i in range(2)]
        self.stb = [es_buf.enter_context(nc.sbuf_tensor(f"{tag}_pstb{i}", [128, maxw], BF16))
                    for i in range(2)]
        self.s_in = Sem(nc, es_sem, f"{tag}_pin")
        self.s_out = Sem(nc, es_sem, f"{tag}_pout")
        self.pos = 0
        self.issued = 0

    def emit(self, blk, n=None):
        left = len(self.jobs) - self.pos
        n = left if n is None else min(n, left)
        if n <= 0:
            return
        i0, i1 = self.pos, self.pos + n
        self.pos = i1

        @blk.gpsimd
        def _(g):
            for i in range(i0, i1):
                while self.issued < min(i + 2, i1):
                    j = self.issued
                    src, dst, W = self.jobs[j]
                    g.dma_start(out=self.stg[j % 2][:, 0:W], in_=src).then_inc(self.s_in.h, 16)
                    self.s_in.cnt += 16
                    self.issued += 1
                src, dst, W = self.jobs[i]
                self.s_in.wait(g, 16 * (i + 1))
                if i >= 2:
                    self.s_out.wait(g, 16 * (i - 1))
                g.tensor_copy(out=self.stb[i % 2][:, 0:W], in_=self.stg[i % 2][:, 0:W])
                g.dma_start(out=dst, in_=self.stb[i % 2][:, 0:W]).then_inc(self.s_out.h, 16)
                self.s_out.cnt += 16

    def dep(self):
        return (self.s_out, 16 * len(self.jobs))



def emit_router(nc, tag, lg, gatesT_dram, C, eps, dbg=None):
    with ExitStack() as es:
        def sb(name, shape, dt=F32):
            return es.enter_context(nc.sbuf_tensor(f"{tag}_{name}", shape, dt))
        m = sb("m", [128, NTT]); d0 = sb("d0", [128, NTT * 8])
        er = sb("er", [128, NTT * 8]); ssm = sb("ssm", [128, NTT])
        rs = sb("rs", [128, NTT]); P = sb("P", [128, NTT * 8])
        msk1 = sb("msk1", [128, NTT * 8]); t1 = sb("t1", [128, NTT * 8])
        Pm = sb("Pm", [128, NTT * 8]); m2 = sb("m2", [128, NTT])
        mskK = sb("mskK", [128, NTT * 8]); Ptop = sb("Ptop", [128, NTT * 8])
        den = sb("den", [128, NTT]); rden = sb("rden", [128, NTT])
        gat = sb("gat", [128, NTT * 8])
        pst = es.enter_context(nc.psum_tensor(f"{tag}_pst", [8, 2048], F32))
        s_tr = Sem(nc, es, f"{tag}_tr")
        s_te = Sem(nc, es, f"{tag}_te")
        s_ch = Sem(nc, es, f"{tag}_ch")
        s_gw = Sem(nc, es, f"{tag}_gw")

        def r3(apx):
            return apx[:, :].rearrange("p (t e) -> p t e", e=8)

        with nc.Block() as blk:
            @blk.vector
            def _(v):
                def step(ins):
                    s_ch.inc(ins)
                    v.wait_ge(s_ch.h, s_ch.cnt)
                step(v.tensor_reduce(out=m[:, :], in_=r3(lg), axis=AX.X, op=ALU.max))
                v.tensor_tensor(out=r3(d0), in0=r3(lg), in1=bfree(m[:, :], 8),
                                op=ALU.subtract)
        with nc.Block() as blk:
            @blk.scalar
            def _(s):
                s.activation(out=er[:, :], in_=d0[:, :], func=AF.Exp)
        with nc.Block() as blk:
            @blk.vector
            def _(v):
                def step(ins):
                    s_ch.inc(ins)
                    v.wait_ge(s_ch.h, s_ch.cnt)
                step(v.tensor_reduce(out=ssm[:, :], in_=r3(er), axis=AX.X, op=ALU.add))
                step(v.reciprocal(rs[:, :], ssm[:, :]))
                step(v.tensor_tensor(out=r3(P), in0=r3(er), in1=bfree(rs[:, :], 8),
                                     op=ALU.mult))
                step(v.tensor_reduce(out=m[:, :], in_=r3(P), axis=AX.X, op=ALU.max))
                step(v.tensor_tensor(out=r3(msk1), in0=r3(P), in1=bfree(m[:, :], 8),
                                     op=ALU.is_equal))
                step(v.tensor_mul(t1[:, :], P[:, :], msk1[:, :]))
                step(v.tensor_sub(Pm[:, :], P[:, :], t1[:, :]))
                step(v.tensor_reduce(out=m2[:, :], in_=r3(Pm), axis=AX.X, op=ALU.max))
                step(v.tensor_tensor(out=r3(mskK), in0=r3(P), in1=bfree(m2[:, :], 8),
                                     op=ALU.is_ge))
                step(v.tensor_mul(Ptop[:, :], P[:, :], mskK[:, :]))
                step(v.tensor_reduce(out=den[:, :], in_=r3(Ptop), axis=AX.X, op=ALU.add))
                if eps:
                    step(v.tensor_scalar(out=den[:, :], in0=den[:, :], scalar1=float(eps),
                                         scalar2=None, op0=ALU.add))
                step(v.reciprocal(rden[:, :], den[:, :]))
                step(v.tensor_tensor(out=r3(gat), in0=r3(Ptop), in1=bfree(rden[:, :], 8),
                                     op=ALU.mult))

        gtt = [es.enter_context(nc.sbuf_tensor(f"{tag}_gtt{i}", [8, 128], BF16))
               for i in range(2)]
        with nc.Block() as blk:
            @blk.tensor
            def _(t):
                for tt in range(NTT):
                    if tt >= 4:
                        s_te.wait(t, tt - 3)
                    ins = t.transpose(pst[:, (tt % 4) * 512:(tt % 4) * 512 + 128],
                                      gat[:, tt * 8:(tt + 1) * 8], C["ident_f"][:, :])
                    s_tr.inc(ins)

            @blk.scalar
            def _(s):
                for tt in range(NTT):
                    s_tr.wait(s, tt + 1)
                    if tt >= 2:
                        s_gw.wait(s, 16 * (tt - 1))
                    s_te.inc(s.copy(gtt[tt % 2][:, :],
                                    pst[0:8, (tt % 4) * 512:(tt % 4) * 512 + 128]))

            @blk.sync
            def _(sp):
                for tt in range(NTT):
                    s_te.wait(sp, tt + 1)
                    sp.dma_start(out=gatesT_dram[0:8, tt * 128:(tt + 1) * 128],
                                 in_=gtt[tt % 2][:, :]).then_inc(s_gw.h, 16)
                s_gw.wait(sp, 16 * NTT)
            s_gw.cnt += 16 * NTT
        if dbg is not None:
            with nc.Block() as blk:
                @blk.sync
                def _(sp):
                    for i, (name, buf) in enumerate([("lg", lg), ("er", er), ("P", P),
                                                     ("mskK", mskK), ("den", den), ("gat", gat)]):
                        if name in dbg:
                            sp.dma_start(out=dbg[name], in_=buf[:, :]).then_inc(s_gw.h, 16)
                            s_gw.cnt += 16
                    sp.wait_ge(s_gw.h, s_gw.cnt)


def emit_load_cast_w(nc, tag, jobs):
    with ExitStack() as es:
        wmax = max(j[0].shape[-1] for j in jobs)
        stg = [es.enter_context(nc.sbuf_tensor(f"{tag}_wstg{i}", [128, wmax], F32))
               for i in range(2)]
        s_ld = Sem(nc, es, f"{tag}_wld")
        s_cv = Sem(nc, es, f"{tag}_wcv")
        with nc.Block() as blk:
            @blk.sync
            def _(sp):
                for i, (s_ap, d_ap) in enumerate(jobs):
                    W = s_ap.shape[-1]
                    if i >= 2:
                        s_cv.wait(sp, i - 1)
                    sp.dma_start(out=stg[i % 2][:, 0:W], in_=s_ap).then_inc(s_ld.h, 16)
            s_ld.cnt += 16 * len(jobs)

            @blk.gpsimd
            def _(g):
                for i, (s_ap, d_ap) in enumerate(jobs):
                    W = s_ap.shape[-1]
                    s_ld.wait(g, 16 * (i + 1))
                    s_cv.inc(g.tensor_copy(out=d_ap, in_=stg[i % 2][:, 0:W]))


def emit_l0_attn(nc, tag, x_in, xnT_dram, W, x_out, C, aux=None):
    with ExitStack() as es:
        def sb(name, shape, dt=F32):
            return es.enter_context(nc.sbuf_tensor(f"{tag}_{name}", shape, dt))
        wq = sb("wq", [128, NCH * 1024], BF16)
        wk = sb("wk", [128, NCH * 1024], BF16)
        wv = sb("wv", [128, NCH * 1024], BF16)
        jobs = []
        for name, dst in (("wq", wq), ("wk", wk), ("wv", wv)):
            for c in range(NCH):
                jobs.append((W[name][c * 128:(c + 1) * 128, :],
                             dst[:, c * 1024:(c + 1) * 1024]))
        emit_load_cast_w(nc, tag, jobs)

        qT = sb("qT", [128, TOK], BF16)
        kT = sb("kT", [128, TOK], BF16)
        v_aug = sb("vaug", [128, NTT * 130], BF16)
        PT = [sb(f"PT{i}", [128, 512], BF16) for i in range(4)]
        oT = sb("oT", [128, TOK], BF16)
        oB_sb = [sb(f"oB{i}", [64, 1024], BF16) for i in range(2)]
        l2 = sb("l2", [128, TOK])
        recl_bc = sb("reclbc", [128, TOK])
        xg = [sb(f"xg{i}", [128, NCH * 512], BF16) for i in range(2)]
        x_sb = sb("x0", [128, TOK])
        tmpf = sb("tmpf", [128, TOK])
        psA = es.enter_context(nc.psum_tensor(f"{tag}_psA", [128, 2048], F32))
        psB = es.enter_context(nc.psum_tensor(f"{tag}_psB", [128, 2048], F32))
        pp = psA
        st_ps = psA
        av_ps = psB
        pb = psA
        s_xg = Sem(nc, es, f"{tag}_xg")
        s_pm = Sem(nc, es, f"{tag}_pm")
        s_pv = Sem(nc, es, f"{tag}_pv")
        s_ms = Sem(nc, es, f"{tag}_ms")
        s_sm = Sem(nc, es, f"{tag}_sm")
        s_ex = Sem(nc, es, f"{tag}_ex")
        s_am = Sem(nc, es, f"{tag}_am")
        s_ov = Sem(nc, es, f"{tag}_ov")
        s_ob = Sem(nc, es, f"{tag}_ob")
        s_rc = Sem(nc, es, f"{tag}_rc")
        s_bm = Sem(nc, es, f"{tag}_bm")
        s_be = Sem(nc, es, f"{tag}_be")
        s_xl = Sem(nc, es, f"{tag}_xl")
        s_xu = Sem(nc, es, f"{tag}_xu")
        s_xs = Sem(nc, es, f"{tag}_xs")

        for g in range(NEXP):
            with nc.Block() as blk:
                @blk.sync
                def _(sp):
                    for cg in range(NCG):
                        if g * 48 + cg * 6 >= 12:
                            s_pm.wait(sp, g * 48 + (cg - 1) * 6)
                        for c in range(NCH):
                            sp.dma_start(out=xg[cg % 2][:, c * 512:(c + 1) * 512],
                                         in_=xnT_dram[c * 128:(c + 1) * 128,
                                                      cg * 512:(cg + 1) * 512]
                                         ).then_inc(s_xg.h, 16)
                s_xg.cnt += 16 * NCH * NCG

                @blk.vector
                def _(v):
                    s_ms.inc(v.memset(v_aug[:, :], 1.0))

                if aux is not None:
                    aux(blk, g)

                @blk.tensor
                def _(t):
                    item = g * 48
                    for cg in range(NCG):
                        s_xg.wait(t, 16 * NCH * (g * NCG + cg + 1))
                        for wname, wsb in (("q", wq), ("k", wk)):
                            if item >= 2:
                                s_pv.wait(t, item - 1)
                            for c in range(NCH):
                                ins = t.matmul(
                                    pp[:, (item % 2) * 512:(item % 2) * 512 + 512],
                                    wsb[:, c * 1024 + g * 128:c * 1024 + (g + 1) * 128],
                                    xg[cg % 2][:, c * 512:(c + 1) * 512],
                                    start=(c == 0), stop=(c == NCH - 1))
                            s_pm.inc(ins)
                            item += 1
                        for i in range(4):
                            tt = cg * 4 + i
                            if item >= 2:
                                s_pv.wait(t, item - 1)
                            for c in range(NCH):
                                ins = t.matmul(
                                    pp[:, (item % 2) * 512:(item % 2) * 512 + 128],
                                    xg[cg % 2][:, c * 512 + i * 128:c * 512 + (i + 1) * 128],
                                    wv[:, c * 1024 + g * 128:c * 1024 + (g + 1) * 128],
                                    start=(c == 0), stop=(c == NCH - 1))
                            s_pm.inc(ins)
                            item += 1

                @blk.scalar
                def _(s):
                    item = g * 48
                    for cg in range(NCG):
                        s_pm.wait(s, item + 1)
                        s_pv.inc(s.copy(qT[:, cg * 512:(cg + 1) * 512],
                                        pp[:, (item % 2) * 512:(item % 2) * 512 + 512]))
                        item += 1
                        s_pm.wait(s, item + 1)
                        s_pv.inc(s.copy(kT[:, cg * 512:(cg + 1) * 512],
                                        pp[:, (item % 2) * 512:(item % 2) * 512 + 512]))
                        item += 1
                        for i in range(4):
                            tt = cg * 4 + i
                            s_pm.wait(s, item + 1)
                            if tt == 0:
                                s_ms.wait(s, g + 1)
                            dst = v_aug[:, tt * 130:(tt + 1) * 130].rearrange(
                                "p (h x) -> p h x", x=65)[:, :, 0:64]
                            s_pv.inc(s.copy(dst,
                                            pp[:, (item % 2) * 512:(item % 2) * 512 + 128]
                                            .rearrange("p (h x) -> p h x", x=64)))
                            item += 1

            for hi in range(2):
                hb = hi * 64
                B = g * 2 + hi
                with nc.Block() as blk:
                    @blk.tensor
                    def _(t):
                        for pair in range(2):
                            for kb in range(NTT):
                                for sub in range(4):
                                    loc = pair * 128 + kb * 4 + sub
                                    gidx = B * 256 + loc
                                    if loc >= 4:
                                        s_ex.wait(t, gidx - 3)
                                    ins = t.matmul(
                                        st_ps[:, sub * 512:sub * 512 + 512],
                                        kT[hb:hb + 64, kb * 128:(kb + 1) * 128],
                                        qT[hb:hb + 64, pair * 2048 + sub * 512:
                                           pair * 2048 + sub * 512 + 512],
                                        start=True, stop=True)
                                    s_sm.inc(ins)
                                for sub in range(4):
                                    loc = pair * 128 + kb * 4 + sub
                                    gidx = B * 256 + loc
                                    s_ex.wait(t, gidx + 1)
                                    if kb == 0:
                                        item = B * 4 + pair * 2 + sub // 2
                                        s_ov.wait(t, max(0, 2 * (item - 1)))
                                    ins = t.matmul(
                                        av_ps[0:65, sub * 512:sub * 512 + 512],
                                        v_aug[:, kb * 130 + hi * 65:kb * 130 + (hi + 1) * 65],
                                        PT[sub][:, :],
                                        start=(kb == 0), stop=(kb == NTT - 1))
                                    s_am.inc(ins)
                        for cg in range(NCG):
                            gi = B * NCG + cg
                            if cg == 0:
                                s_ov.wait(t, 2 * (B * 4 + 4))
                            if gi >= 2:
                                s_be.wait(t, gi - 1)
                            ins = t.matmul(
                                pb[hb:hb + 64, (cg % 2) * 512:(cg % 2) * 512 + 512],
                                C["ones2_f"][64:65, 0:64],
                                l2[64:65, cg * 512:(cg + 1) * 512],
                                start=True, stop=True)
                            s_bm.inc(ins)

                    @blk.scalar
                    def _(s):
                        for pair in range(2):
                            for kb in range(NTT):
                                for sub in range(4):
                                    loc = pair * 128 + kb * 4 + sub
                                    gidx = B * 256 + loc
                                    s_sm.wait(s, gidx + 1)
                                    if gidx >= 4:
                                        s_am.wait(s, gidx - 3)
                                    s_ex.inc(s.activation(
                                        out=PT[sub][:, :],
                                        in_=st_ps[:, sub * 512:sub * 512 + 512],
                                        func=AF.Exp, scale=0.125))
                            for subq in range(2):
                                qq = pair * 2 + subq
                                item = B * 4 + qq
                                s_am.wait(s, B * 256 + pair * 128 +
                                          (NTT - 1) * 4 + 2 * subq + 2)
                                if hi == 0:
                                    s.copy(oT[0:64, qq * 1024:(qq + 1) * 1024],
                                           av_ps[0:64, subq * 1024:subq * 1024 + 1024])
                                else:
                                    gq = g * 4 + qq
                                    if gq >= 2:
                                        s_ob.wait(s, 16 * (gq - 1))
                                    s.copy(oB_sb[qq % 2][:, :],
                                           av_ps[0:64, subq * 1024:subq * 1024 + 1024])
                                s_ov.inc(s.copy(l2[64:65, qq * 1024:(qq + 1) * 1024],
                                                av_ps[64:65, subq * 1024:subq * 1024 + 1024]),
                                         2)
                        for cg in range(NCG):
                            gi = B * NCG + cg
                            s_bm.wait(s, gi + 1)
                            s_be.inc(s.copy(recl_bc[hb:hb + 64, cg * 512:(cg + 1) * 512],
                                            pb[hb:hb + 64, (cg % 2) * 512:(cg % 2) * 512 + 512]))

                    @blk.vector
                    def _(v):
                        s_be.wait(v, (B + 1) * NCG)
                        s_rc.inc(v.reciprocal(recl_bc[hb:hb + 64, :],
                                              recl_bc[hb:hb + 64, :]))

                    if hi == 1:
                        @blk.sync
                        def _(sp):
                            for qq in range(4):
                                item = B * 4 + qq
                                s_ov.wait(sp, 2 * (item + 1))
                                sp.dma_start(out=oT[64:128, qq * 1024:(qq + 1) * 1024],
                                             in_=oB_sb[qq % 2][:, :]).then_inc(s_ob.h, 16)
                        s_ob.cnt += 16 * 4

            with nc.Block() as blk:
                @blk.sync
                def _(sp):
                    if g >= 1:
                        s_xs.wait(sp, 16 * NCH * g)
                    for c in range(NCH):
                        sp.dma_start(out=x_sb[:, c * 512:(c + 1) * 512],
                                     in_=x_in[g * 128:(g + 1) * 128,
                                              c * 512:(c + 1) * 512]).then_inc(s_xl.h, 16)
                    s_xu.wait(sp, 2 * (g + 1))
                    for c in range(NCH):
                        sp.dma_start(out=x_out[g * 128:(g + 1) * 128,
                                               c * 512:(c + 1) * 512],
                                     in_=x_sb[:, c * 512:(c + 1) * 512]).then_inc(s_xs.h, 16)
                    if g == NEXP - 1:
                        s_xs.wait(sp, 16 * NCH * NEXP)
                s_xl.cnt += 16 * NCH
                s_xs.cnt += 16 * NCH

                @blk.vector
                def _(v):
                    s_xl.wait(v, 16 * NCH * (g + 1))
                    s_ob.wait(v, 16 * 4 * (g + 1))
                    s_rc.wait(v, 2 * (g + 1))
                    v.tensor_mul(tmpf[:, :], oT[:, :], recl_bc[:, :])
                    s_xu.inc(v.tensor_add(x_sb[:, :], x_sb[:, :], tmpf[:, :]), 2)


def emit_moe_ffn(nc, tag, kind, xnT_dram, x_in, x_out, C, W, aux=None, deps=()):
    F = W["F"]
    NF = F // 128
    swi = (kind == "swiglu")
    with ExitStack() as es:
        def sb(name, shape, dt=F32):
            return es.enter_context(nc.sbuf_tensor(f"{tag}_{name}", shape, dt))
        gatesT_dram = W["gatesT_dram"]

        wup = sb("wup", [128, NCH * F], BF16)
        wup2 = sb("wup2", [128, NCH * F], BF16) if swi else None
        wdn = sb("wdn", [128, NF * D], BF16)
        hg = sb("hg", [128, NF * 512], BF16)
        htmp = sb("htmp", [128, 512], BF16) if swi else None
        xn_sb = [sb(f"xn{i}", [128, NCH * 512], BF16) for i in range(2)]
        x_sb = sb("x0", [128, NCH * 512])
        gate_sb = [sb(f"gate{i}", [128, 512], BF16) for i in range(2)]
        gate_rhs = [sb(f"gr{i}", [8, 512], BF16) for i in range(2)]
        b1T = sb("b1T", [128, 8 * NF]) if not swi else None
        b2bf = sb("b2bf", [8, D], BF16) if not swi else None
        ph = es.enter_context(nc.psum_tensor(f"{tag}_ph", [128, 1024], F32))
        phu = es.enter_context(nc.psum_tensor(f"{tag}_phu", [128, 1024], F32)) if swi else None
        py = es.enter_context(nc.psum_tensor(f"{tag}_py", [128, 1024], F32))
        pg = es.enter_context(nc.psum_tensor(f"{tag}_pg", [128, 1024], F32))
        s_w = Sem(nc, es, f"{tag}_w")
        s_xn = Sem(nc, es, f"{tag}_xn")
        s_xl = Sem(nc, es, f"{tag}_xl")
        s_xs = Sem(nc, es, f"{tag}_xs")
        s_gm = Sem(nc, es, f"{tag}_gm")
        s_gr = Sem(nc, es, f"{tag}_gr")
        s_ge = Sem(nc, es, f"{tag}_ge")
        s_hm = Sem(nc, es, f"{tag}_hm")
        s_hu = Sem(nc, es, f"{tag}_hu")
        s_hg = Sem(nc, es, f"{tag}_hg")
        s_hs = Sem(nc, es, f"{tag}_hs")
        s_ym = Sem(nc, es, f"{tag}_ym")
        s_ya = Sem(nc, es, f"{tag}_ya")

        if not swi:
            with nc.Block() as blk:
                @blk.sync
                def _(sp):
                    sp.dma_start(out=b1T[:, :], in_=W["b1T"]).then_inc(s_w.h, 16)
                    sp.dma_start(out=x_sb[0:8, 0:D], in_=W["b2"]).then_inc(s_w.h, 16)
                    s_w.wait(sp, 32)
                s_w.cnt += 32
            with nc.Block() as blk:
                @blk.vector
                def _(v):
                    v.tensor_copy(out=b2bf[:, :], in_=x_sb[0:8, 0:D])

        nup = 2 if swi else 1
        nwj = 16 * (nup * NCH + NF)
        nw_base = s_w.cnt
        with nc.Block() as blk:
            if aux is not None:
                aux(blk)

            @blk.sync
            def _(sp):
                for sem, val in deps:
                    sem.wait(sp, val)
                for e in range(NEXP):
                    if e >= 1:
                        s_hm.wait(sp, e * NCG * NF)
                    for c in range(NCH):
                        sp.dma_start(out=wup[:, c * F:(c + 1) * F],
                                     in_=W["w1bf" if not swi else "wgbf"][e, c * 128:(c + 1) * 128, :]
                                     ).then_inc(s_w.h, 16)
                        if swi:
                            sp.dma_start(out=wup2[:, c * F:(c + 1) * F],
                                         in_=W["wubf"][e, c * 128:(c + 1) * 128, :]
                                         ).then_inc(s_w.h, 16)
                    if e >= 1:
                        s_ym.wait(sp, e * NCG * NCH)
                    for f in range(NF):
                        sp.dma_start(out=wdn[:, f * D:(f + 1) * D],
                                     in_=W["w2bf" if not swi else "wdbf"][e, f * 128:(f + 1) * 128, :]
                                     ).then_inc(s_w.h, 16)

                    def store(tb):
                        s_ya.wait(sp, (e * NCG + tb + 1) * NCH)
                        for c in range(NCH):
                            sp.dma_start(out=x_out[c * 128:(c + 1) * 128,
                                                   tb * 512:(tb + 1) * 512],
                                         in_=x_sb[:, c * 512:(c + 1) * 512]
                                         ).then_inc(s_xs.h, 16)
                    for tb in range(NCG):
                        if e * NCG + tb >= 2:
                            s_ym.wait(sp, (e * NCG + tb - 1) * NCH)
                        sp.dma_start(out=gate_rhs[tb % 2][:, :],
                                     in_=gatesT_dram[0:8, tb * 512:(tb + 1) * 512]
                                     ).then_inc(s_gr.h, 16)
                        if e * NCG + tb >= 2:
                            s_hm.wait(sp, (e * NCG + tb - 1) * NF)
                        for c in range(NCH):
                            sp.dma_start(out=xn_sb[tb % 2][:, c * 512:(c + 1) * 512],
                                         in_=xnT_dram[c * 128:(c + 1) * 128,
                                                      tb * 512:(tb + 1) * 512]).then_inc(s_xn.h, 16)
                        if tb >= 1:
                            store(tb - 1)
                        if e * NCG + tb >= 1:
                            s_xs.wait(sp, 16 * NCH * (e * NCG + tb))
                        if e >= 1:
                            s_xs.wait(sp, 16 * NCH * ((e - 1) * NCG + tb + 1))
                        xsrc = x_in if e == 0 else x_out
                        for c in range(NCH):
                            sp.dma_start(out=x_sb[:, c * 512:(c + 1) * 512],
                                         in_=xsrc[c * 128:(c + 1) * 128,
                                                  tb * 512:(tb + 1) * 512]).then_inc(s_xl.h, 16)
                    store(NCG - 1)
                    s_w.cnt += nwj
                    s_xn.cnt += 16 * NCH * NCG
                    s_xl.cnt += 16 * NCH * NCG
                    s_xs.cnt += 16 * NCH * NCG
                    s_gr.cnt += 16 * NCG
                sp.wait_ge(s_xs.h, s_xs.cnt)

            @blk.tensor
            def _(t):
                for e in range(NEXP):
                    nw0 = nw_base + e * nwj
                    t.wait_ge(s_w.h, nw0 + 16 * nup * NCH)
                    for tb in range(NCG):
                        if e * NCG + tb >= 2:
                            s_ge.wait(t, e * NCG + tb - 1)
                        s_gr.wait(t, 16 * (e * NCG + tb + 1))
                        ins = t.matmul(pg[:, (tb % 2) * 512:(tb % 2) * 512 + 512],
                                       C["E8"][:, e * 128:(e + 1) * 128],
                                       gate_rhs[tb % 2][:, :],
                                       start=True, stop=True)
                        s_gm.inc(ins)
                        s_xn.wait(t, 16 * NCH * (e * NCG + tb + 1))
                        fbase = (e * NCG + tb) * NF
                        for f in range(NF):
                            if fbase + f >= 2:
                                s_hg.wait(t, fbase + f - 1)
                            for c in range(NCH):
                                ins = t.matmul(ph[:, (f % 2) * 512:(f % 2) * 512 + 512],
                                               wup[:, c * F + f * 128:c * F + (f + 1) * 128],
                                               xn_sb[tb % 2][:, c * 512:(c + 1) * 512],
                                               start=(c == 0), stop=(c == NCH - 1))
                            if swi:
                                if fbase + f >= 2:
                                    s_hu.wait(t, fbase + f - 1)
                                for c in range(NCH):
                                    ins = t.matmul(phu[:, (f % 2) * 512:(f % 2) * 512 + 512],
                                                   wup2[:, c * F + f * 128:c * F + (f + 1) * 128],
                                                   xn_sb[tb % 2][:, c * 512:(c + 1) * 512],
                                                   start=(c == 0), stop=(c == NCH - 1))
                            s_hm.inc(ins)
                        if tb == 0:
                            t.wait_ge(s_w.h, nw0 + nwj)
                        s_hs.wait(t, fbase + NF)
                        for j in range(NCH):
                            s_ya.wait(t, max(0, (e * NCG + tb) * NCH + j - 1))
                            for f in range(NF):
                                ins = t.matmul(py[:, (j % 2) * 512:(j % 2) * 512 + 512],
                                               wdn[:, f * D + j * 128:f * D + (j + 1) * 128],
                                               hg[:, f * 512:(f + 1) * 512],
                                               start=(f == 0), stop=(f == NF - 1 and not
                                                                     (not swi and e == 0)))
                            if not swi and e == 0:
                                ins = t.matmul(py[:, (j % 2) * 512:(j % 2) * 512 + 512],
                                               b2bf[0:8, j * 128:(j + 1) * 128],
                                               gate_rhs[tb % 2][:, :],
                                               start=False, stop=True)
                            s_ym.inc(ins)

            @blk.scalar
            def _(s):
                for e in range(NEXP):
                    for tb in range(NCG):
                        s_gm.wait(s, e * NCG + tb + 1)
                        s_ge.inc(s.copy(gate_sb[tb % 2][:, :],
                                        pg[:, (tb % 2) * 512:(tb % 2) * 512 + 512]))
                        fbase = (e * NCG + tb) * NF
                        for f in range(NF):
                            s_hm.wait(s, fbase + f + 1)
                            if f == 0 and e * NCG + tb >= 1:
                                s_ym.wait(s, (e * NCG + tb) * NCH)
                            if swi:
                                ins = s.activation(out=hg[:, f * 512:(f + 1) * 512],
                                                   in_=ph[:, (f % 2) * 512:(f % 2) * 512 + 512],
                                                   func=AF.Silu)
                            else:
                                ins = s.activation(out=hg[:, f * 512:(f + 1) * 512],
                                                   in_=ph[:, (f % 2) * 512:(f % 2) * 512 + 512],
                                                   func=AF.Gelu,
                                                   bias=b1T[:, e * NF + f:e * NF + f + 1])
                            s_hg.inc(ins)

            @blk.vector
            def _(v):
                for e in range(NEXP):
                    for tb in range(NCG):
                        fbase = (e * NCG + tb) * NF
                        s_ge.wait(v, e * NCG + tb + 1)
                        for f in range(NF):
                            s_hg.wait(v, fbase + f + 1)
                            if e * NCG + tb >= 1:
                                s_ym.wait(v, (e * NCG + tb) * NCH)
                            if swi:
                                s_hu.inc(v.tensor_mul(htmp[:, :],
                                                      hg[:, f * 512:(f + 1) * 512],
                                                      phu[:, (f % 2) * 512:(f % 2) * 512 + 512]))
                                s_hs.inc(v.tensor_mul(hg[:, f * 512:(f + 1) * 512],
                                                      htmp[:, :], gate_sb[tb % 2][:, :]))
                            else:
                                ins = v.tensor_mul(hg[:, f * 512:(f + 1) * 512],
                                                   hg[:, f * 512:(f + 1) * 512],
                                                   gate_sb[tb % 2][:, :])
                                s_hs.inc(ins)
                        s_xl.wait(v, 16 * NCH * (e * NCG + tb + 1))
                        for j in range(NCH):
                            s_ym.wait(v, (e * NCG + tb) * NCH + j + 1)
                            ins = v.scalar_tensor_tensor(
                                out=x_sb[:, j * 512:(j + 1) * 512],
                                in0=py[:, (j % 2) * 512:(j % 2) * 512 + 512],
                                scalar=0.0, in1=x_sb[:, j * 512:(j + 1) * 512],
                                op0=ALU.add, op1=ALU.add)
                            s_ya.inc(ins)


def build_ffn_launch(kind, debug=False):
    nc = bass.Bass("TRN2", target_bir_lowering=False)
    F = 4096 if kind == "gelu" else 2816
    x_in = nc.dram_tensor("x_in", [D, TOK], F32, kind="ExternalInput").ap()
    wr = nc.dram_tensor("wr", [D, 8], F32, kind="ExternalInput").ap()
    x_out = nc.dram_tensor("x_out", [D, TOK], F32, kind="ExternalOutput").ap()
    W = {"F": F, "wr": wr}
    if kind == "gelu":
        w1 = nc.dram_tensor("w1", [8, D, F], F32, kind="ExternalInput").ap()
        w2 = nc.dram_tensor("w2", [8, F, D], F32, kind="ExternalInput").ap()
        W["b1T"] = nc.dram_tensor("b1T", [128, 8 * (F // 128)], F32, kind="ExternalInput").ap()
        W["b2"] = nc.dram_tensor("b2", [8, D], F32, kind="ExternalInput").ap()
        w1bf = nc.dram_tensor("w1bf", [8, D, F], BF16).ap()
        w2bf = nc.dram_tensor("w2bf", [8, F, D], BF16).ap()
        W["w1bf"], W["w2bf"] = w1bf, w2bf
        cast_pairs = [(w1.rearrange("e d f -> (e d) f"), w1bf.rearrange("e d f -> (e d) f")),
                      (w2.rearrange("e f d -> (e f) d"), w2bf.rearrange("e f d -> (e f) d"))]
    else:
        wg = nc.dram_tensor("wg", [8, D, F], F32, kind="ExternalInput").ap()
        wu = nc.dram_tensor("wu", [8, D, F], F32, kind="ExternalInput").ap()
        wd = nc.dram_tensor("wd", [8, F, D], F32, kind="ExternalInput").ap()
        wgbf = nc.dram_tensor("wgbf", [8, D, F], BF16).ap()
        wubf = nc.dram_tensor("wubf", [8, D, F], BF16).ap()
        wdbf = nc.dram_tensor("wdbf", [8, F, D], BF16).ap()
        W["wgbf"], W["wubf"], W["wdbf"] = wgbf, wubf, wdbf
        cast_pairs = [(wg.rearrange("e d f -> (e d) f"), wgbf.rearrange("e d f -> (e d) f")),
                      (wu.rearrange("e d f -> (e d) f"), wubf.rearrange("e d f -> (e d) f")),
                      (wd.rearrange("e f d -> (e f) d"), wdbf.rearrange("e f d -> (e f) d"))]
    xnT = nc.dram_tensor("xnT", [D, TOK], BF16).ap()
    W["gatesT_dram"] = nc.dram_tensor("gatesT", [8, TOK], BF16).ap()
    if debug:
        xn_dbg = nc.dram_tensor("xn_dbg", [D, TOK], BF16, kind="ExternalOutput").ap()
        g_dbg = nc.dram_tensor("g_dbg", [8, TOK], BF16, kind="ExternalOutput").ap()
        W["rdbg"] = {name: nc.dram_tensor(f"r_{name}", [128, NTT * 8] if name != "den"
                                          else [128, NTT], F32, kind="ExternalOutput").ap()
                     for name in ("lg", "er", "P", "mskK", "den", "gat")}
    with ExitStack() as es:
        C = make_consts(nc, es)
        emit_precast(nc, "pc", cast_pairs)
        lg = es.enter_context(nc.sbuf_tensor("lgr", [128, NTT * 8], F32))
        emit_ln(nc, "ln", x_in, xnT, C, router=(wr, lg))
        emit_router(nc, "rt", lg, W["gatesT_dram"], C,
                    eps=(1e-9 if kind == "gelu" else 0.0), dbg=W.get("rdbg"))
        emit_moe_ffn(nc, "ffn", kind, xnT, x_in, x_out, C, W)
        if debug:
            with nc.Block() as blk:
                @blk.sync
                def _(sp):
                    s_dbg = nc.semaphore("s_dbg").__enter__()
                    sp.dma_start(out=xn_dbg, in_=xnT).then_inc(s_dbg, 16)
                    sp.dma_start(out=g_dbg, in_=W["gatesT_dram"]).then_inc(s_dbg, 16)
                    sp.wait_ge(s_dbg, 32)
    return nc


def host_ffn_inputs(kind, x_bct, inp):
    maps = []
    for c in range(NCORES):
        m = {"x_in": np.ascontiguousarray(x_bct[c])}
        if kind == "gelu":
            F = 4096
            m["wr"] = inp["L0_ffn_Wg"]
            m["w1"] = inp["L0_ffn_W1"]
            m["w2"] = inp["L0_ffn_W2"]
            m["b1T"] = np.ascontiguousarray(
                inp["L0_ffn_b1"].reshape(8, F // 128, 128).transpose(2, 0, 1).reshape(128, -1))
            m["b2"] = inp["L0_ffn_b2"]
        else:
            m["wr"] = inp["L1_ffn_Wr"]
            m["wg"] = inp["L1_ffn_Wg"]
            m["wu"] = inp["L1_ffn_Wu"]
            m["wd"] = inp["L1_ffn_Wd"]
        maps.append(m)
    return maps


def build_l0_attn_launch(debug=False):
    nc = bass.Bass("TRN2", target_bir_lowering=False)
    x_in = nc.dram_tensor("x_in", [D, TOK], F32, kind="ExternalInput").ap()
    wq = nc.dram_tensor("wq", [D, 1024], F32, kind="ExternalInput").ap()
    wk = nc.dram_tensor("wk", [D, 1024], F32, kind="ExternalInput").ap()
    wv = nc.dram_tensor("wv", [D, 1024], F32, kind="ExternalInput").ap()
    x_out = nc.dram_tensor("x_out", [D, TOK], F32, kind="ExternalOutput").ap()
    xnT = nc.dram_tensor("xnT", [D, TOK], BF16).ap()
    with ExitStack() as es:
        C = make_consts(nc, es)
        emit_ln(nc, "ln", x_in, xnT, C)
        emit_l0_attn(nc, "at", x_in, xnT, {"wq": wq, "wk": wk, "wv": wv}, x_out, C)
    return nc


def host_l0_attn_inputs(x_bct, inp):
    W = inp["L0_attn_W"].reshape(1024, 16, 192)
    wq = np.ascontiguousarray(W[:, :, 0:64].reshape(1024, 1024))
    wk = np.ascontiguousarray(W[:, :, 64:128].reshape(1024, 1024))
    wv = np.ascontiguousarray(W[:, :, 128:192].reshape(1024, 1024))
    return [{"x_in": x_bct[c], "wq": wq, "wk": wk, "wv": wv} for c in range(NCORES)]


def emit_l1_attn(nc, tag, x_in, xnT_dram, W, x_out, C, aux=None):
    with ExitStack() as es:
        def sb(name, shape, dt=F32):
            return es.enter_context(nc.sbuf_tensor(f"{tag}_{name}", shape, dt))
        oT_sb = sb("oT", [128, NEXP * TOK], BF16)
        bqT = sb("bqT", [128, 8]); bkT = sb("bkT", [128, 8]); boT = sb("boT", [128, 8])
        bvf = sb("bvf", [1, D]); bvb = sb("bvb", [1, D], BF16)
        s_bl = Sem(nc, es, f"{tag}_bl")
        with nc.Block() as blk:
            @blk.sync
            def _(sp):
                sp.dma_start(out=bqT[:, :], in_=W["bqT"]).then_inc(s_bl.h, 16)
                sp.dma_start(out=bkT[:, :], in_=W["bkT"]).then_inc(s_bl.h, 16)
                sp.dma_start(out=boT[:, :], in_=W["boT"]).then_inc(s_bl.h, 16)
                sp.dma_start(out=bvf[:, :], in_=W["bv"]).then_inc(s_bl.h, 16)
                s_bl.wait(sp, 64)
        with nc.Block() as blk:
            @blk.vector
            def _(v):
                v.tensor_copy(out=bvb[:, :], in_=bvf[:, :])

        with ExitStack() as es2:
            def sb2(name, shape, dt=F32):
                return es2.enter_context(nc.sbuf_tensor(f"{tag}_{name}", shape, dt))
            wq = sb2("wq", [128, NCH * 1024], BF16)
            wk = sb2("wk", [128, NCH * 1024], BF16)
            wv = sb2("wv", [128, NCH * 1024], BF16)
            jobs = []
            for name, dst in (("wqT", wq), ("wkT", wk), ("wvT", wv)):
                for c in range(NCH):
                    jobs.append((W[name][c * 128:(c + 1) * 128, :],
                                 dst[:, c * 1024:(c + 1) * 1024]))
            emit_load_cast_w(nc, tag, jobs)

            qT = sb2("qT", [128, TOK], BF16)
            kT = sb2("kT", [128, TOK], BF16)
            v_g = sb2("v_g", [128, TOK], BF16)
            ktok = sb2("ktok", [128, TOK], BF16)
            xg = [sb2(f"xg{i}", [128, NCH * 512], BF16) for i in range(2)]
            qpre = [sb2(f"qpre{i}", [128, 512]) for i in range(2)]
            t0 = sb2("t0", [128, 512]); t1 = sb2("t1", [128, 512])
            eE = sb2("eE", [128, 512])
            ATm = [sb2(f"ATm{i}", [128, 128], BF16) for i in range(2)]
            S_bf = sb2("S_bf", [128, 64], BF16)
            psP = es2.enter_context(nc.psum_tensor(f"{tag}_psP", [128, 1024], F32))
            psAT = es2.enter_context(nc.psum_tensor(f"{tag}_psAT", [128, 1024], F32))
            psO = es2.enter_context(nc.psum_tensor(f"{tag}_psO", [128, 1024], F32))
            psS = es2.enter_context(nc.psum_tensor(f"{tag}_psS", [128, 1024], F32))
            s_xg = Sem(nc, es2, f"{tag}_xg")
            s_pm = Sem(nc, es2, f"{tag}_pm")
            s_pe1 = Sem(nc, es2, f"{tag}_pe1")
            s_mn = Sem(nc, es2, f"{tag}_mn")
            s_eE = Sem(nc, es2, f"{tag}_eE")
            s_fm = Sem(nc, es2, f"{tag}_fm")
            s_kt = Sem(nc, es2, f"{tag}_kt")
            s_at = Sem(nc, es2, f"{tag}_at")
            s_tm = Sem(nc, es2, f"{tag}_tm")
            s_om = Sem(nc, es2, f"{tag}_om")
            s_sm2 = Sem(nc, es2, f"{tag}_sm2")
            s_se = Sem(nc, es2, f"{tag}_se")
            s_oe = Sem(nc, es2, f"{tag}_oe")

            for g in range(NEXP):
                with nc.Block() as blk:
                    if aux is not None:
                        aux(blk, g)

                    @blk.sync
                    def _(sp):
                        for cg in range(NCG):
                            if g * 48 + cg * 6 >= 12:
                                s_pm.wait(sp, g * 48 + (cg - 1) * 6)
                            for c in range(NCH):
                                sp.dma_start(out=xg[cg % 2][:, c * 512:(c + 1) * 512],
                                             in_=xnT_dram[c * 128:(c + 1) * 128,
                                                          cg * 512:(cg + 1) * 512]
                                             ).then_inc(s_xg.h, 16)
                    s_xg.cnt += 16 * NCH * NCG

                    @blk.tensor
                    def _(t):
                        item = g * 48
                        for cg in range(NCG):
                            s_xg.wait(t, 16 * NCH * (g * NCG + cg + 1))
                            for wsb in (wq, wk):
                                if item >= 2:
                                    s_pe1.wait(t, item - 1)
                                for c in range(NCH):
                                    ins = t.matmul(
                                        psP[:, (item % 2) * 512:(item % 2) * 512 + 512],
                                        wsb[:, c * 1024 + g * 128:c * 1024 + (g + 1) * 128],
                                        xg[cg % 2][:, c * 512:(c + 1) * 512],
                                        start=(c == 0), stop=(c == NCH - 1))
                                s_pm.inc(ins)
                                item += 1
                            for i in range(4):
                                tt = cg * 4 + i
                                if item >= 2:
                                    s_pe1.wait(t, item - 1)
                                for c in range(NCH):
                                    t.matmul(
                                        psP[:, (item % 2) * 512:(item % 2) * 512 + 128],
                                        xg[cg % 2][:, c * 512 + i * 128:c * 512 + (i + 1) * 128],
                                        wv[:, c * 1024 + g * 128:c * 1024 + (g + 1) * 128],
                                        start=(c == 0), stop=False)
                                ins = t.matmul(
                                    psP[:, (item % 2) * 512:(item % 2) * 512 + 128],
                                    C["ones_row"][0:1, :],
                                    bvb[0:1, g * 128:(g + 1) * 128],
                                    start=False, stop=True)
                                s_pm.inc(ins)
                                item += 1

                    @blk.scalar
                    def _(s):
                        item = g * 48
                        for cg in range(NCG):
                            for which in range(2):
                                dstT = qT if which == 0 else kT
                                bias = bqT if which == 0 else bkT
                                s_pm.wait(s, item + 1)
                                s_pe1.inc(s.activation(
                                    out=qpre[item % 2][:, :],
                                    in_=psP[:, (item % 2) * 512:(item % 2) * 512 + 512],
                                    func=AF.Identity, bias=bias[:, g:g + 1]))
                                qk = g * 16 + cg * 2 + which
                                s_mn.wait(s, qk + 1)
                                if qk >= 1:
                                    s_fm.wait(s, qk)
                                s_eE.inc(s.activation(out=eE[:, :], in_=t0[:, :],
                                                      func=AF.Exp))
                                item += 1
                            for i in range(4):
                                s_pm.wait(s, item + 1)
                                tt = cg * 4 + i
                                s_pe1.inc(s.copy(v_g[:, tt * 128:(tt + 1) * 128],
                                                 psP[:, (item % 2) * 512:(item % 2) * 512 + 128]))
                                item += 1

                    @blk.vector
                    def _(v):
                        qk = g * 16
                        for cg in range(NCG):
                            for which in range(2):
                                item = g * 48 + cg * 6 + which
                                s_pe1.wait(v, item + 1)
                                if qk >= 1:
                                    s_eE.wait(v, qk)
                                v.tensor_scalar(out=t0[:, :], in0=qpre[item % 2][:, :],
                                                scalar1=0.0, scalar2=None, op0=ALU.min)
                                s_mn.inc(v.tensor_scalar(
                                    out=t1[:, :], in0=qpre[item % 2][:, :],
                                    scalar1=0.0, scalar2=None, op0=ALU.max))
                                dstT = qT if which == 0 else kT
                                s_eE.wait(v, qk + 1)
                                s_fm.inc(v.tensor_add(
                                    dstT[:, cg * 512:(cg + 1) * 512], eE[:, :], t1[:, :]))
                                qk += 1

                with nc.Block() as blk:
                    @blk.sync
                    def _(sp):
                        for tt in range(NTT):
                            sp.dma_start(out=ktok[:, tt * 128:(tt + 1) * 128],
                                         in_=kT[:, tt * 128:(tt + 1) * 128],
                                         transpose=True).then_inc(s_kt.h, 16)
                        sp.wait_ge(s_kt.h, s_kt.cnt + 16 * NTT)
                    s_kt.cnt += 16 * NTT

                with nc.Block() as blk:
                    @blk.tensor
                    def _(t):
                        for c in range(NTT):
                            for hi in range(2):
                                hb = hi * 64
                                i = (g * NTT + c) * 2 + hi
                                ib = (g * NTT * 2)
                                if i - ib >= 2:
                                    s_tm.wait(t, i - 1)
                                ins = t.matmul(psAT[:, (i % 2) * 512:(i % 2) * 512 + 128],
                                               kT[hb:hb + 64, c * 128:(c + 1) * 128],
                                               qT[hb:hb + 64, c * 128:(c + 1) * 128],
                                               start=True, stop=True)
                                s_at.inc(ins)
                                if i - ib >= 2:
                                    s_oe.wait(t, i - 1)
                                s_tm.wait(t, i + 1)
                                if c >= 1:
                                    s_se.wait(t, i - 1)
                                    t.matmul(psO[hb:hb + 64, hi * 512:hi * 512 + 128],
                                             S_bf[hb:hb + 64, :],
                                             qT[hb:hb + 64, c * 128:(c + 1) * 128],
                                             start=True, stop=False)
                                ins = t.matmul(psO[hb:hb + 64, hi * 512:hi * 512 + 128],
                                               v_g[:, c * 128 + hb:c * 128 + hb + 64],
                                               ATm[i % 2][:, :],
                                               start=(c == 0), stop=True)
                                s_om.inc(ins)
                                ins = t.matmul(psS[hb:hb + 64, hi * 512:hi * 512 + 64],
                                               ktok[:, c * 128 + hb:c * 128 + hb + 64],
                                               v_g[:, c * 128 + hb:c * 128 + hb + 64],
                                               start=(c == 0), stop=(c == NTT - 1))
                                s_sm2.inc(ins)

                    @blk.vector
                    def _(v):
                        for c in range(NTT):
                            for hi in range(2):
                                i = (g * NTT + c) * 2 + hi
                                ib = (g * NTT * 2)
                                s_at.wait(v, i + 1)
                                if i - ib >= 2:
                                    s_om.wait(v, i - 1)
                                s_tm.inc(v.tensor_tensor(out=ATm[i % 2][:, :],
                                                         in0=psAT[:, (i % 2) * 512:(i % 2) * 512 + 128],
                                                         in1=C["triu"][:, :], op=ALU.mult))

                    @blk.scalar
                    def _(s):
                        for c in range(NTT):
                            for hi in range(2):
                                hb = hi * 64
                                i = (g * NTT + c) * 2 + hi
                                s_om.wait(s, i + 1)
                                s_oe.inc(s.copy(
                                    oT_sb[hb:hb + 64, g * TOK + c * 128:g * TOK + (c + 1) * 128],
                                    psO[hb:hb + 64, hi * 512:hi * 512 + 128]))
                                s_sm2.wait(s, i + 1)
                                s_se.inc(s.copy(S_bf[hb:hb + 64, :],
                                                psS[hb:hb + 64, hi * 512:hi * 512 + 64]))

        with ExitStack() as es3:
            def sb3(name, shape, dt=F32):
                return es3.enter_context(nc.sbuf_tensor(f"{tag}_{name}", shape, dt))
            wo = sb3("wo", [128, NCH * 1024], BF16)
            jobs = [(W["woT"][c * 128:(c + 1) * 128, :], wo[:, c * 1024:(c + 1) * 1024])
                    for c in range(NCH)]
            emit_load_cast_w(nc, f"{tag}o", jobs)
            x_sb = sb3("x0", [128, TOK])
            psW = es3.enter_context(nc.psum_tensor(f"{tag}_psW", [128, 1024], F32))
            s_wm = Sem(nc, es3, f"{tag}_wm")
            s_wa = Sem(nc, es3, f"{tag}_wa")
            s_xl = Sem(nc, es3, f"{tag}_xl")
            s_xs = Sem(nc, es3, f"{tag}_xs")
            with nc.Block() as blk:
                @blk.sync
                def _(sp):
                    for dout in range(NCH):
                        if dout >= 1:
                            s_xs.wait(sp, 16 * NCH * dout)
                        for c in range(NCH):
                            sp.dma_start(out=x_sb[:, c * 512:(c + 1) * 512],
                                         in_=x_in[dout * 128:(dout + 1) * 128,
                                                  c * 512:(c + 1) * 512]).then_inc(s_xl.h, 16)
                        s_wa.wait(sp, NCG * (dout + 1))
                        for c in range(NCH):
                            sp.dma_start(out=x_out[dout * 128:(dout + 1) * 128,
                                                   c * 512:(c + 1) * 512],
                                         in_=x_sb[:, c * 512:(c + 1) * 512]).then_inc(s_xs.h, 16)
                    s_xs.wait(sp, 16 * NCH * NCH)
                s_xl.cnt += 16 * NCH * NCH
                s_xs.cnt += 16 * NCH * NCH

                @blk.tensor
                def _(t):
                    for dout in range(NCH):
                        for cg in range(NCG):
                            gi = dout * NCG + cg
                            if gi >= 2:
                                s_wa.wait(t, gi - 1)
                            for c in range(NCH):
                                ins = t.matmul(psW[:, (gi % 2) * 512:(gi % 2) * 512 + 512],
                                               wo[:, c * 1024 + dout * 128:c * 1024 + (dout + 1) * 128],
                                               oT_sb[:, c * TOK + cg * 512:c * TOK + (cg + 1) * 512],
                                               start=(c == 0), stop=(c == NCH - 1))
                            s_wm.inc(ins)

                @blk.vector
                def _(v):
                    for dout in range(NCH):
                        s_xl.wait(v, 16 * NCH * (dout + 1))
                        for cg in range(NCG):
                            gi = dout * NCG + cg
                            s_wm.wait(v, gi + 1)
                            ins = v.scalar_tensor_tensor(
                                out=x_sb[:, cg * 512:(cg + 1) * 512],
                                in0=psW[:, (gi % 2) * 512:(gi % 2) * 512 + 512],
                                scalar=boT[:, dout:dout + 1],
                                in1=x_sb[:, cg * 512:(cg + 1) * 512],
                                op0=ALU.add, op1=ALU.add)
                            s_wa.inc(ins)


def build_l1_attn_launch(debug=False):
    nc = bass.Bass("TRN2", target_bir_lowering=False)
    x_in = nc.dram_tensor("x_in", [D, TOK], F32, kind="ExternalInput").ap()
    names = ["wqT", "wkT", "wvT", "woT"]
    W = {n: nc.dram_tensor(n, [D, D], F32, kind="ExternalInput").ap() for n in names}
    for n in ("bqT", "bkT", "boT"):
        W[n] = nc.dram_tensor(n, [128, 8], F32, kind="ExternalInput").ap()
    W["bv"] = nc.dram_tensor("bv", [1, D], F32, kind="ExternalInput").ap()
    x_out = nc.dram_tensor("x_out", [D, TOK], F32, kind="ExternalOutput").ap()
    xnT = nc.dram_tensor("xnT", [D, TOK], BF16).ap()
    with ExitStack() as es:
        C = make_consts(nc, es)
        emit_ln(nc, "ln", x_in, xnT, C)
        emit_l1_attn(nc, "la", x_in, xnT, W, x_out, C)
    return nc


def host_l1_attn_inputs(x_bct, inp):
    m = {}
    m["wqT"] = np.ascontiguousarray(inp["L1_attn_Wq"].T)
    m["wkT"] = np.ascontiguousarray(inp["L1_attn_Wk"].T)
    m["wvT"] = np.ascontiguousarray(inp["L1_attn_Wv"].T)
    m["woT"] = np.ascontiguousarray(inp["L1_attn_Wo"].T)
    m["bqT"] = np.ascontiguousarray(inp["L1_attn_bq"].reshape(8, 128).T)
    m["bkT"] = np.ascontiguousarray(inp["L1_attn_bk"].reshape(8, 128).T)
    m["boT"] = np.ascontiguousarray(inp["L1_attn_bo"].reshape(8, 128).T)
    m["bv"] = inp["L1_attn_bv"].reshape(1, 1024)
    return [dict(m, x_in=x_bct[c]) for c in range(NCORES)]


_BUILD_CACHE = {}


def _get(name, fn):
    if name not in _BUILD_CACHE:
        _BUILD_CACHE[name] = fn()
    return _BUILD_CACHE[name]


def build_fused():
    nc = bass.Bass("TRN2", target_bir_lowering=False)
    x0 = nc.dram_tensor("x0", [D, TOK], F32, kind="ExternalInput").ap()
    out = nc.dram_tensor("out", [D, TOK], F32, kind="ExternalOutput").ap()
    wq = nc.dram_tensor("wq", [D, 1024], F32, kind="ExternalInput").ap()
    wk = nc.dram_tensor("wk", [D, 1024], F32, kind="ExternalInput").ap()
    wv = nc.dram_tensor("wv", [D, 1024], F32, kind="ExternalInput").ap()
    f0_wr = nc.dram_tensor("f0_wr", [D, 8], F32, kind="ExternalInput").ap()
    f0_w1 = nc.dram_tensor("f0_w1", [8, D, 4096], F32, kind="ExternalInput").ap()
    f0_w2 = nc.dram_tensor("f0_w2", [8, 4096, D], F32, kind="ExternalInput").ap()
    f0_b1T = nc.dram_tensor("f0_b1T", [128, 8 * 32], F32, kind="ExternalInput").ap()
    f0_b2 = nc.dram_tensor("f0_b2", [8, D], F32, kind="ExternalInput").ap()
    WA = {}
    for n in ("wqT", "wkT", "wvT", "woT"):
        WA[n] = nc.dram_tensor(n, [D, D], F32, kind="ExternalInput").ap()
    for n in ("bqT", "bkT", "boT"):
        WA[n] = nc.dram_tensor(n, [128, 8], F32, kind="ExternalInput").ap()
    WA["bv"] = nc.dram_tensor("bv", [1, D], F32, kind="ExternalInput").ap()
    f1_wr = nc.dram_tensor("f1_wr", [D, 8], F32, kind="ExternalInput").ap()
    f1_wg = nc.dram_tensor("f1_wg", [8, D, 2816], F32, kind="ExternalInput").ap()
    f1_wu = nc.dram_tensor("f1_wu", [8, D, 2816], F32, kind="ExternalInput").ap()
    f1_wd = nc.dram_tensor("f1_wd", [8, 2816, D], F32, kind="ExternalInput").ap()
    x1 = nc.dram_tensor("x1", [D, TOK], F32).ap()
    x2 = nc.dram_tensor("x2", [D, TOK], F32).ap()
    x3 = nc.dram_tensor("x3", [D, TOK], F32).ap()
    xnT = nc.dram_tensor("xnT", [D, TOK], BF16).ap()
    gatesT = nc.dram_tensor("gatesT", [8, TOK], BF16).ap()
    w1bf = nc.dram_tensor("w1bf", [8, D, 4096], BF16).ap()
    w2bf = nc.dram_tensor("w2bf", [8, 4096, D], BF16).ap()
    wgbf = nc.dram_tensor("wgbf", [8, D, 2816], BF16).ap()
    wubf = nc.dram_tensor("wubf", [8, D, 2816], BF16).ap()
    wdbf = nc.dram_tensor("wdbf", [8, 2816, D], BF16).ap()

    W0 = {"F": 4096, "wr": f0_wr, "w1bf": w1bf, "w2bf": w2bf,
          "b1T": f0_b1T, "b2": f0_b2, "gatesT_dram": gatesT}
    W1 = {"F": 2816, "wr": f1_wr, "wgbf": wgbf, "wubf": wubf, "wdbf": wdbf,
          "gatesT_dram": gatesT}

    with ExitStack() as es:
        C = make_consts(nc, es)
        lg = es.enter_context(nc.sbuf_tensor("lgr", [128, NTT * 8], F32))
        emit_ln(nc, "ln0", x0, xnT, C)
        with ExitStack() as esA:
            pc0 = PoolPrecast(nc, esA, es, "pc0", [
                (f0_w1.rearrange("e d f -> (e d) f"), w1bf.rearrange("e d f -> (e d) f")),
                (f0_w2.rearrange("e f d -> (e f) d"), w2bf.rearrange("e f d -> (e f) d"))],
                maxw=2048)
            npg = (len(pc0.jobs) + NEXP - 1) // NEXP
            emit_l0_attn(nc, "at0", x0, xnT, {"wq": wq, "wk": wk, "wv": wv}, x1, C,
                         aux=lambda blk, g: pc0.emit(blk, npg))
            dep0 = pc0.dep()
        emit_ln(nc, "ln1", x1, xnT, C, router=(f0_wr, lg))
        emit_router(nc, "rt1", lg, gatesT, C, eps=1e-9)
        emit_moe_ffn(nc, "ff1", "gelu", xnT, x1, x2, C, W0, deps=[dep0])
        emit_ln(nc, "ln2", x2, xnT, C)
        with ExitStack() as esB:
            pc1 = PoolPrecast(nc, esB, es, "pc1", [
                (f1_wg.rearrange("e d f -> (e d) f"), wgbf.rearrange("e d f -> (e d) f")),
                (f1_wu.rearrange("e d f -> (e d) f"), wubf.rearrange("e d f -> (e d) f")),
                (f1_wd.rearrange("e f d -> (e f) d"), wdbf.rearrange("e f d -> (e f) d"))],
                maxw=1024)
            npg1 = (len(pc1.jobs) + NEXP - 1) // NEXP
            emit_l1_attn(nc, "la2", x2, xnT, WA, x3, C,
                         aux=lambda blk, g: pc1.emit(blk, npg1))
            dep1 = pc1.dep()
        emit_ln(nc, "ln3", x3, xnT, C, router=(f1_wr, lg))
        emit_router(nc, "rt3", lg, gatesT, C, eps=0.0)
        emit_moe_ffn(nc, "ff3", "swiglu", xnT, x3, out, C, W1, deps=[dep1])
    return nc


def host_fused_inputs(x_bct, inp):
    W = inp["L0_attn_W"].reshape(1024, 16, 192)
    base = {
        "wq": np.ascontiguousarray(W[:, :, 0:64].reshape(1024, 1024)),
        "wk": np.ascontiguousarray(W[:, :, 64:128].reshape(1024, 1024)),
        "wv": np.ascontiguousarray(W[:, :, 128:192].reshape(1024, 1024)),
        "f0_wr": inp["L0_ffn_Wg"],
        "f0_w1": inp["L0_ffn_W1"],
        "f0_w2": inp["L0_ffn_W2"],
        "f0_b1T": np.ascontiguousarray(
            inp["L0_ffn_b1"].reshape(8, 32, 128).transpose(2, 0, 1).reshape(128, 256)),
        "f0_b2": inp["L0_ffn_b2"],
        "wqT": np.ascontiguousarray(inp["L1_attn_Wq"].T),
        "wkT": np.ascontiguousarray(inp["L1_attn_Wk"].T),
        "wvT": np.ascontiguousarray(inp["L1_attn_Wv"].T),
        "woT": np.ascontiguousarray(inp["L1_attn_Wo"].T),
        "bqT": np.ascontiguousarray(inp["L1_attn_bq"].reshape(8, 128).T),
        "bkT": np.ascontiguousarray(inp["L1_attn_bk"].reshape(8, 128).T),
        "boT": np.ascontiguousarray(inp["L1_attn_bo"].reshape(8, 128).T),
        "bv": inp["L1_attn_bv"].reshape(1, 1024),
        "f1_wr": inp["L1_ffn_Wr"],
        "f1_wg": inp["L1_ffn_Wg"],
        "f1_wu": inp["L1_ffn_Wu"],
        "f1_wd": inp["L1_ffn_Wd"],
    }
    return [dict(base, x0=np.ascontiguousarray(x_bct[c])) for c in range(NCORES)]


def kernel_fused(**inputs):
    x = inputs["L0_attn_x"]
    x_bct = np.ascontiguousarray(x.transpose(0, 2, 1))
    nc = _get("fused", build_fused)
    maps = host_fused_inputs(x_bct, inputs)
    res = run_bass_kernel_spmd(nc, maps, core_ids=list(range(NCORES)))
    out = np.stack([np.asarray(res.results[c]["out"]).T for c in range(NCORES)])
    return np.ascontiguousarray(out)


def kernel(**inputs):
    return kernel_fused(**inputs)


def kernel_multilaunch(**inputs):
    x = inputs["L0_attn_x"]
    x_bct = np.ascontiguousarray(x.transpose(0, 2, 1))

    core_ids = list(range(NCORES))

    nc1 = _get("l0a", build_l0_attn_launch)
    res = run_bass_kernel_spmd(nc1, host_l0_attn_inputs(x_bct, inputs), core_ids=core_ids)
    x_bct = [np.asarray(res.results[c]["x_out"]) for c in range(NCORES)]

    nc2 = _get("l0f", lambda: build_ffn_launch("gelu"))
    maps = host_ffn_inputs("gelu", x_bct, inputs)
    res = run_bass_kernel_spmd(nc2, maps, core_ids=core_ids)
    x_bct = [np.asarray(res.results[c]["x_out"]) for c in range(NCORES)]

    nc3 = _get("l1a", build_l1_attn_launch)
    res = run_bass_kernel_spmd(nc3, host_l1_attn_inputs(x_bct, inputs), core_ids=core_ids)
    x_bct = [np.asarray(res.results[c]["x_out"]) for c in range(NCORES)]

    nc4 = _get("l1f", lambda: build_ffn_launch("swiglu"))
    maps = host_ffn_inputs("swiglu", x_bct, inputs)
    res = run_bass_kernel_spmd(nc4, maps, core_ids=core_ids)
    out = np.stack([np.asarray(res.results[c]["x_out"]).T for c in range(NCORES)])
    return np.ascontiguousarray(out)
